# version 03
# baseline (speedup 1.0000x reference)
import contextlib
import numpy as np
import concourse.bass as bass
import concourse.mybir as mybir
from concourse.alu_op_type import AluOpType as AO

F32 = mybir.dt.float32
BF16 = mybir.dt.bfloat16
AF = mybir.ActivationFunctionType
AX = mybir.AxisListType

ENGS = ("sync", "tensor", "vector", "scalar", "gpsimd")
DMA_QUEUES = {"sync": "qs", "gpsimd": "qg", "scalar": "qa"}


R_DMA_SEMS = 10


class Sched:
    def __init__(self, nc):
        self.nc = nc
        self.ops = []
        self.eng_ops = {e: [] for e in ENGS}
        self.res = {}
        self.qcount = {q: 0 for q in DMA_QUEUES.values()}
        self.slot_n = {}
        self.slot_consumers = {}

    def op(self, eng, fn, reads=(), writes=(), dma=False, tag=""):
        idx = len(self.ops)
        q = DMA_QUEUES[eng] if dma else None
        o = {"i": idx, "eng": eng, "fn": fn, "dma_q": q, "need_inc": dma,
             "waits": set(), "tag": tag}
        deps = set()
        for r in reads:
            e = self.res.get(r)
            if e and e["w"] is not None:
                deps.add(e["w"])
        for w in writes:
            e = self.res.get(w)
            if e:
                if e["w"] is not None:
                    deps.add(e["w"])
                deps.update(e["r"])
        if dma:
            slot = (q, self.qcount[q] % R_DMA_SEMS)
            self.qcount[q] += 1
            n = self.slot_n.get(slot, 0)
            o["slot"], o["sval"] = slot, 16 * (n + 1)
            self.slot_n[slot] = n + 1
            deps.update(self.slot_consumers.get(slot, ()))
            self.slot_consumers[slot] = set()
        rdeps = set()
        for r in reads:
            e = self.res.get(r)
            if e and e["w"] is not None:
                rdeps.add(e["w"])
        for d in deps:
            dop = self.ops[d]
            if dop["dma_q"] is not None:
                self.slot_consumers.setdefault(dop["slot"], set()).add(idx)
            elif dop["eng"] == eng:
                if (eng != "tensor" and d in rdeps
                        and len(self.eng_ops[eng]) - dop["eng_pos"] <= 2):
                    pass
                else:
                    continue
            dop["need_inc"] = True
            o["waits"].add(d)
        o["eng_pos"] = len(self.eng_ops[eng])
        self.ops.append(o)
        self.eng_ops[eng].append(o)
        for r in reads:
            self.res.setdefault(r, {"w": None, "r": set()})["r"].add(idx)
        for w in writes:
            self.res[w] = {"w": idx, "r": set()}
        return idx

    def dma(self, eng, fn, reads=(), writes=(), tag=""):
        return self.op(eng, fn, reads, writes, dma=True, tag=tag)

    PARANOID = None

    def finalize(self, es):
        nc = self.nc
        if self.PARANOID:
            lo, hi = self.PARANOID
            for i in range(max(lo, 1), min(hi, len(self.ops))):
                self.ops[i - 1]["need_inc"] = True
                self.ops[i]["waits"].add(i - 1)
        self.esem = {e: es.enter_context(nc.semaphore("sem_" + e)) for e in ENGS}
        self.qsem = {}
        for q in set(DMA_QUEUES.values()):
            for si in range(R_DMA_SEMS):
                self.qsem[(q, si)] = es.enter_context(
                    nc.semaphore(f"sem_{q}_{si}"))
        for e in ENGS:
            n = 0
            for o in self.eng_ops[e]:
                if o["need_inc"] and o["dma_q"] is None:
                    n += 1
                    o["inc_seq"] = n
        for o in self.ops:
            wl = {}
            for d in o["waits"]:
                dop = self.ops[d]
                if dop["dma_q"] is not None:
                    sem, val = self.qsem[dop["slot"]], dop["sval"]
                else:
                    sem, val = self.esem[dop["eng"]], dop["inc_seq"]
                k = id(sem)
                if k not in wl or wl[k][1] < val:
                    wl[k] = (sem, val)
            o["wlist"] = list(wl.values())

    def emit(self, eng_name, engine):
        seen = {}
        for o in self.eng_ops[eng_name]:
            for sem, val in o["wlist"]:
                k = id(sem)
                if seen.get(k, -1) >= val:
                    continue
                seen[k] = val
                engine.wait_ge(sem, val)
            inst = o["fn"](engine)
            if o["need_inc"]:
                assert inst is not None, f"op {o['i']} {o['tag']} returned no inst"
                if o["dma_q"] is not None:
                    inst.then_inc(self.qsem[o["slot"]], 16)
                else:
                    inst.then_inc(self.esem[o["eng"]], 1)

    def stats(self):
        return {e: len(self.eng_ops[e]) for e in ENGS} | {
            "incs": sum(1 for o in self.ops if o["need_inc"]),
            "total": len(self.ops), "q": dict(self.qcount)}


class Tile:
    def __init__(self, ap, keys):
        self.ap = ap
        self.keys = tuple(keys)

    def __getitem__(self, idx):
        return self.ap[idx]


NSEQ = 4
S = 2048
D = 1024
NH, NKV, HD = 16, 4, 64
QKV = NH * HD + 2 * NKV * HD
II = 2816
R_SOFT = 1.442695041
EPS = 1e-5
BLK = 1024


class Build:
    def __init__(self, dbg=(), upto=99, nseq=NSEQ):
        self.dbg = set(dbg)
        self.upto = upto
        self.nseq = nseq
        self.nc = bass.Bass("TRN2", target_bir_lowering=False)
        self.es = contextlib.ExitStack()
        self.sc = Sched(self.nc)

    def dram(self, name, shape, dtype=F32, kind="ExternalInput"):
        return self.nc.dram_tensor(name, list(shape), dtype, kind=kind).ap()

    def sb(self, name, shape, dtype=F32):
        ap = self.es.enter_context(self.nc.sbuf_tensor(name, list(shape), dtype))
        return Tile(ap[...], [("t", name)])

    def carve(self, off_elems, shape, dtype=F32):
        n = int(np.prod(shape[1:]))
        width = n * (2 if dtype == F32 else 1)
        ap = self.pool[:, off_elems:off_elems + width]
        if dtype != BF16:
            ap = ap.bitcast(dtype)
        if len(shape) > 2:
            kw = {}
            names = []
            for i, s_ in enumerate(shape[1:]):
                names.append(chr(ord('a') + i))
                kw[names[i]] = s_
            ap = ap.rearrange(f"p ({' '.join(names)}) -> p {' '.join(names)}", **kw)
        b0, b1 = off_elems // BLK, (off_elems + width - 1) // BLK
        return Tile(ap, [("P", b) for b in range(b0, b1 + 1)])

    def ps(self, name, shape):
        ap = self.es.enter_context(self.nc.psum_tensor(name, list(shape), F32))
        return Tile(ap[...], [(name, 0), (name, 1)])

    def _keys(self, tiles):
        ks = []
        for t in tiles:
            if isinstance(t, Tile):
                ks.extend(t.keys)
            else:
                ks.append(t)
        return ks

    def v(self, fn, r=(), w=(), tag=""):
        return self.sc.op("vector", fn, self._keys(r), self._keys(w), tag=tag)

    def a(self, fn, r=(), w=(), tag=""):
        return self.sc.op("scalar", fn, self._keys(r), self._keys(w), tag=tag)

    def t(self, fn, r=(), w=(), tag=""):
        return self.sc.op("tensor", fn, self._keys(r), self._keys(w), tag=tag)

    def g(self, fn, r=(), w=(), tag=""):
        return self.sc.op("gpsimd", fn, self._keys(r), self._keys(w), tag=tag)

    def dget(self, fn, r=(), w=(), tag=""):
        return self.sc.dma("sync", fn, self._keys(r), self._keys(w), tag=tag)

    def dput(self, fn, r=(), w=(), tag=""):
        return self.sc.dma("gpsimd", fn, self._keys(r), self._keys(w), tag=tag)

    def dump(self, tag, ap, shape, dtype=F32, deps=()):
        if tag not in self.dbg:
            return
        t = self.dram("dbg_" + tag, shape, dtype, kind="ExternalOutput")
        self.dput(lambda g, t=t, ap=ap: g.dma_start(out=t[...], in_=ap),
                  r=list(deps), tag="dump_" + tag)


POOL_N = 69632

OFF_qT = 0
OFF_kvT = 16384
OFF_h1T = 24576
OFF_ktil = 24576
OFF_vT = 32768
OFF_paA0 = 37888
OFF_paA1 = 38912
OFF_paB0 = 39936
OFF_paB1 = 40960
OFF_xnb = 40960
OFF_attnT = 45056
OFF_wgs0 = 0
OFF_wgs1 = 4096
OFF_wus0 = 8192
OFF_wus1 = 12288
OFF_wds0 = 16384
OFF_wds1 = 19456
OFF_gd2 = 22528
OFF_hqT = 30720
OFF_cb1r = 38912
OFF_sbr = 40960
OFF_hq2 = 45056
OFF_actqT = 46080
OFF_wtmp0 = 0
OFF_wtmp1 = 5632
OFF_wbf0 = 11264
OFF_wbf1 = 14336
OFF_stg = 17408
OFF_cbuf = 18432
OFF_mslab = 20480
OFF_wmbuf0 = 24576
OFF_wmbuf1 = 32768
OFF_wmbf0 = 40960
OFF_wmbf1 = 45056
OFF_csb = 49152


class BuildK(Build):
    def build(self):
        nc, sc = self.nc, self.sc
        self.x_in = self.dram("x", (NSEQ, S, D))
        self.c_in = self.dram("c", (NSEQ, D))
        self.scaling = self.dram("scaling", (HD, 1))
        self.qkv_w = self.dram("qkv_w", (QKV, D))
        self.qkv_b = self.dram("qkv_b", (QKV,))
        self.o_w = self.dram("o_w", (D, D))
        self.o_b = self.dram("o_b", (D,))
        self.w_gate = self.dram("w_gate", (II, D))
        self.w_up = self.dram("w_up", (II, D))
        self.w_down = self.dram("w_down", (D, II))
        self.wmod = self.dram("wmod", (D, 6 * D))
        self.out = self.dram("out", (NSEQ, S, D), kind="ExternalOutput")
        self.wgqT_d = self.dram("wgqT_d", (D, II), BF16,
                                kind="ExternalOutput" if "wgq_d" in self.dbg else "Internal")
        self.wuqT_d = self.dram("wuqT_d", (D, II), BF16, kind="Internal")
        self.wdqT_d = self.dram("wdqT_d", (II, D), BF16, kind="Internal")
        self.xnew_d = self.dram("xnew_d", (S, D), F32, kind="Internal")
        self.mod_d = self.dram("mod_d", (4, 6 * D), F32, kind="Internal")
        self.bounce = self.dram("bounce", (8, 1024), F32, kind="Internal")
        if "xnew" in self.dbg:
            self.xnew_dbg = self.dram("dbg_xnew", (S, D), F32, kind="ExternalOutput")

        self.pool = self.es.enter_context(
            self.nc.sbuf_tensor("POOL", [128, POOL_N], BF16))[...]
        self.identb = self.sb("identb", (128, 128), BF16)
        self.onesb = self.sb("onesb", (128, 1), BF16)
        self.modT = self.sb("modT", (128, 192))
        self.qscT = self.sb("qscT", (128, 12))
        self.qbsT = self.sb("qbsT", (128, 12))
        self.obT = self.sb("obT", (128, 8))
        self.gobT = self.sb("gobT", (128, 32))
        self.sc64 = self.sb("sc64", (64, 1))
        self.epsT = self.sb("epsT", (128, 1))
        self.stgr = self.sb("stgr", (128, 2, 128), BF16)
        self.sc128 = self.sb("sc128", (128, 1))
        self.wqkvT = self.sb("wqkvT", (128, 8, QKV), BF16)
        self.owT = self.sb("owT", (128, 8, D), BF16)
        self.qU = self.sb("qU", (128, 512))
        self.qAU = self.sb("qAU", (128, 512))
        self.qT1 = self.sb("qT1", (128, 512), BF16)
        self.qT2 = self.sb("qT2", (128, 512), BF16)
        self.qT3 = self.sb("qT3", (128, 512), BF16)
        self.qT4 = self.sb("qT4", (128, 512), BF16)
        self.qBM = self.sb("qBM", (128, 32))
        self.qR = self.sb("qR", (128, 32))
        self.qS2 = self.sb("qS2", (128, 32))
        self.st1 = self.sb("st_a", (128, 8))
        self.st2 = self.sb("st_b", (128, 8))
        self.chb = self.sb("chb", (128, 16))
        self.recb = self.sb("recb", (64, 2, 1024))
        self.ntmp = self.sb("ntmp", (64, 2, 1024), BF16)
        self.xio = self.sb("xio", (128, 2, D))
        self.pA = self.ps("pA", (128, 1024))
        self.pB = self.ps("pB", (128, 1024))
        self.pC = self.ps("pC", (128, 1024))
        self.pD = self.ps("pD", (128, 1024))

        cv = self.carve
        self.qT = cv(OFF_qT, (128, 8, S), BF16)
        self.kvT = cv(OFF_kvT, (128, 4, S), BF16)
        self.h1T = cv(OFF_h1T, (128, 8, S), BF16)
        self.xnb = cv(OFF_xnb, (128, 4, D), BF16)
        self.attnT = cv(OFF_attnT, (128, 8, S), BF16)
        self.ktil = cv(OFF_ktil, (128, 4, S), BF16)
        self.vT = cv(OFF_vT, (128, 4, 1040), BF16)
        self.paA0 = cv(OFF_paA0, (128, 1024), BF16)
        self.paA1 = cv(OFF_paA1, (128, 1024), BF16)
        self.paB0 = cv(OFF_paB0, (128, 1024), BF16)
        self.paB1 = cv(OFF_paB1, (128, 1024), BF16)
        self.wgs = [cv(OFF_wgs0, (128, 8, 512), BF16), cv(OFF_wgs1, (128, 8, 512), BF16)]
        self.wus = [cv(OFF_wus0, (128, 8, 512), BF16), cv(OFF_wus1, (128, 8, 512), BF16)]
        self.wds = [cv(OFF_wds0, (128, 22, 128), BF16), cv(OFF_wds1, (128, 22, 128), BF16)]
        self.gd2 = cv(OFF_gd2, (128, 2, 8, 512), BF16)
        self.hqT = cv(OFF_hqT, (128, 8, 1024), BF16)
        self.cb1r = cv(OFF_cb1r, (128, D), F32)
        self.sbr = cv(OFF_sbr, (128, D), F32)
        self.hq2 = cv(OFF_hq2, (128, D), BF16)
        self.actqT = cv(OFF_actqT, (128, 22, 1024), BF16)
        self.q2s = cv(16384, (128, 2048), BF16)
        self.h2t = cv(63488, (128, D), F32)
        self.sg = cv(43008, (128, 512), F32)
        self.acts = cv(44032, (128, 512), F32)
        self.sqd = cv(61440, (128, D), F32)
        self.wqt = [cv(65536, (128, 512), F32), cv(66560, (128, 512), F32)]
        self.wqb = [cv(67584, (128, 512), BF16), cv(68096, (128, 512), BF16)]
        self.identf = cv(45056, (128, 128), F32)
        self.iot1 = cv(45312, (128, 128), F32)
        self.iot2 = cv(45568, (128, 128), F32)
        self.wtmp = [cv(OFF_wtmp0, (128, II), F32), cv(OFF_wtmp1, (128, II), F32)]
        self.wbf = [cv(OFF_wbf0, (128, II), BF16), cv(OFF_wbf1, (128, II), BF16)]
        self.stg = cv(OFF_stg, (128, 4, 128), BF16)
        self.cbuf = cv(OFF_cbuf, (128, D), F32)
        self.mslab = cv(OFF_mslab, (128, 2048), F32)
        self.wmbuf = [cv(OFF_wmbuf0, (128, 8, 512), F32), cv(OFF_wmbuf1, (128, 8, 512), F32)]
        self.wmbf = [cv(OFF_wmbf0, (128, 8, 512), BF16), cv(OFF_wmbf1, (128, 8, 512), BF16)]
        self.csf = cv(OFF_csb, (128, 8, 4), F32)

        self.setup()
        for s in range(self.nseq):
            if self.upto >= 1: self.phaseA(s)
            if self.upto >= 2: self.phaseB(s)
            if self.upto >= 3: self.phaseC(s)
            if self.upto >= 4: self.phaseD(s)
            if self.upto >= 5:
                self.pump_wq(200)
                self.phaseE(s)
        for name in self.dbg:
            if not hasattr(self, name) or not isinstance(getattr(self, name), Tile):
                continue
            tile = getattr(self, name)
            ap = tile.ap
            sh = list(ap.shape)
            self.dump(name, ap, sh, ap.dtype, deps=[tile])
        self.finish()
        return self

    def run(self, in_map, n_cores=1, **kw):
        from concourse.bass_utils import run_bass_kernel_spmd
        maps = [in_map] * n_cores if isinstance(in_map, dict) else in_map
        return run_bass_kernel_spmd(self.nc, maps, core_ids=list(range(n_cores)), **kw)

    def emit_quant(self, X, Q, W, xsl=None, qsl=None, rr=(), tag="q", qout=False):
        v, a = self.v, self.a
        NB = W // 16
        Xap = xsl(X.ap) if xsl else X.ap
        Qap = qsl(Q.ap) if qsl else Q.ap
        qU, qAU, qT1, qT2, qT3, qT4 = self.qU, self.qAU, self.qT1, self.qT2, self.qT3, self.qT4
        qBM, qR, qS2 = self.qBM, self.qR, self.qS2
        x3 = Xap.rearrange("p (g k) -> p g k", k=16)
        v(lambda e: e.tensor_reduce(qBM.ap[:, :NB], x3, AX.X, AO.max,
                                    apply_absolute_value=True),
          r=[X] + list(rr), w=[qBM], tag=tag + "bm")
        v(lambda e: e.tensor_scalar(qBM.ap[:, :NB], qBM.ap[:, :NB], 1.0 / 6.0, 1e-12,
                                    AO.mult, AO.max), r=[qBM], w=[qBM], tag=tag)
        v(lambda e: e.reciprocal(qR.ap[:, :NB], qBM.ap[:, :NB]), r=[qBM], w=[qR])
        v(lambda e: e.tensor_scalar(qS2.ap[:, :NB], qBM.ap[:, :NB], 0.5, None, AO.mult),
          r=[qBM], w=[qS2])
        v(lambda e: e.tensor_tensor(qU.ap[:, :W].rearrange("p (g k) -> p g k", k=16), x3,
                                    qR.ap[:, :NB].broadcast_to([128, NB, 16]), AO.mult),
          r=[X, qR], w=[qU], tag=tag + "u")
        a(lambda e: e.activation(qAU.ap[:, :W], qU.ap[:, :W], AF.Abs), r=[qU], w=[qAU])
        a(lambda e: e.activation(qT4.ap[:, :W], qU.ap[:, :W], AF.Sign), r=[qU], w=[qT4])
        v(lambda e: e.tensor_scalar(qT1.ap[:, :W], qAU.ap[:, :W], 0.25, None, AO.is_gt),
          r=[qAU], w=[qT1])
        v(lambda e: e.tensor_scalar(qT2.ap[:, :W], qAU.ap[:, :W], 0.75, None, AO.is_gt),
          r=[qAU], w=[qT2])
        v(lambda e: e.tensor_scalar(qT3.ap[:, :W], qAU.ap[:, :W], 1.25, None, AO.is_gt),
          r=[qAU], w=[qT3])
        v(lambda e: e.tensor_add(qT1.ap[:, :W], qT1.ap[:, :W], qT2.ap[:, :W]),
          r=[qT1, qT2], w=[qT1])
        v(lambda e: e.tensor_scalar(qT2.ap[:, :W], qAU.ap[:, :W], 1.75, None, AO.is_gt),
          r=[qAU], w=[qT2])
        v(lambda e: e.tensor_add(qT3.ap[:, :W], qT3.ap[:, :W], qT2.ap[:, :W]),
          r=[qT3, qT2], w=[qT3])
        v(lambda e: e.tensor_scalar(qT2.ap[:, :W], qAU.ap[:, :W], 2.5, 2.0, AO.is_gt,
                                    AO.mult), r=[qAU], w=[qT2])
        v(lambda e: e.tensor_add(qT1.ap[:, :W], qT1.ap[:, :W], qT3.ap[:, :W]),
          r=[qT1, qT3], w=[qT1])
        v(lambda e: e.tensor_scalar(qT3.ap[:, :W], qAU.ap[:, :W], 3.5, 2.0, AO.is_gt,
                                    AO.mult), r=[qAU], w=[qT3])
        v(lambda e: e.tensor_add(qT2.ap[:, :W], qT2.ap[:, :W], qT3.ap[:, :W]),
          r=[qT2, qT3], w=[qT2])
        v(lambda e: e.tensor_scalar(qT3.ap[:, :W], qAU.ap[:, :W], 5.0, 4.0, AO.is_gt,
                                    AO.mult), r=[qAU], w=[qT3])
        v(lambda e: e.tensor_add(qT2.ap[:, :W], qT2.ap[:, :W], qT3.ap[:, :W]),
          r=[qT2, qT3], w=[qT2])
        v(lambda e: e.tensor_add(qT1.ap[:, :W], qT1.ap[:, :W], qT2.ap[:, :W]),
          r=[qT1, qT2], w=[qT1])
        v(lambda e: e.tensor_tensor(qT2.ap[:, :W].rearrange("p (g k) -> p g k", k=16),
                                    qT4.ap[:, :W].rearrange("p (g k) -> p g k", k=16),
                                    qS2.ap[:, :NB].broadcast_to([128, NB, 16]), AO.mult),
          r=[qT4, qS2], w=[qT2])
        v(lambda e: e.tensor_mul(Qap, qT1.ap[:, :W], qT2.ap[:, :W]),
          r=[qT1, qT2], w=[Q], tag=tag + "fin")

    def ln_stats(self, xap, xtile, tag=""):
        v, a = self.v, self.a
        st1, st2, sqd = self.st1, self.st2, self.sqd
        v(lambda e: e.reduce_sum(st1.ap[:, 0:1], xap, AX.X), r=[xtile], w=[st1], tag=tag + "sum")
        a(lambda e: e.activation(sqd.ap[...], xap, AF.Square,
                                 accum_out=st1.ap[:, 1:2]), r=[xtile, st1], w=[sqd, st1])
        a(lambda e: e.activation(st1.ap[:, 3:4], st1.ap[:, 0:1], AF.Square,
                                 scale=1.0 / D), r=[st1], w=[st1], tag=tag + "m2")
        v(lambda e: e.tensor_scalar(st1.ap[:, 2:3], st1.ap[:, 0:1], 1.0 / D, None,
                                    AO.mult), r=[st1], w=[st1])
        v(lambda e: e.scalar_tensor_tensor(st1.ap[:, 5:6], st1.ap[:, 1:2], 1.0 / D,
                                           st1.ap[:, 3:4], AO.mult, AO.subtract),
          r=[st1], w=[st1], tag=tag + "var")
        a(lambda e: e.activation(st2.ap[:, 0:1], st1.ap[:, 5:6], AF.Ln,
                                 bias=self.epsT.ap),
          r=[st1, st2, self.epsT], w=[st2])
        a(lambda e: e.activation(st2.ap[:, 4:5], st2.ap[:, 0:1], AF.Exp, scale=-0.5),
          r=[st2], w=[st2], tag=tag + "rs")
        v(lambda e: e.scalar_tensor_tensor(st2.ap[:, 5:6], st1.ap[:, 2:3], -1.0,
                                           st2.ap[:, 4:5], AO.mult, AO.mult),
          r=[st1, st2], w=[st2])
        return st2.ap[:, 4:5], st2.ap[:, 5:6]

    def setup(self):
        v, a, t, g = self.v, self.a, self.t, self.g
        dget, dput = self.dget, self.dput
        nc = self.nc
        g(lambda e: e.iota(self.iot1.ap, [[1, 128]], channel_multiplier=0,
                           allow_small_or_imprecise_dtypes=True), w=[self.iot1])
        g(lambda e: e.iota(self.iot2.ap, [[0, 128]], channel_multiplier=1,
                           allow_small_or_imprecise_dtypes=True), w=[self.iot2])
        v(lambda e: e.tensor_tensor(self.identf.ap, self.iot1.ap, self.iot2.ap,
                                    AO.is_equal), r=[self.iot1, self.iot2], w=[self.identf])
        v(lambda e: e.tensor_copy(self.identb.ap, self.identf.ap),
          r=[self.identf], w=[self.identb])
        v(lambda e: e.memset(self.onesb.ap, 1.0), w=[self.onesb])
        v(lambda e: e.memset(self.epsT.ap, EPS), w=[self.epsT])
        dget(lambda e: e.dma_start(out=self.sc64.ap, in_=self.scaling[...]),
             w=[self.sc64], tag="ld_scaling")
        a(lambda e: e.activation(self.sc64.ap, self.sc64.ap, AF.Exp),
          r=[self.sc64], w=[self.sc64])
        v(lambda e: e.tensor_scalar(self.sc64.ap, self.sc64.ap, 1.0, None, AO.add),
          r=[self.sc64], w=[self.sc64])
        a(lambda e: e.activation(self.sc64.ap, self.sc64.ap, AF.Ln),
          r=[self.sc64], w=[self.sc64])
        v(lambda e: e.tensor_scalar(self.sc64.ap, self.sc64.ap, R_SOFT / 8.0, None,
                                    AO.mult), r=[self.sc64], w=[self.sc64])
        dget(lambda e: e.dma_start(out=self.bounce[3:4, 0:64], in_=self.sc64.ap),
             r=[self.sc64], w=[("bnc", 3)], tag="sp_out")
        dget(lambda e: e.dma_start(out=self.sc128.ap[0:64, :], in_=self.bounce[3:4, 0:64]),
             r=[("bnc", 3)], w=[self.sc128])
        dget(lambda e: e.dma_start(out=self.sc128.ap[64:128, :], in_=self.bounce[3:4, 0:64]),
             r=[("bnc", 3)], w=[self.sc128])
        v(lambda e: e.tensor_copy(self.qscT.ap[:, 0:8].rearrange("p (a b) -> p a b", a=1),
                                  self.sc128.ap.broadcast_to([128, 1, 8])),
          r=[self.sc128], w=[self.qscT])
        v(lambda e: e.memset(self.qscT.ap[:, 8:12], 1.0), r=[self.qscT], w=[self.qscT])
        dget(lambda e: e.dma_start(out=self.qbsT.ap,
                                   in_=self.qkv_b.rearrange("(c p) -> p c", p=128),
                                   allow_slow_non_contiguous=True),
             w=[self.qbsT], tag="ld_qkvb")
        v(lambda e: e.tensor_mul(self.qbsT.ap, self.qbsT.ap, self.qscT.ap),
          r=[self.qbsT, self.qscT], w=[self.qbsT])
        dget(lambda e: e.dma_start(out=self.obT.ap,
                                   in_=self.o_b.rearrange("(c p) -> p c", p=128),
                                   allow_slow_non_contiguous=True),
             w=[self.obT], tag="ld_ob")
        dget(lambda e: e.dma_start(out=self.cbuf.ap[0:4, :], in_=self.c_in[...]),
             w=[self.cbuf], tag="ld_c")
        for c8 in range(8):
            t(lambda e, c8=c8: e.transpose(self.pC.ap[:, 0:4],
                                           self.cbuf.ap[0:4, c8 * 128:(c8 + 1) * 128],
                                           self.identf.ap[0:4, 0:4]),
              r=[self.cbuf, self.identf], w=[self.pC], tag="trc")
            a(lambda e, c8=c8: e.activation(self.csf.ap[:, c8, :], self.pC.ap[:, 0:4],
                                            AF.Silu), r=[self.pC], w=[self.csf])
        for j in range(12):
            wmb = self.wmbuf[j % 2]
            wmf = self.wmbf[j % 2]
            dget(lambda e, j=j, wmb=wmb: e.dma_start(
                out=wmb.ap,
                in_=self.wmod[:, j * 512:(j + 1) * 512]
                    .rearrange("(c p) n -> p c n", p=128)),
                 r=[wmb], w=[wmb], tag="ld_wmod")
            for c in range(8):
                t(lambda e, c=c, wmb=wmb: e.matmul(
                    self.pD.ap[0:4, 0:512], self.csf.ap[:, c, :],
                    wmb.ap[:, c, :],
                    start=(c == 0), stop=(c == 7)),
                  r=[self.csf, wmb], w=[self.pD], tag="mm_mod")
            a(lambda e: e.activation(self.mslab.ap[0:4, 0:512], self.pD.ap[0:4, 0:512],
                                     AF.Copy), r=[self.pD], w=[self.mslab])
            vec = (j * 512) // 1024
            if vec in (1, 4):
                v(lambda e: e.tensor_scalar(self.mslab.ap[0:4, 0:512],
                                            self.mslab.ap[0:4, 0:512], 1.0, None, AO.add),
                  r=[self.mslab], w=[self.mslab])
            if vec in (2, 5):
                sl = self.mslab.ap[0:4, 0:512]
                tm = self.mslab.ap[0:4, 512:1024]
                sn = self.mslab.ap[0:4, 1024:1536]
                v(lambda e, sl=sl, tm=tm: e.scalar_tensor_tensor(tm, sl, -1.0, sl,
                                                                 AO.mult, AO.max),
                  r=[self.mslab], w=[self.mslab])
                v(lambda e, tm=tm: e.tensor_scalar(tm, tm, 0.25, None, AO.max),
                  r=[self.mslab], w=[self.mslab])
                v(lambda e, sl=sl, sn=sn: e.tensor_scalar(sn, sl, 1e38, -1.0,
                                                          AO.mult, AO.max),
                  r=[self.mslab], w=[self.mslab])
                v(lambda e, sl=sl, tm=tm, sn=sn: e.scalar_tensor_tensor(
                    sl, sn, 1.0, tm, AO.min, AO.mult),
                  r=[self.mslab], w=[self.mslab])
            dput(lambda e, j=j: e.dma_start(out=self.mod_d[0:4, j * 512:(j + 1) * 512],
                                            in_=self.mslab.ap[0:4, 0:512]),
                 r=[self.mslab], w=[("mod_d", j)], tag="st_mod")
            for u in range(4):
                cn = 4 * j + u
                t(lambda e, u=u: e.transpose(self.pC.ap[:, 0:4],
                                             self.mslab.ap[0:4, u * 128:(u + 1) * 128],
                                             self.identf.ap[0:4, 0:4]),
                  r=[self.mslab, self.identf], w=[self.pC], tag="tr_mod")
                v(lambda e, cn=cn: e.tensor_copy(self.modT.ap[:, cn * 4:(cn + 1) * 4],
                                                 self.pC.ap[:, 0:4]),
                  r=[self.pC], w=[self.modT])
        for b in range(4):
            v(lambda e, b=b: e.tensor_mul(self.gobT.ap[:, b:32:4],
                                          self.modT.ap[:, 64 + b:96:4], self.obT.ap),
              r=[self.modT, self.obT], w=[self.gobT])
        for (src_w, dstT, nr) in ((self.qkv_w, self.wqkvT, 12), (self.o_w, self.owT, 8)):
            for r_ in range(nr):
                wt = self.wtmp[r_ % 2]
                wb = self.wbf[r_ % 2]
                dget(lambda e, src_w=src_w, r_=r_, wt=wt: e.dma_start(
                    out=wt.ap[:, 0:1024],
                    in_=src_w[r_ * 128:(r_ + 1) * 128, :]),
                     r=[wt], w=[wt], tag="ld_qow")
                v(lambda e, wt=wt, wb=wb: e.tensor_copy(wb.ap[:, 0:1024], wt.ap[:, 0:1024]),
                  r=[wt, wb], w=[wb])
                for c in range(8):
                    p = (self.pA, self.pB)[c % 2]
                    t(lambda e, wb=wb, c=c, p=p: e.transpose(
                        p.ap.bitcast(BF16)[:, 0:128], wb.ap[:, c * 128:(c + 1) * 128],
                        self.identb.ap), r=[wb, self.identb], w=[p], tag="tr_w")
                    v(lambda e, dstT=dstT, r_=r_, c=c, p=p: e.tensor_copy(
                        dstT.ap[:, c, r_ * 128:(r_ + 1) * 128], p.ap.bitcast(BF16)[:, 0:128]),
                      r=[p], w=[dstT])
        self._wq_gen = self._wq_generator()
        self._wq_done = False

    def _wq_generator(self):
        v, t, dget, dput = self.v, self.t, self.dget, self.dput
        for (w_in, w_out, wkey, nr, F) in (
                (self.w_gate, self.wgqT_d, "wgq", 22, 1024),
                (self.w_up, self.wuqT_d, "wuq", 22, 1024),
                (self.w_down, self.wdqT_d, "wdq", 8, II)):
            for r_ in range(nr):
                for so in range(0, F, 512):
                    w_ = min(512, F - so)
                    sl = (r_ * F + so) // 512 % 2
                    wt = self.wqt[sl]
                    wb = self.wqb[sl]
                    dget(lambda e, w_in=w_in, r_=r_, so=so, w_=w_, wt=wt: e.dma_start(
                        out=wt.ap[:, 0:w_],
                        in_=w_in[r_ * 128:(r_ + 1) * 128, so:so + w_]),
                         r=[wt], w=[wt], tag="ld_fw")
                    self.emit_quant(Tile(wt.ap[:, 0:w_], wt.keys),
                                    Tile(wb.ap[:, 0:w_], wb.keys), w_)
                    for u in range(w_ // 128):
                        c = (so + u * 128) // 128
                        p = (self.pA, self.pB)[u % 2]
                        sg_ = u % 2
                        t(lambda e, wb=wb, u=u, p=p: e.transpose(
                            p.ap.bitcast(BF16)[:, 1024 + u % 2 * 128:1024 + u % 2 * 128 + 128],
                            wb.ap[:, u * 128:(u + 1) * 128],
                            self.identb.ap), r=[wb, self.identb], w=[p], tag="tr_fw")
                        v(lambda e, sg_=sg_, p=p, u=u: e.tensor_copy(
                            self.stgr.ap[:, sg_, :],
                            p.ap.bitcast(BF16)[:, 1024 + u % 2 * 128:1024 + u % 2 * 128 + 128]),
                          r=[p, ("stg", sg_)], w=[("stg", sg_)])
                        dput(lambda e, w_out=w_out, r_=r_, c=c, sg_=sg_: e.dma_start(
                            out=w_out[c * 128:(c + 1) * 128, r_ * 128:(r_ + 1) * 128],
                            in_=self.stgr.ap[:, sg_, :]),
                             r=[("stg", sg_), (wkey,)], w=[(wkey,)], tag="st_fw")
                yield

    def pump_wq(self, n):
        if self._wq_done:
            return
        for _ in range(n):
            try:
                next(self._wq_gen)
            except StopIteration:
                self._wq_done = True
                return

    def emit_quant_sl(self, xap, xkey, qap, qkey, W):
        self.emit_quant(Tile(xap, [xkey] if not isinstance(xkey, list) else xkey),
                        Tile(qap, [qkey] if not isinstance(qkey, list) else qkey), W)

    def phaseA(self, s):
        v, a, t = self.v, self.a, self.t
        xio = self.xio
        for g_ in range(4):
            if s == 0 and g_:
                self.pump_wq(3)
            for t4 in range(4):
                tc = 4 * g_ + t4
                buf = tc % 2
                xk = ("xio", buf)
                self.dget(lambda e, s=s, tc=tc, buf=buf: e.dma_start(
                    out=xio.ap[:, buf, :],
                    in_=self.x_in[s, tc * 128:(tc + 1) * 128, :]),
                     r=[xk], w=[xk], tag="ldx")
                sc_, bi_ = self.ln_stats(xio.ap[:, buf, :], xk, tag="ln1")
                a(lambda e, t4=t4, buf=buf, sc_=sc_, bi_=bi_: e.activation(
                    self.xnb.ap[:, t4, :], xio.ap[:, buf, :], AF.Identity,
                    bias=bi_, scale=sc_),
                  r=[xk, self.st2], w=[self.xnb], tag="xn")
            for c in range(8):
                p = (self.pA, self.pB)[c % 2]
                for t4 in range(4):
                    t(lambda e, t4=t4, c=c, p=p: e.transpose(
                        p.ap.bitcast(BF16)[:, t4 * 128:(t4 + 1) * 128],
                        self.xnb.ap[:, t4, c * 128:(c + 1) * 128], self.identb.ap),
                      r=[self.xnb, self.identb], w=[p], tag="trA")
                csa = (0 * 8 + c) * 4 + s
                cca = (1 * 8 + c) * 4 + s
                a(lambda e, c=c, g_=g_, p=p, csa=csa, cca=cca: e.activation(
                    self.h1T.ap[:, c, g_ * 512:(g_ + 1) * 512],
                    p.ap.bitcast(BF16)[:, 0:512],
                    AF.Identity, bias=self.modT.ap[:, csa:csa + 1],
                    scale=self.modT.ap[:, cca:cca + 1]),
                  r=[p, self.modT], w=[self.h1T], tag="h1ev")
        for j in range(12):
            for g_ in range(4):
                p = (self.pC, self.pD)[g_ % 2]
                for c in range(8):
                    t(lambda e, j=j, g_=g_, c=c, p=p: e.matmul(
                        p.ap[:, 0:512], self.wqkvT.ap[:, c, j * 128:(j + 1) * 128],
                        self.h1T.ap[:, c, g_ * 512:(g_ + 1) * 512],
                        start=(c == 0), stop=(c == 7)),
                      r=[self.wqkvT, self.h1T], w=[p], tag="mmqkv")
                dst = self.qT if j < 8 else self.kvT
                jj = j if j < 8 else j - 8
                a(lambda e, dst=dst, jj=jj, g_=g_, p=p, j=j: e.activation(
                    dst.ap[:, jj, g_ * 512:(g_ + 1) * 512], p.ap[:, 0:512],
                    AF.Identity, bias=self.qbsT.ap[:, j:j + 1],
                    scale=self.qscT.ap[:, j:j + 1]),
                  r=[p, self.qbsT, self.qscT], w=[dst], tag="qkvev")

    def phaseB(self, s):
        v, t = self.v, self.t
        for g_ in range(4):
            half = g_ % 2
            ch = g_ // 2
            for dsthalf in range(2):
                self.dget(lambda e, g_=g_, half=half, ch=ch, dh=dsthalf: e.dma_start(
                    out=self.ktil.ap[dh * 64:(dh + 1) * 64, g_, :],
                    in_=self.kvT.ap[half * 64:(half + 1) * 64, ch, :]),
                     r=[self.kvT], w=[self.ktil], tag="ktil")
            v(lambda e, g_=g_: e.memset(self.vT.ap[:, g_, 64:1040:65], 1.0),
              w=[self.vT], tag="vones")
            for kc in range(16):
                p = (self.pA, self.pB)[kc % 2]
                t(lambda e, g_=g_, half=half, ch=ch, kc=kc, p=p: e.transpose(
                    p.ap.bitcast(BF16)[:, 0:64],
                    self.kvT.ap[half * 64:(half + 1) * 64, 2 + ch, kc * 128:(kc + 1) * 128],
                    self.identb.ap[half * 64:(half + 1) * 64, half * 64:(half + 1) * 64]),
                  r=[self.kvT, self.identb], w=[p], tag="trv")
                v(lambda e, g_=g_, kc=kc, p=p: e.tensor_copy(
                    self.vT.ap[:, g_, kc * 65:kc * 65 + 64], p.ap.bitcast(BF16)[:, 0:64]),
                  r=[p], w=[self.vT], tag="vcp")
            v(lambda e, g_=g_: e.tensor_mul(self.q2s.ap[0:64, :],
                                            self.ktil.ap[0:64, g_, :],
                                            self.ktil.ap[0:64, g_, :]),
              r=[self.ktil, self.q2s], w=[self.q2s], tag="k2")
            for u in range(4):
                p = (self.pC, self.pD)[u % 2]
                t(lambda e, u=u, p=p: e.matmul(
                    p.ap[0:1, 0:512], self.onesb.ap[0:64, :],
                    self.q2s.ap[0:64, u * 512:(u + 1) * 512], start=True, stop=True),
                  r=[self.onesb, self.q2s], w=[p], tag="knrm")
                v(lambda e, u=u, p=p: e.reduce_max(self.st1.ap[0:1, u:u + 1],
                                                   p.ap[0:1, 0:512], AX.X),
                  r=[p, self.st1], w=[self.st1])
            v(lambda e, g_=g_: e.reduce_max(self.st2.ap[0:1, g_:g_ + 1],
                                            self.st1.ap[0:1, 0:4], AX.X),
              r=[self.st1, ("km2",)], w=[("km2",), self.st2], tag="km2")

    def phaseC(self, s):
        v, a, t = self.v, self.a, self.t
        for hp in range(8):
            if s == 0:
                self.pump_wq(9)
            kv = hp // 2
            v(lambda e, hp=hp: e.tensor_mul(self.q2s.ap, self.qT.ap[:, hp, :],
                                            self.qT.ap[:, hp, :]),
              r=[self.qT, self.q2s], w=[self.q2s], tag="q2")
            for hh in range(2):
                base = 64 * hh
                for u in range(4):
                    p = (self.pC, self.pD)[u % 2]
                    t(lambda e, base=base, u=u, p=p: e.matmul(
                        p.ap[0:1, 0:512], self.onesb.ap[base:base + 64, :],
                        self.q2s.ap[base:base + 64, u * 512:(u + 1) * 512],
                        start=True, stop=True,
                        tile_position=(base, 0)),
                      r=[self.onesb, self.q2s], w=[p], tag="qnrm")
                    v(lambda e, u=u, p=p: e.reduce_max(self.st1.ap[0:1, u:u + 1],
                                                       p.ap[0:1, 0:512], AX.X),
                      r=[p, self.st1], w=[self.st1])
                v(lambda e, hh=hh: e.reduce_max(self.st1.ap[0:1, 4 + hh:5 + hh],
                                                self.st1.ap[0:1, 0:4], AX.X),
                  r=[self.st1], w=[self.st1])
                v(lambda e, hh=hh, kv=kv: e.tensor_mul(
                    self.st1.ap[0:1, 4 + hh:5 + hh], self.st1.ap[0:1, 4 + hh:5 + hh],
                    self.st2.ap[0:1, kv:kv + 1]),
                  r=[self.st1, self.st2], w=[self.st1])
                a(lambda e, hh=hh: e.activation(self.st1.ap[0:1, 6 + hh:7 + hh],
                                                self.st1.ap[0:1, 4 + hh:5 + hh], AF.Sqrt),
                  r=[self.st1], w=[self.st1])
            v(lambda e: e.tensor_scalar(self.st1.ap[0:1, 6:8], self.st1.ap[0:1, 6:8],
                                        -1.0, None, AO.mult),
              r=[self.st1], w=[self.st1], tag="negch")
            self.dget(lambda e, hp=hp: e.dma_start(
                out=self.bounce[2:3, 2 * hp:2 * hp + 2], in_=self.st1.ap[0:1, 6:8]),
                 r=[self.st1], w=[("bnc", 2, hp)], tag="ch_out")
            self.dget(lambda e, hp=hp: e.dma_start(
                out=self.chb.ap[:, 2 * hp:2 * hp + 2],
                in_=self.bounce[2:3, 2 * hp:2 * hp + 2].broadcast_to([128, 2])),
                 r=[("bnc", 2, hp), self.chb], w=[self.chb], tag="ch_in")
            for qg in range(2):
                pas = ((self.paA0, self.paA1), (self.paB0, self.paB1))
                for kc in range(16):
                    bA = pas[0][kc % 2]
                    bB = pas[1][kc % 2]
                    for colh in range(2):
                        q0 = qg * 1024 + colh * 512
                        pAk = ("pA", colh)
                        pBk = ("pB", colh)
                        t(lambda e, hp=hp, kv=kv, kc=kc, q0=q0, colh=colh: e.matmul(
                            self.pA.ap[:, colh * 512:(colh + 1) * 512],
                            self.ktil.ap[0:64, kv, kc * 128:(kc + 1) * 128],
                            self.qT.ap[0:64, hp, q0:q0 + 512],
                            start=True, stop=True, tile_position=(0, 0)),
                          r=[self.ktil, self.qT], w=[pAk], tag="qkA")
                        t(lambda e, hp=hp, kv=kv, kc=kc, q0=q0, colh=colh: e.matmul(
                            self.pB.ap[:, colh * 512:(colh + 1) * 512],
                            self.ktil.ap[64:128, kv, kc * 128:(kc + 1) * 128],
                            self.qT.ap[64:128, hp, q0:q0 + 512],
                            start=True, stop=True, tile_position=(64, 0)),
                          r=[self.ktil, self.qT], w=[pBk], tag="qkB")
                        a(lambda e, hp=hp, bA=bA, colh=colh: e.activation(
                            bA.ap[:, colh * 512:(colh + 1) * 512],
                            self.pA.ap[:, colh * 512:(colh + 1) * 512], AF.Exp,
                            bias=self.chb.ap[:, 2 * hp:2 * hp + 1]),
                          r=[pAk, self.chb], w=[(id(bA), colh)] + list(bA.keys), tag="expA")
                        a(lambda e, hp=hp, bB=bB, colh=colh: e.activation(
                            bB.ap[:, colh * 512:(colh + 1) * 512],
                            self.pB.ap[:, colh * 512:(colh + 1) * 512], AF.Exp,
                            bias=self.chb.ap[:, 2 * hp + 1:2 * hp + 2]),
                          r=[pBk, self.chb], w=[(id(bB), colh)] + list(bB.keys), tag="expB")
                        t(lambda e, kv=kv, kc=kc, colh=colh, bA=bA: e.matmul(
                            self.pC.ap[0:65, colh * 512:(colh + 1) * 512],
                            self.vT.ap[:, kv, kc * 65:kc * 65 + 65],
                            bA.ap[:, colh * 512:(colh + 1) * 512],
                            start=(kc == 0), stop=(kc == 15)),
                          r=[self.vT, (id(bA), colh)], w=[("pC", colh)], tag="pvA")
                        t(lambda e, kv=kv, kc=kc, colh=colh, bB=bB: e.matmul(
                            self.pD.ap[0:65, colh * 512:(colh + 1) * 512],
                            self.vT.ap[:, kv, kc * 65:kc * 65 + 65],
                            bB.ap[:, colh * 512:(colh + 1) * 512],
                            start=(kc == 0), stop=(kc == 15)),
                          r=[self.vT, (id(bB), colh)], w=[("pD", colh)], tag="pvB")
                for hh, pv in ((0, self.pC), (1, self.pD)):
                    pvk = [("pC", 0), ("pC", 1)] if hh == 0 else [("pD", 0), ("pD", 1)]
                    dent = (self.h2t, self.sqd)[hh]
                    v(lambda e, pv=pv, dent=dent: e.reciprocal(dent.ap[64:65, :],
                                                               pv.ap[64:65, 0:1024]),
                      r=pvk + [dent], w=[dent], tag="denrec")
                    self.dget(lambda e, hh=hh, dent=dent: e.dma_start(
                        out=self.bounce[hh:hh + 1, :],
                        in_=dent.ap[64:65, :]),
                         r=[dent], w=[("bnc", hh)], tag="den_out")
                    self.dget(lambda e, hh=hh: e.dma_start(
                        out=self.recb.ap[:, hh, :],
                        in_=self.bounce[hh:hh + 1, :].broadcast_to([64, 1024])),
                         r=[("bnc", hh), self.recb], w=[self.recb], tag="den_in")
                    if hh == 0:
                        v(lambda e, hp=hp, qg=qg, pv=pv: e.tensor_tensor(
                            self.attnT.ap[0:64, hp, qg * 1024:(qg + 1) * 1024],
                            pv.ap[0:64, 0:1024], self.recb.ap[:, 0, :], AO.mult),
                          r=pvk + [self.recb], w=[self.attnT], tag="nrmA")
                    else:
                        v(lambda e, qg=qg, pv=pv: e.tensor_tensor(
                            self.ntmp.ap[:, qg, :], pv.ap[0:64, 0:1024],
                            self.recb.ap[:, 1, :], AO.mult),
                          r=pvk + [self.recb, self.ntmp], w=[self.ntmp], tag="nrmB")
                        self.dget(lambda e, hp=hp, qg=qg: e.dma_start(
                            out=self.attnT.ap[64:128, hp, qg * 1024:(qg + 1) * 1024],
                            in_=self.ntmp.ap[:, qg, :]),
                             r=[self.ntmp], w=[self.attnT], tag="nrmBmv")

    def phaseD(self, s):
        v, a, t = self.v, self.a, self.t
        for g_ in range(4):
            if s == 0:
                self.pump_wq(4)
            for dc in range(8):
                p = (self.pA, self.pB)[dc % 2]
                for hp in range(8):
                    t(lambda e, g_=g_, dc=dc, hp=hp, p=p: e.matmul(
                        p.ap[:, 0:512], self.owT.ap[:, hp, dc * 128:(dc + 1) * 128],
                        self.attnT.ap[:, hp, g_ * 512:(g_ + 1) * 512],
                        start=(hp == 0), stop=(hp == 7)),
                      r=[self.owT, self.attnT], w=[p], tag="mmo")
                cga = (2 * 8 + dc) * 4 + s
                a(lambda e, dc=dc, p=p, cga=cga, s=s: e.activation(
                    self.gd2.ap[:, 0, dc, :], p.ap[:, 0:512], AF.Identity,
                    bias=self.gobT.ap[:, dc * 4 + s:dc * 4 + s + 1],
                    scale=self.modT.ap[:, cga:cga + 1]),
                  r=[p, self.gobT, self.modT], w=[self.gd2], tag="gdev")
            for t4 in range(4):
                tc = g_ * 4 + t4
                buf = t4 % 2
                xk = ("xio", buf)
                self.dget(lambda e, s=s, tc=tc, buf=buf: e.dma_start(
                    out=self.xio.ap[:, buf, :],
                    in_=self.x_in[s, tc * 128:(tc + 1) * 128, :]),
                     r=[xk], w=[xk], tag="ldx2")
                p2 = (self.pC, self.pD)[t4 % 2]
                for dc in range(8):
                    t(lambda e, dc=dc, t4=t4, p2=p2: e.transpose(
                        p2.ap.bitcast(BF16)[:, dc * 128:(dc + 1) * 128],
                        self.gd2.ap[:, 0, dc, t4 * 128:(t4 + 1) * 128], self.identb.ap),
                      r=[self.gd2, self.identb], w=[p2], tag="trD")
                v(lambda e, buf=buf, p2=p2: e.tensor_tensor(
                    self.xio.ap[:, buf, :], self.xio.ap[:, buf, :],
                    p2.ap.bitcast(BF16)[:, 0:1024], AO.add),
                  r=[xk, p2], w=[xk], tag="resadd")
                self.dput(lambda e, tc=tc, buf=buf: e.dma_start(
                    out=self.xnew_d[tc * 128:(tc + 1) * 128, :],
                    in_=self.xio.ap[:, buf, :]),
                     r=[xk], w=[("xn_d", tc)], tag="stxn")
                if "xnew" in self.dbg and s == 0:
                    self.dput(lambda e, tc=tc, buf=buf: e.dma_start(
                        out=self.xnew_dbg[tc * 128:(tc + 1) * 128, :],
                        in_=self.xio.ap[:, buf, :]),
                         r=[xk], w=[("xnd", tc)], tag="dbgxn")

    def phaseE(self, s):
        v, a, t = self.v, self.a, self.t
        self.dget(lambda e, s=s: e.dma_start(
            out=self.cb1r.ap, in_=self.mod_d[s:s + 1, 4 * D:5 * D].broadcast_to([128, D])),
             r=[("mod_d", 8), ("mod_d", 9), self.cb1r], w=[self.cb1r], tag="cb1r")
        self.dget(lambda e, s=s: e.dma_start(
            out=self.sbr.ap, in_=self.mod_d[s:s + 1, 3 * D:4 * D].broadcast_to([128, D])),
             r=[("mod_d", 6), ("mod_d", 7), self.sbr], w=[self.sbr], tag="sbr")
        for half in range(2):
            for tcl in range(8):
                tc = half * 8 + tcl
                buf = tcl % 2
                xk = ("xio", buf)
                self.dget(lambda e, tc=tc, buf=buf: e.dma_start(
                    out=self.xio.ap[:, buf, :],
                    in_=self.xnew_d[tc * 128:(tc + 1) * 128, :]),
                     r=[xk, ("xn_d", tc)], w=[xk], tag="ldxn")
                sc_, bi_ = self.ln_stats(self.xio.ap[:, buf, :], xk, tag="ln2")
                a(lambda e, buf=buf, sc_=sc_, bi_=bi_: e.activation(
                    self.h2t.ap, self.xio.ap[:, buf, :], AF.Identity,
                    bias=bi_, scale=sc_),
                  r=[xk, self.st2, self.h2t], w=[self.h2t], tag="xn2")
                v(lambda e: e.tensor_mul(self.h2t.ap, self.h2t.ap, self.cb1r.ap),
                  r=[self.h2t, self.cb1r], w=[self.h2t])
                v(lambda e: e.tensor_add(self.h2t.ap, self.h2t.ap, self.sbr.ap),
                  r=[self.h2t, self.sbr], w=[self.h2t])
                for so in (0, 512):
                    self.emit_quant_sl(self.h2t.ap[:, so:so + 512], self.h2t.keys[0],
                                       self.hq2.ap[:, so:so + 512], self.hq2.keys[0], 512)
                for c in range(8):
                    pbk = ("pB", c % 2)
                    t(lambda e, c=c: e.transpose(
                        self.pB.ap.bitcast(BF16)[:, c % 2 * 1024:c % 2 * 1024 + 128],
                        self.hq2.ap[:, c * 128:(c + 1) * 128],
                        self.identb.ap), r=[self.hq2, self.identb], w=[pbk], tag="trh")
                    v(lambda e, c=c, tcl=tcl: e.tensor_copy(
                        self.hqT.ap[:, c, tcl * 128:(tcl + 1) * 128],
                        self.pB.ap.bitcast(BF16)[:, c % 2 * 1024:c % 2 * 1024 + 128]),
                      r=[pbk], w=[self.hqT], tag="hqcp")
            chunks = [(0, 512), (512, 512), (1024, 512), (1536, 512), (2048, 512),
                      (2560, 256)]
            for ici, (woff, w_) in enumerate(chunks):
                wgt = self.wgs[ici % 2]
                wut = self.wus[ici % 2]
                self.dget(lambda e, woff=woff, w_=w_, wgt=wgt: e.dma_start(
                    out=wgt.ap[:, :, 0:w_],
                    in_=self.wgqT_d[:, woff:woff + w_].rearrange("(c p) n -> p c n", p=128)),
                     r=[wgt, ("wgq",)], w=[wgt], tag="ldwg")
                self.dget(lambda e, woff=woff, w_=w_, wut=wut: e.dma_start(
                    out=wut.ap[:, :, 0:w_],
                    in_=self.wuqT_d[:, woff:woff + w_].rearrange("(c p) n -> p c n", p=128)),
                     r=[wut, ("wuq",)], w=[wut], tag="ldwu")
                for tcl in range(8):
                    p = (self.pC, self.pD, self.pA)[(ici * 8 + tcl) % 3]
                    pk = [(p.keys[0][0], 0), (p.keys[0][0], 1)]
                    for c in range(8):
                        t(lambda e, c=c, tcl=tcl, wgt=wgt, w_=w_, p=p: e.matmul(
                            p.ap[:, 0:w_], self.hqT.ap[:, c, tcl * 128:(tcl + 1) * 128],
                            wgt.ap[:, c, 0:w_], start=(c == 0), stop=(c == 7)),
                          r=[self.hqT, wgt], w=[pk[0]], tag="mmg")
                    for c in range(8):
                        t(lambda e, c=c, tcl=tcl, wut=wut, w_=w_, p=p: e.matmul(
                            p.ap[:, 512:512 + w_],
                            self.hqT.ap[:, c, tcl * 128:(tcl + 1) * 128],
                            wut.ap[:, c, 0:w_], start=(c == 0), stop=(c == 7)),
                          r=[self.hqT, wut], w=[pk[1]], tag="mmu")
                    a(lambda e, w_=w_, p=p: e.activation(self.sg.ap[:, 0:w_],
                                                         p.ap[:, 0:w_], AF.Silu),
                      r=[pk[0], self.sg], w=[self.sg], tag="silu")
                    v(lambda e, w_=w_, p=p: e.tensor_tensor(
                        self.acts.ap[:, 0:w_], self.sg.ap[:, 0:w_],
                        p.ap[:, 512:512 + w_], AO.mult),
                      r=[self.sg, pk[1], self.acts], w=[self.acts], tag="actm")
                    self.emit_quant(Tile(self.acts.ap[:, 0:w_], self.acts.keys),
                                    Tile(self.qT3.ap[:, 0:w_], self.qT3.keys), w_,
                                    qout=True)
                    for u in range(w_ // 128):
                        c2 = (woff + u * 128) // 128
                        pbk = ("pB", u % 2)
                        t(lambda e, u=u: e.transpose(
                            self.pB.ap.bitcast(BF16)[:, u % 2 * 1024:u % 2 * 1024 + 128],
                            self.qT3.ap[:, u * 128:(u + 1) * 128],
                            self.identb.ap), r=[self.qT3, self.identb], w=[pbk], tag="tra")
                        v(lambda e, c2=c2, tcl=tcl, u=u: e.tensor_copy(
                            self.actqT.ap[:, c2, tcl * 128:(tcl + 1) * 128],
                            self.pB.ap.bitcast(BF16)[:, u % 2 * 1024:u % 2 * 1024 + 128]),
                          r=[pbk], w=[self.actqT], tag="aqcp")
            for dc in range(8):
                wdt = self.wds[dc % 2]
                self.dget(lambda e, dc=dc, wdt=wdt: e.dma_start(
                    out=wdt.ap,
                    in_=self.wdqT_d[:, dc * 128:(dc + 1) * 128]
                        .rearrange("(c p) n -> p c n", p=128)),
                     r=[wdt, ("wdq",)], w=[wdt], tag="ldwd")
                for g2 in range(2):
                    p = (self.pC, self.pD)[g2]
                    for c2 in range(22):
                        t(lambda e, c2=c2, g2=g2, wdt=wdt, p=p: e.matmul(
                            p.ap[:, 0:512], wdt.ap[:, c2, :],
                            self.actqT.ap[:, c2, g2 * 512:(g2 + 1) * 512],
                            start=(c2 == 0), stop=(c2 == 21)),
                          r=[wdt, self.actqT], w=[p], tag="mmd")
                    cgb = (5 * 8 + dc) * 4 + s
                    a(lambda e, dc=dc, g2=g2, p=p, cgb=cgb: e.activation(
                        self.gd2.ap[:, g2, dc, :], p.ap[:, 0:512], AF.Identity,
                        scale=self.modT.ap[:, cgb:cgb + 1]),
                      r=[p, self.modT], w=[self.gd2], tag="fdev")
            for g2 in range(2):
                for t4 in range(4):
                    tcl = g2 * 4 + t4
                    tc = half * 8 + tcl
                    buf = t4 % 2
                    xk = ("xio", buf)
                    self.dget(lambda e, tc=tc, buf=buf: e.dma_start(
                        out=self.xio.ap[:, buf, :],
                        in_=self.xnew_d[tc * 128:(tc + 1) * 128, :]),
                         r=[xk, ("xn_d", tc)], w=[xk], tag="ldxn2")
                    p2 = (self.pC, self.pD)[t4 % 2]
                    for dc in range(8):
                        t(lambda e, dc=dc, g2=g2, t4=t4, p2=p2: e.transpose(
                            p2.ap.bitcast(BF16)[:, dc * 128:(dc + 1) * 128],
                            self.gd2.ap[:, g2, dc, t4 * 128:(t4 + 1) * 128],
                            self.identb.ap),
                          r=[self.gd2, self.identb], w=[p2], tag="trE")
                    v(lambda e, buf=buf, p2=p2: e.tensor_tensor(
                        self.xio.ap[:, buf, :], self.xio.ap[:, buf, :],
                        p2.ap.bitcast(BF16)[:, 0:1024],
                        AO.add), r=[xk, p2], w=[xk], tag="fres")
                    self.dput(lambda e, s=s, tc=tc, buf=buf: e.dma_start(
                        out=self.out[s, tc * 128:(tc + 1) * 128, :],
                        in_=self.xio.ap[:, buf, :]),
                         r=[xk], w=[("out", s, tc)], tag="stout")

    def finish(self):
        sc = self.sc

        def final_wait(g):
            last = None
            for slot, n in sc.slot_n.items():
                last = g.wait_ge(sc.qsem[slot], 16 * n)
            return last
        sc.op("gpsimd", final_wait, tag="final_wait")
        sc.finalize(self.es)
        blocks = self.es.enter_context(self.nc.Block())
        for e in ENGS:
            getattr(blocks, e)(lambda eng, e=e: sc.emit(e, eng))
        self.es.close()



def kernel(L0_attn_hidden_states, L0_attn_scaling, L0_attn_qkv_w, L0_attn_qkv_b,
           L0_attn_o_w, L0_attn_o_b, L0_ffn_w_gate, L0_ffn_w_up, L0_ffn_w_down,
           L0_c, L0_wmod):
    from concourse.bass_utils import run_bass_kernel_spmd
    x = np.ascontiguousarray(np.asarray(L0_attn_hidden_states, dtype=np.float32))
    c = np.ascontiguousarray(np.asarray(L0_c, dtype=np.float32))
    common = {
        "scaling": np.ascontiguousarray(np.asarray(L0_attn_scaling, np.float32).reshape(HD, 1)),
        "qkv_w": np.ascontiguousarray(np.asarray(L0_attn_qkv_w, np.float32)),
        "qkv_b": np.ascontiguousarray(np.asarray(L0_attn_qkv_b, np.float32)),
        "o_w": np.ascontiguousarray(np.asarray(L0_attn_o_w, np.float32)),
        "o_b": np.ascontiguousarray(np.asarray(L0_attn_o_b, np.float32)),
        "w_gate": np.ascontiguousarray(np.asarray(L0_ffn_w_gate, np.float32)),
        "w_up": np.ascontiguousarray(np.asarray(L0_ffn_w_up, np.float32)),
        "w_down": np.ascontiguousarray(np.asarray(L0_ffn_w_down, np.float32)),
        "wmod": np.ascontiguousarray(np.asarray(L0_wmod, np.float32)),
    }
    b = BuildK().build()
    in_maps = [dict(common, x=np.ascontiguousarray(x[4 * i:4 * i + 4]),
                    c=np.ascontiguousarray(c[4 * i:4 * i + 4])) for i in range(8)]
    res = run_bass_kernel_spmd(b.nc, in_maps, core_ids=list(range(8)))
    return np.concatenate([np.asarray(r["out"]) for r in res.results], axis=0)


# version 04
# speedup vs baseline: 1.1924x; 1.1924x over previous
import contextlib
import numpy as np
import concourse.bass as bass
import concourse.mybir as mybir
from concourse.alu_op_type import AluOpType as AO

F32 = mybir.dt.float32
BF16 = mybir.dt.bfloat16
AF = mybir.ActivationFunctionType
AX = mybir.AxisListType

ENGS = ("sync", "tensor", "vector", "scalar", "gpsimd")
DMA_QUEUES = {"sync": "qs", "gpsimd": "qg", "scalar": "qa"}


R_DMA_SEMS = 10


class Sched:
    def __init__(self, nc):
        self.nc = nc
        self.ops = []
        self.eng_ops = {e: [] for e in ENGS}
        self.res = {}
        self.qcount = {q: 0 for q in DMA_QUEUES.values()}
        self.slot_n = {}
        self.slot_consumers = {}

    def op(self, eng, fn, reads=(), writes=(), dma=False, tag=""):
        idx = len(self.ops)
        q = DMA_QUEUES[eng] if dma else None
        o = {"i": idx, "eng": eng, "fn": fn, "dma_q": q, "need_inc": dma,
             "waits": set(), "tag": tag}
        deps = set()
        for r in reads:
            e = self.res.get(r)
            if e and e["w"] is not None:
                deps.add(e["w"])
        for w in writes:
            e = self.res.get(w)
            if e:
                if e["w"] is not None:
                    deps.add(e["w"])
                deps.update(e["r"])
        if dma:
            slot = (q, self.qcount[q] % R_DMA_SEMS)
            self.qcount[q] += 1
            n = self.slot_n.get(slot, 0)
            o["slot"], o["sval"] = slot, 16 * (n + 1)
            self.slot_n[slot] = n + 1
            deps.update(self.slot_consumers.get(slot, ()))
            self.slot_consumers[slot] = set()
        rdeps = set()
        for r in reads:
            e = self.res.get(r)
            if e and e["w"] is not None:
                rdeps.add(e["w"])
        for d in deps:
            dop = self.ops[d]
            if dop["dma_q"] is not None:
                self.slot_consumers.setdefault(dop["slot"], set()).add(idx)
            elif dop["eng"] == eng:
                if (eng != "tensor" and d in rdeps
                        and len(self.eng_ops[eng]) - dop["eng_pos"] <= 1):
                    pass
                else:
                    continue
            dop["need_inc"] = True
            o["waits"].add(d)
        o["eng_pos"] = len(self.eng_ops[eng])
        self.ops.append(o)
        self.eng_ops[eng].append(o)
        for r in reads:
            self.res.setdefault(r, {"w": None, "r": set()})["r"].add(idx)
        for w in writes:
            self.res[w] = {"w": idx, "r": set()}
        return idx

    def dma(self, eng, fn, reads=(), writes=(), tag=""):
        return self.op(eng, fn, reads, writes, dma=True, tag=tag)

    PARANOID = None

    def finalize(self, es):
        nc = self.nc
        if self.PARANOID:
            lo, hi = self.PARANOID
            for i in range(max(lo, 1), min(hi, len(self.ops))):
                self.ops[i - 1]["need_inc"] = True
                self.ops[i]["waits"].add(i - 1)
        self.esem = {e: es.enter_context(nc.semaphore("sem_" + e)) for e in ENGS}
        self.qsem = {}
        for q in set(DMA_QUEUES.values()):
            for si in range(R_DMA_SEMS):
                self.qsem[(q, si)] = es.enter_context(
                    nc.semaphore(f"sem_{q}_{si}"))
        for e in ENGS:
            n = 0
            for o in self.eng_ops[e]:
                if o["need_inc"] and o["dma_q"] is None:
                    n += 1
                    o["inc_seq"] = n
        for o in self.ops:
            wl = {}
            for d in o["waits"]:
                dop = self.ops[d]
                if dop["dma_q"] is not None:
                    sem, val = self.qsem[dop["slot"]], dop["sval"]
                else:
                    sem, val = self.esem[dop["eng"]], dop["inc_seq"]
                k = id(sem)
                if k not in wl or wl[k][1] < val:
                    wl[k] = (sem, val)
            o["wlist"] = list(wl.values())

    def emit(self, eng_name, engine):
        seen = {}
        for o in self.eng_ops[eng_name]:
            for sem, val in o["wlist"]:
                k = id(sem)
                if seen.get(k, -1) >= val:
                    continue
                seen[k] = val
                engine.wait_ge(sem, val)
            inst = o["fn"](engine)
            if o["need_inc"]:
                assert inst is not None, f"op {o['i']} {o['tag']} returned no inst"
                if o["dma_q"] is not None:
                    inst.then_inc(self.qsem[o["slot"]], 16)
                else:
                    inst.then_inc(self.esem[o["eng"]], 1)

    def stats(self):
        return {e: len(self.eng_ops[e]) for e in ENGS} | {
            "incs": sum(1 for o in self.ops if o["need_inc"]),
            "total": len(self.ops), "q": dict(self.qcount)}


class Tile:
    def __init__(self, ap, keys):
        self.ap = ap
        self.keys = tuple(keys)

    def __getitem__(self, idx):
        return self.ap[idx]


NSEQ = 4
S = 2048
D = 1024
NH, NKV, HD = 16, 4, 64
QKV = NH * HD + 2 * NKV * HD
II = 2816
R_SOFT = 1.442695041
EPS = 1e-5
BLK = 1024


class Build:
    def __init__(self, dbg=(), upto=99, nseq=NSEQ):
        self.dbg = set(dbg)
        self.upto = upto
        self.nseq = nseq
        self.nc = bass.Bass("TRN2", target_bir_lowering=False)
        self.es = contextlib.ExitStack()
        self.sc = Sched(self.nc)

    def dram(self, name, shape, dtype=F32, kind="ExternalInput"):
        return self.nc.dram_tensor(name, list(shape), dtype, kind=kind).ap()

    def sb(self, name, shape, dtype=F32):
        ap = self.es.enter_context(self.nc.sbuf_tensor(name, list(shape), dtype))
        return Tile(ap[...], [("t", name)])

    def carve(self, off_elems, shape, dtype=F32):
        n = int(np.prod(shape[1:]))
        width = n * (2 if dtype == F32 else 1)
        ap = self.pool[:, off_elems:off_elems + width]
        if dtype != BF16:
            ap = ap.bitcast(dtype)
        if len(shape) > 2:
            kw = {}
            names = []
            for i, s_ in enumerate(shape[1:]):
                names.append(chr(ord('a') + i))
                kw[names[i]] = s_
            ap = ap.rearrange(f"p ({' '.join(names)}) -> p {' '.join(names)}", **kw)
        b0, b1 = off_elems // BLK, (off_elems + width - 1) // BLK
        return Tile(ap, [("P", b) for b in range(b0, b1 + 1)])

    def ps(self, name, shape):
        ap = self.es.enter_context(self.nc.psum_tensor(name, list(shape), F32))
        return Tile(ap[...], [(name, 0), (name, 1)])

    def _keys(self, tiles):
        ks = []
        for t in tiles:
            if isinstance(t, Tile):
                ks.extend(t.keys)
            else:
                ks.append(t)
        return ks

    def v(self, fn, r=(), w=(), tag=""):
        return self.sc.op("vector", fn, self._keys(r), self._keys(w), tag=tag)

    def a(self, fn, r=(), w=(), tag=""):
        return self.sc.op("scalar", fn, self._keys(r), self._keys(w), tag=tag)

    def t(self, fn, r=(), w=(), tag=""):
        return self.sc.op("tensor", fn, self._keys(r), self._keys(w), tag=tag)

    def g(self, fn, r=(), w=(), tag=""):
        return self.sc.op("gpsimd", fn, self._keys(r), self._keys(w), tag=tag)

    def dget(self, fn, r=(), w=(), tag=""):
        return self.sc.dma("sync", fn, self._keys(r), self._keys(w), tag=tag)

    def dput(self, fn, r=(), w=(), tag=""):
        return self.sc.dma("gpsimd", fn, self._keys(r), self._keys(w), tag=tag)

    def dump(self, tag, ap, shape, dtype=F32, deps=()):
        if tag not in self.dbg:
            return
        t = self.dram("dbg_" + tag, shape, dtype, kind="ExternalOutput")
        self.dput(lambda g, t=t, ap=ap: g.dma_start(out=t[...], in_=ap),
                  r=list(deps), tag="dump_" + tag)


POOL_N = 69632

OFF_qT = 0
OFF_kvT = 16384
OFF_h1T = 24576
OFF_ktil = 24576
OFF_vT = 32768
OFF_paA0 = 37888
OFF_paA1 = 38912
OFF_paB0 = 39936
OFF_paB1 = 40960
OFF_xnb = 40960
OFF_attnT = 45056
OFF_wgs0 = 0
OFF_wgs1 = 4096
OFF_wus0 = 8192
OFF_wus1 = 12288
OFF_wds0 = 16384
OFF_wds1 = 19456
OFF_gd2 = 22528
OFF_hqT = 30720
OFF_cb1r = 38912
OFF_sbr = 40960
OFF_hq2 = 45056
OFF_actqT = 46080
OFF_wtmp0 = 0
OFF_wtmp1 = 5632
OFF_wbf0 = 11264
OFF_wbf1 = 14336
OFF_stg = 17408
OFF_cbuf = 18432
OFF_mslab = 20480
OFF_wmbuf0 = 24576
OFF_wmbuf1 = 32768
OFF_wmbf0 = 40960
OFF_wmbf1 = 45056
OFF_csb = 49152


class BuildK(Build):
    def build(self):
        nc, sc = self.nc, self.sc
        self.x_in = self.dram("x", (NSEQ, S, D))
        self.c_in = self.dram("c", (NSEQ, D))
        self.scaling = self.dram("scaling", (HD, 1))
        self.qkv_w = self.dram("qkv_w", (QKV, D))
        self.qkv_b = self.dram("qkv_b", (QKV,))
        self.o_w = self.dram("o_w", (D, D))
        self.o_b = self.dram("o_b", (D,))
        self.w_gate = self.dram("w_gate", (II, D))
        self.w_up = self.dram("w_up", (II, D))
        self.w_down = self.dram("w_down", (D, II))
        self.wmod = self.dram("wmod", (D, 6 * D))
        self.out = self.dram("out", (NSEQ, S, D), kind="ExternalOutput")
        self.wgqT_d = self.dram("wgqT_d", (D, II), BF16,
                                kind="ExternalOutput" if "wgq_d" in self.dbg else "Internal")
        self.wuqT_d = self.dram("wuqT_d", (D, II), BF16, kind="Internal")
        self.wdqT_d = self.dram("wdqT_d", (II, D), BF16, kind="Internal")
        self.xnew_d = self.dram("xnew_d", (S, D), F32, kind="Internal")
        self.mod_d = self.dram("mod_d", (4, 6 * D), F32, kind="Internal")
        self.bounce = self.dram("bounce", (8, 1024), F32, kind="Internal")
        if "xnew" in self.dbg:
            self.xnew_dbg = self.dram("dbg_xnew", (S, D), F32, kind="ExternalOutput")

        self.pool = self.es.enter_context(
            self.nc.sbuf_tensor("POOL", [128, POOL_N], BF16))[...]
        self.identb = self.sb("identb", (128, 128), BF16)
        self.onesb = self.sb("onesb", (128, 1), BF16)
        self.modT = self.sb("modT", (128, 192))
        self.qscT = self.sb("qscT", (128, 12))
        self.qbsT = self.sb("qbsT", (128, 12))
        self.obT = self.sb("obT", (128, 8))
        self.gobT = self.sb("gobT", (128, 32))
        self.sc64 = self.sb("sc64", (64, 1))
        self.epsT = self.sb("epsT", (128, 1))
        self.stgr = self.sb("stgr", (128, 2, 128), BF16)
        self.sc128 = self.sb("sc128", (128, 1))
        self.wqkvT = self.sb("wqkvT", (128, 8, QKV), BF16)
        self.owT = self.sb("owT", (128, 8, D), BF16)
        self.qU = self.sb("qU", (128, 512))
        self.qAU = self.sb("qAU", (128, 512))
        self.qT1 = self.sb("qT1", (128, 512), BF16)
        self.qT2 = self.sb("qT2", (128, 512), BF16)
        self.qT3 = self.sb("qT3", (128, 512), BF16)
        self.qT4 = self.sb("qT4", (128, 512), BF16)
        self.qBM = self.sb("qBM", (128, 32))
        self.qR = self.sb("qR", (128, 32))
        self.qS2 = self.sb("qS2", (128, 32))
        self.st1 = self.sb("st_a", (128, 8))
        self.st2 = self.sb("st_b", (128, 8))
        self.chb = self.sb("chb", (128, 16))
        self.recb = self.sb("recb", (64, 2, 1024))
        self.ntmp = self.sb("ntmp", (64, 2, 1024), BF16)
        self.xio = self.sb("xio", (128, 2, D))
        self.pA = self.ps("pA", (128, 1024))
        self.pB = self.ps("pB", (128, 1024))
        self.pC = self.ps("pC", (128, 1024))
        self.pD = self.ps("pD", (128, 1024))

        cv = self.carve
        self.qT = cv(OFF_qT, (128, 8, S), BF16)
        self.kvT = cv(OFF_kvT, (128, 4, S), BF16)
        self.h1T = cv(OFF_h1T, (128, 8, S), BF16)
        self.xnb = cv(OFF_xnb, (128, 4, D), BF16)
        self.attnT = cv(OFF_attnT, (128, 8, S), BF16)
        self.ktil = cv(OFF_ktil, (128, 4, S), BF16)
        self.vT = cv(OFF_vT, (128, 4, 1040), BF16)
        self.paA0 = cv(OFF_paA0, (128, 1024), BF16)
        self.paA1 = cv(OFF_paA1, (128, 1024), BF16)
        self.paB0 = cv(OFF_paB0, (128, 1024), BF16)
        self.paB1 = cv(OFF_paB1, (128, 1024), BF16)
        self.wgs = [cv(OFF_wgs0, (128, 8, 512), BF16), cv(OFF_wgs1, (128, 8, 512), BF16)]
        self.wus = [cv(OFF_wus0, (128, 8, 512), BF16), cv(OFF_wus1, (128, 8, 512), BF16)]
        self.wds = [cv(OFF_wds0, (128, 22, 128), BF16), cv(OFF_wds1, (128, 22, 128), BF16)]
        self.gd2 = cv(OFF_gd2, (128, 2, 8, 512), BF16)
        self.hqT = cv(OFF_hqT, (128, 8, 1024), BF16)
        self.cb1r = cv(OFF_cb1r, (128, D), F32)
        self.sbr = cv(OFF_sbr, (128, D), F32)
        self.hq2 = cv(OFF_hq2, (128, D), BF16)
        self.actqT = cv(OFF_actqT, (128, 22, 1024), BF16)
        self.q2s = cv(16384, (128, 2048), BF16)
        self.h2t = cv(63488, (128, D), F32)
        self.sg = cv(43008, (128, 512), F32)
        self.acts = cv(44032, (128, 512), F32)
        self.sqd = cv(61440, (128, D), F32)
        self.wqt = [cv(65536, (128, 512), F32), cv(66560, (128, 512), F32)]
        self.wqb = [cv(67584, (128, 512), BF16), cv(68096, (128, 512), BF16)]
        self.identf = cv(45056, (128, 128), F32)
        self.iot1 = cv(45312, (128, 128), F32)
        self.iot2 = cv(45568, (128, 128), F32)
        self.wtmp = [cv(OFF_wtmp0, (128, II), F32), cv(OFF_wtmp1, (128, II), F32)]
        self.wbf = [cv(OFF_wbf0, (128, II), BF16), cv(OFF_wbf1, (128, II), BF16)]
        self.stg = cv(OFF_stg, (128, 4, 128), BF16)
        self.cbuf = cv(OFF_cbuf, (128, D), F32)
        self.mslab = cv(OFF_mslab, (128, 2048), F32)
        self.wmbuf = [cv(OFF_wmbuf0, (128, 8, 512), F32), cv(OFF_wmbuf1, (128, 8, 512), F32)]
        self.wmbf = [cv(OFF_wmbf0, (128, 8, 512), BF16), cv(OFF_wmbf1, (128, 8, 512), BF16)]
        self.csf = cv(OFF_csb, (128, 8, 4), F32)

        self.setup()
        for s in range(self.nseq):
            if self.upto >= 1: self.phaseA(s)
            if self.upto >= 2: self.phaseB(s)
            if self.upto >= 3: self.phaseC(s)
            if self.upto >= 4: self.phaseD(s)
            if self.upto >= 5:
                self.pump_wq(200)
                self.phaseE(s)
        for name in self.dbg:
            if not hasattr(self, name) or not isinstance(getattr(self, name), Tile):
                continue
            tile = getattr(self, name)
            ap = tile.ap
            sh = list(ap.shape)
            self.dump(name, ap, sh, ap.dtype, deps=[tile])
        self.finish()
        return self

    def run(self, in_map, n_cores=1, **kw):
        from concourse.bass_utils import run_bass_kernel_spmd
        maps = [in_map] * n_cores if isinstance(in_map, dict) else in_map
        return run_bass_kernel_spmd(self.nc, maps, core_ids=list(range(n_cores)), **kw)

    def emit_quant(self, X, Q, W, xsl=None, qsl=None, rr=(), tag="q", qout=False):
        v, a = self.v, self.a
        NB = W // 16
        Xap = xsl(X.ap) if xsl else X.ap
        Qap = qsl(Q.ap) if qsl else Q.ap
        qU, qAU, qT1, qT2, qT3, qT4 = self.qU, self.qAU, self.qT1, self.qT2, self.qT3, self.qT4
        qBM, qR, qS2 = self.qBM, self.qR, self.qS2
        x3 = Xap.rearrange("p (g k) -> p g k", k=16)
        v(lambda e: e.tensor_reduce(qBM.ap[:, :NB], x3, AX.X, AO.max,
                                    apply_absolute_value=True),
          r=[X] + list(rr), w=[qBM], tag=tag + "bm")
        v(lambda e: e.tensor_scalar(qBM.ap[:, :NB], qBM.ap[:, :NB], 1.0 / 6.0, 1e-12,
                                    AO.mult, AO.max), r=[qBM], w=[qBM], tag=tag)
        v(lambda e: e.reciprocal(qR.ap[:, :NB], qBM.ap[:, :NB]), r=[qBM], w=[qR])
        v(lambda e: e.tensor_scalar(qS2.ap[:, :NB], qBM.ap[:, :NB], 0.5, None, AO.mult),
          r=[qBM], w=[qS2])
        v(lambda e: e.tensor_tensor(qU.ap[:, :W].rearrange("p (g k) -> p g k", k=16), x3,
                                    qR.ap[:, :NB].broadcast_to([128, NB, 16]), AO.mult),
          r=[X, qR], w=[qU], tag=tag + "u")
        a(lambda e: e.activation(qAU.ap[:, :W], qU.ap[:, :W], AF.Abs), r=[qU], w=[qAU])
        a(lambda e: e.activation(qT4.ap[:, :W], qU.ap[:, :W], AF.Sign), r=[qU], w=[qT4])
        v(lambda e: e.tensor_scalar(qT1.ap[:, :W], qAU.ap[:, :W], 0.25, None, AO.is_gt),
          r=[qAU], w=[qT1])
        v(lambda e: e.tensor_scalar(qT2.ap[:, :W], qAU.ap[:, :W], 0.75, None, AO.is_gt),
          r=[qAU], w=[qT2])
        v(lambda e: e.tensor_scalar(qT3.ap[:, :W], qAU.ap[:, :W], 1.25, None, AO.is_gt),
          r=[qAU], w=[qT3])
        v(lambda e: e.tensor_add(qT1.ap[:, :W], qT1.ap[:, :W], qT2.ap[:, :W]),
          r=[qT1, qT2], w=[qT1])
        v(lambda e: e.tensor_scalar(qT2.ap[:, :W], qAU.ap[:, :W], 1.75, None, AO.is_gt),
          r=[qAU], w=[qT2])
        v(lambda e: e.tensor_add(qT1.ap[:, :W], qT1.ap[:, :W], qT3.ap[:, :W]),
          r=[qT1, qT3], w=[qT1])
        v(lambda e: e.tensor_scalar(qT3.ap[:, :W], qAU.ap[:, :W], 2.5, 2.0, AO.is_gt,
                                    AO.mult), r=[qAU], w=[qT3])
        v(lambda e: e.tensor_add(qT1.ap[:, :W], qT1.ap[:, :W], qT2.ap[:, :W]),
          r=[qT1, qT2], w=[qT1])
        v(lambda e: e.tensor_scalar(qT2.ap[:, :W], qAU.ap[:, :W], 3.5, 2.0, AO.is_gt,
                                    AO.mult), r=[qAU], w=[qT2])
        v(lambda e: e.tensor_add(qT1.ap[:, :W], qT1.ap[:, :W], qT3.ap[:, :W]),
          r=[qT1, qT3], w=[qT1])
        v(lambda e: e.tensor_scalar(qT3.ap[:, :W], qAU.ap[:, :W], 5.0, 4.0, AO.is_gt,
                                    AO.mult), r=[qAU], w=[qT3])
        v(lambda e: e.tensor_add(qT1.ap[:, :W], qT1.ap[:, :W], qT2.ap[:, :W]),
          r=[qT1, qT2], w=[qT1])
        v(lambda e: e.tensor_tensor(qT4.ap[:, :W].rearrange("p (g k) -> p g k", k=16),
                                    qT4.ap[:, :W].rearrange("p (g k) -> p g k", k=16),
                                    qS2.ap[:, :NB].broadcast_to([128, NB, 16]), AO.mult),
          r=[qT4, qS2], w=[qT4])
        v(lambda e: e.tensor_add(qT1.ap[:, :W], qT1.ap[:, :W], qT3.ap[:, :W]),
          r=[qT1, qT3], w=[qT1])
        v(lambda e: e.tensor_mul(Qap, qT1.ap[:, :W], qT4.ap[:, :W]),
          r=[qT1, qT4], w=[Q], tag=tag + "fin")
    def ln_stats(self, xap, xtile, tag=""):
        v, a = self.v, self.a
        st1, st2, sqd = self.st1, self.st2, self.sqd
        v(lambda e: e.reduce_sum(st1.ap[:, 0:1], xap, AX.X), r=[xtile], w=[st1], tag=tag + "sum")
        a(lambda e: e.activation(sqd.ap[...], xap, AF.Square,
                                 accum_out=st1.ap[:, 1:2]), r=[xtile, st1], w=[sqd, st1])
        a(lambda e: e.activation(st1.ap[:, 3:4], st1.ap[:, 0:1], AF.Square,
                                 scale=1.0 / D), r=[st1], w=[st1], tag=tag + "m2")
        v(lambda e: e.tensor_scalar(st1.ap[:, 2:3], st1.ap[:, 0:1], 1.0 / D, None,
                                    AO.mult), r=[st1], w=[st1])
        v(lambda e: e.scalar_tensor_tensor(st1.ap[:, 5:6], st1.ap[:, 1:2], 1.0 / D,
                                           st1.ap[:, 3:4], AO.mult, AO.subtract),
          r=[st1], w=[st1], tag=tag + "var")
        a(lambda e: e.activation(st2.ap[:, 0:1], st1.ap[:, 5:6], AF.Ln,
                                 bias=self.epsT.ap),
          r=[st1, st2, self.epsT], w=[st2])
        a(lambda e: e.activation(st2.ap[:, 4:5], st2.ap[:, 0:1], AF.Exp, scale=-0.5),
          r=[st2], w=[st2], tag=tag + "rs")
        v(lambda e: e.scalar_tensor_tensor(st2.ap[:, 5:6], st1.ap[:, 2:3], -1.0,
                                           st2.ap[:, 4:5], AO.mult, AO.mult),
          r=[st1, st2], w=[st2])
        return st2.ap[:, 4:5], st2.ap[:, 5:6]

    def setup(self):
        v, a, t, g = self.v, self.a, self.t, self.g
        dget, dput = self.dget, self.dput
        nc = self.nc
        g(lambda e: e.iota(self.iot1.ap, [[1, 128]], channel_multiplier=0,
                           allow_small_or_imprecise_dtypes=True), w=[self.iot1])
        g(lambda e: e.iota(self.iot2.ap, [[0, 128]], channel_multiplier=1,
                           allow_small_or_imprecise_dtypes=True), w=[self.iot2])
        v(lambda e: e.tensor_tensor(self.identf.ap, self.iot1.ap, self.iot2.ap,
                                    AO.is_equal), r=[self.iot1, self.iot2], w=[self.identf])
        v(lambda e: e.tensor_copy(self.identb.ap, self.identf.ap),
          r=[self.identf], w=[self.identb])
        v(lambda e: e.memset(self.onesb.ap, 1.0), w=[self.onesb])
        v(lambda e: e.memset(self.epsT.ap, EPS), w=[self.epsT])
        dget(lambda e: e.dma_start(out=self.sc64.ap, in_=self.scaling[...]),
             w=[self.sc64], tag="ld_scaling")
        a(lambda e: e.activation(self.sc64.ap, self.sc64.ap, AF.Exp),
          r=[self.sc64], w=[self.sc64])
        v(lambda e: e.tensor_scalar(self.sc64.ap, self.sc64.ap, 1.0, None, AO.add),
          r=[self.sc64], w=[self.sc64])
        a(lambda e: e.activation(self.sc64.ap, self.sc64.ap, AF.Ln),
          r=[self.sc64], w=[self.sc64])
        v(lambda e: e.tensor_scalar(self.sc64.ap, self.sc64.ap, R_SOFT / 8.0, None,
                                    AO.mult), r=[self.sc64], w=[self.sc64])
        dget(lambda e: e.dma_start(out=self.bounce[3:4, 0:64], in_=self.sc64.ap),
             r=[self.sc64], w=[("bnc", 3)], tag="sp_out")
        dget(lambda e: e.dma_start(out=self.sc128.ap[0:64, :], in_=self.bounce[3:4, 0:64]),
             r=[("bnc", 3)], w=[self.sc128])
        dget(lambda e: e.dma_start(out=self.sc128.ap[64:128, :], in_=self.bounce[3:4, 0:64]),
             r=[("bnc", 3)], w=[self.sc128])
        v(lambda e: e.tensor_copy(self.qscT.ap[:, 0:8].rearrange("p (a b) -> p a b", a=1),
                                  self.sc128.ap.broadcast_to([128, 1, 8])),
          r=[self.sc128], w=[self.qscT])
        v(lambda e: e.memset(self.qscT.ap[:, 8:12], 1.0), r=[self.qscT], w=[self.qscT])
        dget(lambda e: e.dma_start(out=self.qbsT.ap,
                                   in_=self.qkv_b.rearrange("(c p) -> p c", p=128),
                                   allow_slow_non_contiguous=True),
             w=[self.qbsT], tag="ld_qkvb")
        v(lambda e: e.tensor_mul(self.qbsT.ap, self.qbsT.ap, self.qscT.ap),
          r=[self.qbsT, self.qscT], w=[self.qbsT])
        dget(lambda e: e.dma_start(out=self.obT.ap,
                                   in_=self.o_b.rearrange("(c p) -> p c", p=128),
                                   allow_slow_non_contiguous=True),
             w=[self.obT], tag="ld_ob")
        dget(lambda e: e.dma_start(out=self.cbuf.ap[0:4, :], in_=self.c_in[...]),
             w=[self.cbuf], tag="ld_c")
        for c8 in range(8):
            t(lambda e, c8=c8: e.transpose(self.pC.ap[:, 0:4],
                                           self.cbuf.ap[0:4, c8 * 128:(c8 + 1) * 128],
                                           self.identf.ap[0:4, 0:4]),
              r=[self.cbuf, self.identf], w=[self.pC], tag="trc")
            a(lambda e, c8=c8: e.activation(self.csf.ap[:, c8, :], self.pC.ap[:, 0:4],
                                            AF.Silu), r=[self.pC], w=[self.csf])
        for j in range(12):
            wmb = self.wmbuf[j % 2]
            wmf = self.wmbf[j % 2]
            dget(lambda e, j=j, wmb=wmb: e.dma_start(
                out=wmb.ap,
                in_=self.wmod[:, j * 512:(j + 1) * 512]
                    .rearrange("(c p) n -> p c n", p=128)),
                 r=[wmb], w=[wmb], tag="ld_wmod")
            for c in range(8):
                t(lambda e, c=c, wmb=wmb: e.matmul(
                    self.pD.ap[0:4, 0:512], self.csf.ap[:, c, :],
                    wmb.ap[:, c, :],
                    start=(c == 0), stop=(c == 7)),
                  r=[self.csf, wmb], w=[self.pD], tag="mm_mod")
            a(lambda e: e.activation(self.mslab.ap[0:4, 0:512], self.pD.ap[0:4, 0:512],
                                     AF.Copy), r=[self.pD], w=[self.mslab])
            vec = (j * 512) // 1024
            if vec in (1, 4):
                v(lambda e: e.tensor_scalar(self.mslab.ap[0:4, 0:512],
                                            self.mslab.ap[0:4, 0:512], 1.0, None, AO.add),
                  r=[self.mslab], w=[self.mslab])
            if vec in (2, 5):
                sl = self.mslab.ap[0:4, 0:512]
                tm = self.mslab.ap[0:4, 512:1024]
                sn = self.mslab.ap[0:4, 1024:1536]
                v(lambda e, sl=sl, tm=tm: e.scalar_tensor_tensor(tm, sl, -1.0, sl,
                                                                 AO.mult, AO.max),
                  r=[self.mslab], w=[self.mslab])
                v(lambda e, tm=tm: e.tensor_scalar(tm, tm, 0.25, None, AO.max),
                  r=[self.mslab], w=[self.mslab])
                v(lambda e, sl=sl, sn=sn: e.tensor_scalar(sn, sl, 1e38, -1.0,
                                                          AO.mult, AO.max),
                  r=[self.mslab], w=[self.mslab])
                v(lambda e, sl=sl, tm=tm, sn=sn: e.scalar_tensor_tensor(
                    sl, sn, 1.0, tm, AO.min, AO.mult),
                  r=[self.mslab], w=[self.mslab])
            dput(lambda e, j=j: e.dma_start(out=self.mod_d[0:4, j * 512:(j + 1) * 512],
                                            in_=self.mslab.ap[0:4, 0:512]),
                 r=[self.mslab], w=[("mod_d", j)], tag="st_mod")
            for u in range(4):
                cn = 4 * j + u
                t(lambda e, u=u: e.transpose(self.pC.ap[:, 0:4],
                                             self.mslab.ap[0:4, u * 128:(u + 1) * 128],
                                             self.identf.ap[0:4, 0:4]),
                  r=[self.mslab, self.identf], w=[self.pC], tag="tr_mod")
                v(lambda e, cn=cn: e.tensor_copy(self.modT.ap[:, cn * 4:(cn + 1) * 4],
                                                 self.pC.ap[:, 0:4]),
                  r=[self.pC], w=[self.modT])
        for b in range(4):
            v(lambda e, b=b: e.tensor_mul(self.gobT.ap[:, b:32:4],
                                          self.modT.ap[:, 64 + b:96:4], self.obT.ap),
              r=[self.modT, self.obT], w=[self.gobT])
        for (src_w, dstT, nr) in ((self.qkv_w, self.wqkvT, 12), (self.o_w, self.owT, 8)):
            for r_ in range(nr):
                wt = self.wtmp[r_ % 2]
                wb = self.wbf[r_ % 2]
                dget(lambda e, src_w=src_w, r_=r_, wt=wt: e.dma_start(
                    out=wt.ap[:, 0:1024],
                    in_=src_w[r_ * 128:(r_ + 1) * 128, :]),
                     r=[wt], w=[wt], tag="ld_qow")
                v(lambda e, wt=wt, wb=wb: e.tensor_copy(wb.ap[:, 0:1024], wt.ap[:, 0:1024]),
                  r=[wt, wb], w=[wb])
                for c in range(8):
                    p = (self.pA, self.pB)[c % 2]
                    t(lambda e, wb=wb, c=c, p=p: e.transpose(
                        p.ap.bitcast(BF16)[:, 0:128], wb.ap[:, c * 128:(c + 1) * 128],
                        self.identb.ap), r=[wb, self.identb], w=[p], tag="tr_w")
                    v(lambda e, dstT=dstT, r_=r_, c=c, p=p: e.tensor_copy(
                        dstT.ap[:, c, r_ * 128:(r_ + 1) * 128], p.ap.bitcast(BF16)[:, 0:128]),
                      r=[p], w=[dstT])
        self._wq_gen = self._wq_generator()
        self._wq_done = False

    def _wq_generator(self):
        v, t, dget, dput = self.v, self.t, self.dget, self.dput
        for (w_in, w_out, wkey, nr, F) in (
                (self.w_gate, self.wgqT_d, "wgq", 22, 1024),
                (self.w_up, self.wuqT_d, "wuq", 22, 1024),
                (self.w_down, self.wdqT_d, "wdq", 8, II)):
            for r_ in range(nr):
                for so in range(0, F, 512):
                    w_ = min(512, F - so)
                    sl = (r_ * F + so) // 512 % 2
                    wt = self.wqt[sl]
                    wb = self.wqb[sl]
                    dget(lambda e, w_in=w_in, r_=r_, so=so, w_=w_, wt=wt: e.dma_start(
                        out=wt.ap[:, 0:w_],
                        in_=w_in[r_ * 128:(r_ + 1) * 128, so:so + w_]),
                         r=[wt], w=[wt], tag="ld_fw")
                    self.emit_quant(Tile(wt.ap[:, 0:w_], wt.keys),
                                    Tile(wb.ap[:, 0:w_], wb.keys), w_)
                    for u in range(w_ // 128):
                        c = (so + u * 128) // 128
                        p = (self.pA, self.pB)[u % 2]
                        sg_ = u % 2
                        t(lambda e, wb=wb, u=u, p=p: e.transpose(
                            p.ap.bitcast(BF16)[:, 1024 + u % 2 * 128:1024 + u % 2 * 128 + 128],
                            wb.ap[:, u * 128:(u + 1) * 128],
                            self.identb.ap), r=[wb, self.identb], w=[p], tag="tr_fw")
                        v(lambda e, sg_=sg_, p=p, u=u: e.tensor_copy(
                            self.stgr.ap[:, sg_, :],
                            p.ap.bitcast(BF16)[:, 1024 + u % 2 * 128:1024 + u % 2 * 128 + 128]),
                          r=[p, ("stg", sg_)], w=[("stg", sg_)])
                        dput(lambda e, w_out=w_out, r_=r_, c=c, sg_=sg_: e.dma_start(
                            out=w_out[c * 128:(c + 1) * 128, r_ * 128:(r_ + 1) * 128],
                            in_=self.stgr.ap[:, sg_, :]),
                             r=[("stg", sg_), (wkey,)], w=[(wkey,)], tag="st_fw")
                yield

    def pump_wq(self, n):
        if self._wq_done:
            return
        for _ in range(n):
            try:
                next(self._wq_gen)
            except StopIteration:
                self._wq_done = True
                return

    def emit_quant_sl(self, xap, xkey, qap, qkey, W):
        self.emit_quant(Tile(xap, [xkey] if not isinstance(xkey, list) else xkey),
                        Tile(qap, [qkey] if not isinstance(qkey, list) else qkey), W)

    def phaseA(self, s):
        v, a, t = self.v, self.a, self.t
        xio = self.xio
        for g_ in range(4):
            if s == 0 and g_:
                self.pump_wq(3)
            for t4 in range(4):
                tc = 4 * g_ + t4
                buf = tc % 2
                xk = ("xio", buf)
                self.dget(lambda e, s=s, tc=tc, buf=buf: e.dma_start(
                    out=xio.ap[:, buf, :],
                    in_=self.x_in[s, tc * 128:(tc + 1) * 128, :]),
                     r=[xk], w=[xk], tag="ldx")
                sc_, bi_ = self.ln_stats(xio.ap[:, buf, :], xk, tag="ln1")
                a(lambda e, t4=t4, buf=buf, sc_=sc_, bi_=bi_: e.activation(
                    self.xnb.ap[:, t4, :], xio.ap[:, buf, :], AF.Identity,
                    bias=bi_, scale=sc_),
                  r=[xk, self.st2], w=[self.xnb], tag="xn")
            for c in range(8):
                p = (self.pA, self.pB)[c % 2]
                for t4 in range(4):
                    t(lambda e, t4=t4, c=c, p=p: e.transpose(
                        p.ap.bitcast(BF16)[:, t4 * 128:(t4 + 1) * 128],
                        self.xnb.ap[:, t4, c * 128:(c + 1) * 128], self.identb.ap),
                      r=[self.xnb, self.identb], w=[p], tag="trA")
                csa = (0 * 8 + c) * 4 + s
                cca = (1 * 8 + c) * 4 + s
                a(lambda e, c=c, g_=g_, p=p, csa=csa, cca=cca: e.activation(
                    self.h1T.ap[:, c, g_ * 512:(g_ + 1) * 512],
                    p.ap.bitcast(BF16)[:, 0:512],
                    AF.Identity, bias=self.modT.ap[:, csa:csa + 1],
                    scale=self.modT.ap[:, cca:cca + 1]),
                  r=[p, self.modT], w=[self.h1T], tag="h1ev")
        for j in range(12):
            for g_ in range(4):
                p = (self.pC, self.pD)[g_ % 2]
                for c in range(8):
                    t(lambda e, j=j, g_=g_, c=c, p=p: e.matmul(
                        p.ap[:, 0:512], self.wqkvT.ap[:, c, j * 128:(j + 1) * 128],
                        self.h1T.ap[:, c, g_ * 512:(g_ + 1) * 512],
                        start=(c == 0), stop=(c == 7)),
                      r=[self.wqkvT, self.h1T], w=[p], tag="mmqkv")
                dst = self.qT if j < 8 else self.kvT
                jj = j if j < 8 else j - 8
                a(lambda e, dst=dst, jj=jj, g_=g_, p=p, j=j: e.activation(
                    dst.ap[:, jj, g_ * 512:(g_ + 1) * 512], p.ap[:, 0:512],
                    AF.Identity, bias=self.qbsT.ap[:, j:j + 1],
                    scale=self.qscT.ap[:, j:j + 1]),
                  r=[p, self.qbsT, self.qscT], w=[dst], tag="qkvev")

    def phaseB(self, s):
        v, t = self.v, self.t
        for g_ in range(4):
            half = g_ % 2
            ch = g_ // 2
            for dsthalf in range(2):
                self.dget(lambda e, g_=g_, half=half, ch=ch, dh=dsthalf: e.dma_start(
                    out=self.ktil.ap[dh * 64:(dh + 1) * 64, g_, :],
                    in_=self.kvT.ap[half * 64:(half + 1) * 64, ch, :]),
                     r=[self.kvT], w=[self.ktil], tag="ktil")
            v(lambda e, g_=g_: e.memset(self.vT.ap[:, g_, 64:1040:65], 1.0),
              w=[self.vT], tag="vones")
            for kc in range(16):
                p = (self.pA, self.pB)[kc % 2]
                t(lambda e, g_=g_, half=half, ch=ch, kc=kc, p=p: e.transpose(
                    p.ap.bitcast(BF16)[:, 0:64],
                    self.kvT.ap[half * 64:(half + 1) * 64, 2 + ch, kc * 128:(kc + 1) * 128],
                    self.identb.ap[half * 64:(half + 1) * 64, half * 64:(half + 1) * 64]),
                  r=[self.kvT, self.identb], w=[p], tag="trv")
                v(lambda e, g_=g_, kc=kc, p=p: e.tensor_copy(
                    self.vT.ap[:, g_, kc * 65:kc * 65 + 64], p.ap.bitcast(BF16)[:, 0:64]),
                  r=[p], w=[self.vT], tag="vcp")
            v(lambda e, g_=g_: e.tensor_mul(self.q2s.ap[0:64, :],
                                            self.ktil.ap[0:64, g_, :],
                                            self.ktil.ap[0:64, g_, :]),
              r=[self.ktil, self.q2s], w=[self.q2s], tag="k2")
            for u in range(4):
                p = (self.pC, self.pD)[u % 2]
                t(lambda e, u=u, p=p: e.matmul(
                    p.ap[0:1, 0:512], self.onesb.ap[0:64, :],
                    self.q2s.ap[0:64, u * 512:(u + 1) * 512], start=True, stop=True),
                  r=[self.onesb, self.q2s], w=[p], tag="knrm")
                v(lambda e, u=u, p=p: e.reduce_max(self.st1.ap[0:1, u:u + 1],
                                                   p.ap[0:1, 0:512], AX.X),
                  r=[p, self.st1], w=[self.st1])
            v(lambda e, g_=g_: e.reduce_max(self.st2.ap[0:1, g_:g_ + 1],
                                            self.st1.ap[0:1, 0:4], AX.X),
              r=[self.st1, ("km2",)], w=[("km2",), self.st2], tag="km2")

    def phaseC(self, s):
        v, a, t = self.v, self.a, self.t
        for hp in range(8):
            if s == 0:
                self.pump_wq(9)
            kv = hp // 2
            v(lambda e, hp=hp: e.tensor_mul(self.q2s.ap, self.qT.ap[:, hp, :],
                                            self.qT.ap[:, hp, :]),
              r=[self.qT, self.q2s], w=[self.q2s], tag="q2")
            for hh in range(2):
                base = 64 * hh
                for u in range(4):
                    p = (self.pC, self.pD)[u % 2]
                    t(lambda e, base=base, u=u, p=p: e.matmul(
                        p.ap[0:1, 0:512], self.onesb.ap[base:base + 64, :],
                        self.q2s.ap[base:base + 64, u * 512:(u + 1) * 512],
                        start=True, stop=True,
                        tile_position=(base, 0)),
                      r=[self.onesb, self.q2s], w=[p], tag="qnrm")
                    v(lambda e, u=u, p=p: e.reduce_max(self.st1.ap[0:1, u:u + 1],
                                                       p.ap[0:1, 0:512], AX.X),
                      r=[p, self.st1], w=[self.st1])
                v(lambda e, hh=hh: e.reduce_max(self.st1.ap[0:1, 4 + hh:5 + hh],
                                                self.st1.ap[0:1, 0:4], AX.X),
                  r=[self.st1], w=[self.st1])
                v(lambda e, hh=hh, kv=kv: e.tensor_mul(
                    self.st1.ap[0:1, 4 + hh:5 + hh], self.st1.ap[0:1, 4 + hh:5 + hh],
                    self.st2.ap[0:1, kv:kv + 1]),
                  r=[self.st1, self.st2], w=[self.st1])
                a(lambda e, hh=hh: e.activation(self.st1.ap[0:1, 6 + hh:7 + hh],
                                                self.st1.ap[0:1, 4 + hh:5 + hh], AF.Sqrt),
                  r=[self.st1], w=[self.st1])
            v(lambda e: e.tensor_scalar(self.st1.ap[0:1, 6:8], self.st1.ap[0:1, 6:8],
                                        -1.0, None, AO.mult),
              r=[self.st1], w=[self.st1], tag="negch")
            self.dget(lambda e, hp=hp: e.dma_start(
                out=self.bounce[2:3, 2 * hp:2 * hp + 2], in_=self.st1.ap[0:1, 6:8]),
                 r=[self.st1], w=[("bnc", 2, hp)], tag="ch_out")
            self.dget(lambda e, hp=hp: e.dma_start(
                out=self.chb.ap[:, 2 * hp:2 * hp + 2],
                in_=self.bounce[2:3, 2 * hp:2 * hp + 2].broadcast_to([128, 2])),
                 r=[("bnc", 2, hp), self.chb], w=[self.chb], tag="ch_in")
            for qg in range(2):
                pas = ((self.paA0, self.paA1), (self.paB0, self.paB1))

                def emit_qk(i, hp=hp, kv=kv, qg=qg):
                    kc, colh = i // 2, i % 2
                    q0 = qg * 1024 + colh * 512
                    t(lambda e, hp=hp, kv=kv, kc=kc, q0=q0, colh=colh: e.matmul(
                        self.pA.ap[:, colh * 512:(colh + 1) * 512],
                        self.ktil.ap[0:64, kv, kc * 128:(kc + 1) * 128],
                        self.qT.ap[0:64, hp, q0:q0 + 512],
                        start=True, stop=True, tile_position=(0, 0)),
                      r=[self.ktil, self.qT], w=[("pA", colh)], tag="qkA")
                    t(lambda e, hp=hp, kv=kv, kc=kc, q0=q0, colh=colh: e.matmul(
                        self.pB.ap[:, colh * 512:(colh + 1) * 512],
                        self.ktil.ap[64:128, kv, kc * 128:(kc + 1) * 128],
                        self.qT.ap[64:128, hp, q0:q0 + 512],
                        start=True, stop=True, tile_position=(64, 0)),
                      r=[self.ktil, self.qT], w=[("pB", colh)], tag="qkB")

                emit_qk(0)
                for i in range(32):
                    kc, colh = i // 2, i % 2
                    bA = pas[0][kc % 2]
                    bB = pas[1][kc % 2]
                    if i + 1 < 32:
                        emit_qk(i + 1)
                    a(lambda e, hp=hp, bA=bA, colh=colh: e.activation(
                        bA.ap[:, colh * 512:(colh + 1) * 512],
                        self.pA.ap[:, colh * 512:(colh + 1) * 512], AF.Exp,
                        bias=self.chb.ap[:, 2 * hp:2 * hp + 1]),
                      r=[("pA", colh), self.chb], w=[(id(bA), colh)] + list(bA.keys),
                      tag="expA")
                    a(lambda e, hp=hp, bB=bB, colh=colh: e.activation(
                        bB.ap[:, colh * 512:(colh + 1) * 512],
                        self.pB.ap[:, colh * 512:(colh + 1) * 512], AF.Exp,
                        bias=self.chb.ap[:, 2 * hp + 1:2 * hp + 2]),
                      r=[("pB", colh), self.chb], w=[(id(bB), colh)] + list(bB.keys),
                      tag="expB")
                    t(lambda e, kv=kv, kc=kc, colh=colh, bA=bA: e.matmul(
                        self.pC.ap[0:65, colh * 512:(colh + 1) * 512],
                        self.vT.ap[:, kv, kc * 65:kc * 65 + 65],
                        bA.ap[:, colh * 512:(colh + 1) * 512],
                        start=(kc == 0), stop=(kc == 15)),
                      r=[self.vT, (id(bA), colh)], w=[("pC", colh)], tag="pvA")
                    t(lambda e, kv=kv, kc=kc, colh=colh, bB=bB: e.matmul(
                        self.pD.ap[0:65, colh * 512:(colh + 1) * 512],
                        self.vT.ap[:, kv, kc * 65:kc * 65 + 65],
                        bB.ap[:, colh * 512:(colh + 1) * 512],
                        start=(kc == 0), stop=(kc == 15)),
                      r=[self.vT, (id(bB), colh)], w=[("pD", colh)], tag="pvB")
                for hh, pv in ((0, self.pC), (1, self.pD)):
                    pvk = [("pC", 0), ("pC", 1)] if hh == 0 else [("pD", 0), ("pD", 1)]
                    dent = (self.h2t, self.sqd)[hh]
                    v(lambda e, pv=pv, dent=dent: e.reciprocal(dent.ap[64:65, :],
                                                               pv.ap[64:65, 0:1024]),
                      r=pvk + [dent], w=[dent], tag="denrec")
                    self.dget(lambda e, hh=hh, dent=dent: e.dma_start(
                        out=self.bounce[hh:hh + 1, :],
                        in_=dent.ap[64:65, :]),
                         r=[dent], w=[("bnc", hh)], tag="den_out")
                    self.dget(lambda e, hh=hh: e.dma_start(
                        out=self.recb.ap[:, hh, :],
                        in_=self.bounce[hh:hh + 1, :].broadcast_to([64, 1024])),
                         r=[("bnc", hh), self.recb], w=[self.recb], tag="den_in")
                    if hh == 0:
                        v(lambda e, hp=hp, qg=qg, pv=pv: e.tensor_tensor(
                            self.attnT.ap[0:64, hp, qg * 1024:(qg + 1) * 1024],
                            pv.ap[0:64, 0:1024], self.recb.ap[:, 0, :], AO.mult),
                          r=pvk + [self.recb], w=[self.attnT], tag="nrmA")
                    else:
                        v(lambda e, qg=qg, pv=pv: e.tensor_tensor(
                            self.ntmp.ap[:, qg, :], pv.ap[0:64, 0:1024],
                            self.recb.ap[:, 1, :], AO.mult),
                          r=pvk + [self.recb, self.ntmp], w=[self.ntmp], tag="nrmB")
                        self.dget(lambda e, hp=hp, qg=qg: e.dma_start(
                            out=self.attnT.ap[64:128, hp, qg * 1024:(qg + 1) * 1024],
                            in_=self.ntmp.ap[:, qg, :]),
                             r=[self.ntmp], w=[self.attnT], tag="nrmBmv")

    def phaseD(self, s):
        v, a, t = self.v, self.a, self.t
        for g_ in range(4):
            if s == 0:
                self.pump_wq(4)
            for dc in range(8):
                p = (self.pA, self.pB)[dc % 2]
                for hp in range(8):
                    t(lambda e, g_=g_, dc=dc, hp=hp, p=p: e.matmul(
                        p.ap[:, 0:512], self.owT.ap[:, hp, dc * 128:(dc + 1) * 128],
                        self.attnT.ap[:, hp, g_ * 512:(g_ + 1) * 512],
                        start=(hp == 0), stop=(hp == 7)),
                      r=[self.owT, self.attnT], w=[p], tag="mmo")
                cga = (2 * 8 + dc) * 4 + s
                a(lambda e, dc=dc, p=p, cga=cga, s=s: e.activation(
                    self.gd2.ap[:, 0, dc, :], p.ap[:, 0:512], AF.Identity,
                    bias=self.gobT.ap[:, dc * 4 + s:dc * 4 + s + 1],
                    scale=self.modT.ap[:, cga:cga + 1]),
                  r=[p, self.gobT, self.modT], w=[self.gd2], tag="gdev")
            for t4 in range(4):
                tc = g_ * 4 + t4
                buf = t4 % 2
                xk = ("xio", buf)
                self.dget(lambda e, s=s, tc=tc, buf=buf: e.dma_start(
                    out=self.xio.ap[:, buf, :],
                    in_=self.x_in[s, tc * 128:(tc + 1) * 128, :]),
                     r=[xk], w=[xk], tag="ldx2")
                p2 = (self.pC, self.pD)[t4 % 2]
                for dc in range(8):
                    t(lambda e, dc=dc, t4=t4, p2=p2: e.transpose(
                        p2.ap.bitcast(BF16)[:, dc * 128:(dc + 1) * 128],
                        self.gd2.ap[:, 0, dc, t4 * 128:(t4 + 1) * 128], self.identb.ap),
                      r=[self.gd2, self.identb], w=[p2], tag="trD")
                v(lambda e, buf=buf, p2=p2: e.tensor_tensor(
                    self.xio.ap[:, buf, :], self.xio.ap[:, buf, :],
                    p2.ap.bitcast(BF16)[:, 0:1024], AO.add),
                  r=[xk, p2], w=[xk], tag="resadd")
                self.dput(lambda e, tc=tc, buf=buf: e.dma_start(
                    out=self.xnew_d[tc * 128:(tc + 1) * 128, :],
                    in_=self.xio.ap[:, buf, :]),
                     r=[xk], w=[("xn_d", tc)], tag="stxn")
                if "xnew" in self.dbg and s == 0:
                    self.dput(lambda e, tc=tc, buf=buf: e.dma_start(
                        out=self.xnew_dbg[tc * 128:(tc + 1) * 128, :],
                        in_=self.xio.ap[:, buf, :]),
                         r=[xk], w=[("xnd", tc)], tag="dbgxn")

    def phaseE(self, s):
        v, a, t = self.v, self.a, self.t
        self.dget(lambda e, s=s: e.dma_start(
            out=self.cb1r.ap, in_=self.mod_d[s:s + 1, 4 * D:5 * D].broadcast_to([128, D])),
             r=[("mod_d", 8), ("mod_d", 9), self.cb1r], w=[self.cb1r], tag="cb1r")
        self.dget(lambda e, s=s: e.dma_start(
            out=self.sbr.ap, in_=self.mod_d[s:s + 1, 3 * D:4 * D].broadcast_to([128, D])),
             r=[("mod_d", 6), ("mod_d", 7), self.sbr], w=[self.sbr], tag="sbr")
        for half in range(2):
            for tcl in range(8):
                tc = half * 8 + tcl
                buf = tcl % 2
                xk = ("xio", buf)
                self.dget(lambda e, tc=tc, buf=buf: e.dma_start(
                    out=self.xio.ap[:, buf, :],
                    in_=self.xnew_d[tc * 128:(tc + 1) * 128, :]),
                     r=[xk, ("xn_d", tc)], w=[xk], tag="ldxn")
                sc_, bi_ = self.ln_stats(self.xio.ap[:, buf, :], xk, tag="ln2")
                a(lambda e, buf=buf, sc_=sc_, bi_=bi_: e.activation(
                    self.h2t.ap, self.xio.ap[:, buf, :], AF.Identity,
                    bias=bi_, scale=sc_),
                  r=[xk, self.st2, self.h2t], w=[self.h2t], tag="xn2")
                v(lambda e: e.tensor_mul(self.h2t.ap, self.h2t.ap, self.cb1r.ap),
                  r=[self.h2t, self.cb1r], w=[self.h2t])
                v(lambda e: e.tensor_add(self.h2t.ap, self.h2t.ap, self.sbr.ap),
                  r=[self.h2t, self.sbr], w=[self.h2t])
                for so in (0, 512):
                    self.emit_quant_sl(self.h2t.ap[:, so:so + 512], self.h2t.keys[0],
                                       self.hq2.ap[:, so:so + 512], self.hq2.keys[0], 512)
                for c in range(8):
                    pbk = ("pB", c % 2)
                    t(lambda e, c=c: e.transpose(
                        self.pB.ap.bitcast(BF16)[:, c % 2 * 1024:c % 2 * 1024 + 128],
                        self.hq2.ap[:, c * 128:(c + 1) * 128],
                        self.identb.ap), r=[self.hq2, self.identb], w=[pbk], tag="trh")
                    a(lambda e, c=c, tcl=tcl: e.activation(
                        self.hqT.ap[:, c, tcl * 128:(tcl + 1) * 128],
                        self.pB.ap.bitcast(BF16)[:, c % 2 * 1024:c % 2 * 1024 + 128],
                        AF.Copy),
                      r=[pbk], w=[self.hqT], tag="hqcp")
            chunks = [(0, 512), (512, 512), (1024, 512), (1536, 512), (2048, 512),
                      (2560, 256)]
            for ici, (woff, w_) in enumerate(chunks):
                wgt = self.wgs[ici % 2]
                wut = self.wus[ici % 2]
                self.dget(lambda e, woff=woff, w_=w_, wgt=wgt: e.dma_start(
                    out=wgt.ap[:, :, 0:w_],
                    in_=self.wgqT_d[:, woff:woff + w_].rearrange("(c p) n -> p c n", p=128)),
                     r=[wgt, ("wgq",)], w=[wgt], tag="ldwg")
                self.dget(lambda e, woff=woff, w_=w_, wut=wut: e.dma_start(
                    out=wut.ap[:, :, 0:w_],
                    in_=self.wuqT_d[:, woff:woff + w_].rearrange("(c p) n -> p c n", p=128)),
                     r=[wut, ("wuq",)], w=[wut], tag="ldwu")
                for tcl in range(8):
                    p = (self.pC, self.pD, self.pA)[(ici * 8 + tcl) % 3]
                    pk = [(p.keys[0][0], 0), (p.keys[0][0], 1)]
                    for c in range(8):
                        t(lambda e, c=c, tcl=tcl, wgt=wgt, w_=w_, p=p: e.matmul(
                            p.ap[:, 0:w_], self.hqT.ap[:, c, tcl * 128:(tcl + 1) * 128],
                            wgt.ap[:, c, 0:w_], start=(c == 0), stop=(c == 7)),
                          r=[self.hqT, wgt], w=[pk[0]], tag="mmg")
                    for c in range(8):
                        t(lambda e, c=c, tcl=tcl, wut=wut, w_=w_, p=p: e.matmul(
                            p.ap[:, 512:512 + w_],
                            self.hqT.ap[:, c, tcl * 128:(tcl + 1) * 128],
                            wut.ap[:, c, 0:w_], start=(c == 0), stop=(c == 7)),
                          r=[self.hqT, wut], w=[pk[1]], tag="mmu")
                    a(lambda e, w_=w_, p=p: e.activation(self.sg.ap[:, 0:w_],
                                                         p.ap[:, 0:w_], AF.Silu),
                      r=[pk[0], self.sg], w=[self.sg], tag="silu")
                    v(lambda e, w_=w_, p=p: e.tensor_tensor(
                        self.acts.ap[:, 0:w_], self.sg.ap[:, 0:w_],
                        p.ap[:, 512:512 + w_], AO.mult),
                      r=[self.sg, pk[1], self.acts], w=[self.acts], tag="actm")
                    self.emit_quant(Tile(self.acts.ap[:, 0:w_], self.acts.keys),
                                    Tile(self.qT3.ap[:, 0:w_], self.qT3.keys), w_,
                                    qout=True)
                    for u in range(w_ // 128):
                        c2 = (woff + u * 128) // 128
                        pbk = ("pB", u % 2)
                        t(lambda e, u=u: e.transpose(
                            self.pB.ap.bitcast(BF16)[:, u % 2 * 1024:u % 2 * 1024 + 128],
                            self.qT3.ap[:, u * 128:(u + 1) * 128],
                            self.identb.ap), r=[self.qT3, self.identb], w=[pbk], tag="tra")
                        a(lambda e, c2=c2, tcl=tcl, u=u: e.activation(
                            self.actqT.ap[:, c2, tcl * 128:(tcl + 1) * 128],
                            self.pB.ap.bitcast(BF16)[:, u % 2 * 1024:u % 2 * 1024 + 128],
                            AF.Copy),
                          r=[pbk], w=[self.actqT], tag="aqcp")
            for dc in range(8):
                wdt = self.wds[dc % 2]
                self.dget(lambda e, dc=dc, wdt=wdt: e.dma_start(
                    out=wdt.ap,
                    in_=self.wdqT_d[:, dc * 128:(dc + 1) * 128]
                        .rearrange("(c p) n -> p c n", p=128)),
                     r=[wdt, ("wdq",)], w=[wdt], tag="ldwd")
                for g2 in range(2):
                    p = (self.pC, self.pD)[g2]
                    for c2 in range(22):
                        t(lambda e, c2=c2, g2=g2, wdt=wdt, p=p: e.matmul(
                            p.ap[:, 0:512], wdt.ap[:, c2, :],
                            self.actqT.ap[:, c2, g2 * 512:(g2 + 1) * 512],
                            start=(c2 == 0), stop=(c2 == 21)),
                          r=[wdt, self.actqT], w=[p], tag="mmd")
                    cgb = (5 * 8 + dc) * 4 + s
                    a(lambda e, dc=dc, g2=g2, p=p, cgb=cgb: e.activation(
                        self.gd2.ap[:, g2, dc, :], p.ap[:, 0:512], AF.Identity,
                        scale=self.modT.ap[:, cgb:cgb + 1]),
                      r=[p, self.modT], w=[self.gd2], tag="fdev")
            for g2 in range(2):
                for t4 in range(4):
                    tcl = g2 * 4 + t4
                    tc = half * 8 + tcl
                    buf = t4 % 2
                    xk = ("xio", buf)
                    self.dget(lambda e, tc=tc, buf=buf: e.dma_start(
                        out=self.xio.ap[:, buf, :],
                        in_=self.xnew_d[tc * 128:(tc + 1) * 128, :]),
                         r=[xk, ("xn_d", tc)], w=[xk], tag="ldxn2")
                    p2 = (self.pC, self.pD)[t4 % 2]
                    for dc in range(8):
                        t(lambda e, dc=dc, g2=g2, t4=t4, p2=p2: e.transpose(
                            p2.ap.bitcast(BF16)[:, dc * 128:(dc + 1) * 128],
                            self.gd2.ap[:, g2, dc, t4 * 128:(t4 + 1) * 128],
                            self.identb.ap),
                          r=[self.gd2, self.identb], w=[p2], tag="trE")
                    v(lambda e, buf=buf, p2=p2: e.tensor_tensor(
                        self.xio.ap[:, buf, :], self.xio.ap[:, buf, :],
                        p2.ap.bitcast(BF16)[:, 0:1024],
                        AO.add), r=[xk, p2], w=[xk], tag="fres")
                    self.dput(lambda e, s=s, tc=tc, buf=buf: e.dma_start(
                        out=self.out[s, tc * 128:(tc + 1) * 128, :],
                        in_=self.xio.ap[:, buf, :]),
                         r=[xk], w=[("out", s, tc)], tag="stout")

    def finish(self):
        sc = self.sc

        def final_wait(g):
            last = None
            for slot, n in sc.slot_n.items():
                last = g.wait_ge(sc.qsem[slot], 16 * n)
            return last
        sc.op("gpsimd", final_wait, tag="final_wait")
        sc.finalize(self.es)
        blocks = self.es.enter_context(self.nc.Block())
        for e in ENGS:
            getattr(blocks, e)(lambda eng, e=e: sc.emit(e, eng))
        self.es.close()



def kernel(L0_attn_hidden_states, L0_attn_scaling, L0_attn_qkv_w, L0_attn_qkv_b,
           L0_attn_o_w, L0_attn_o_b, L0_ffn_w_gate, L0_ffn_w_up, L0_ffn_w_down,
           L0_c, L0_wmod):
    from concourse.bass_utils import run_bass_kernel_spmd
    x = np.ascontiguousarray(np.asarray(L0_attn_hidden_states, dtype=np.float32))
    c = np.ascontiguousarray(np.asarray(L0_c, dtype=np.float32))
    common = {
        "scaling": np.ascontiguousarray(np.asarray(L0_attn_scaling, np.float32).reshape(HD, 1)),
        "qkv_w": np.ascontiguousarray(np.asarray(L0_attn_qkv_w, np.float32)),
        "qkv_b": np.ascontiguousarray(np.asarray(L0_attn_qkv_b, np.float32)),
        "o_w": np.ascontiguousarray(np.asarray(L0_attn_o_w, np.float32)),
        "o_b": np.ascontiguousarray(np.asarray(L0_attn_o_b, np.float32)),
        "w_gate": np.ascontiguousarray(np.asarray(L0_ffn_w_gate, np.float32)),
        "w_up": np.ascontiguousarray(np.asarray(L0_ffn_w_up, np.float32)),
        "w_down": np.ascontiguousarray(np.asarray(L0_ffn_w_down, np.float32)),
        "wmod": np.ascontiguousarray(np.asarray(L0_wmod, np.float32)),
    }
    b = BuildK().build()
    in_maps = [dict(common, x=np.ascontiguousarray(x[4 * i:4 * i + 4]),
                    c=np.ascontiguousarray(c[4 * i:4 * i + 4])) for i in range(8)]
    res = run_bass_kernel_spmd(b.nc, in_maps, core_ids=list(range(8)))
    return np.concatenate([np.asarray(r["out"]) for r in res.results], axis=0)


# version 05
# speedup vs baseline: 1.4198x; 1.1907x over previous
import contextlib
import numpy as np
import concourse.bass as bass
import concourse.mybir as mybir
from concourse.alu_op_type import AluOpType as AO

F32 = mybir.dt.float32
BF16 = mybir.dt.bfloat16
AF = mybir.ActivationFunctionType
AX = mybir.AxisListType

ENGS = ("sync", "tensor", "vector", "scalar", "gpsimd")
DMA_QUEUES = {"sync": "qs", "gpsimd": "qg", "scalar": "qa"}


R_DMA_SEMS = 10


class Sched:
    def __init__(self, nc):
        self.nc = nc
        self.ops = []
        self.eng_ops = {e: [] for e in ENGS}
        self.res = {}
        self.qcount = {q: 0 for q in DMA_QUEUES.values()}
        self.slot_n = {}
        self.slot_consumers = {}

    def op(self, eng, fn, reads=(), writes=(), dma=False, tag=""):
        idx = len(self.ops)
        q = DMA_QUEUES[eng] if dma else None
        o = {"i": idx, "eng": eng, "fn": fn, "dma_q": q, "need_inc": dma,
             "waits": set(), "tag": tag}
        deps = set()
        for r in reads:
            e = self.res.get(r)
            if e and e["w"] is not None:
                deps.add(e["w"])
        for w in writes:
            e = self.res.get(w)
            if e:
                if e["w"] is not None:
                    deps.add(e["w"])
                deps.update(e["r"])
        if dma:
            slot = (q, self.qcount[q] % R_DMA_SEMS)
            self.qcount[q] += 1
            n = self.slot_n.get(slot, 0)
            o["slot"], o["sval"] = slot, 16 * (n + 1)
            self.slot_n[slot] = n + 1
            deps.update(self.slot_consumers.get(slot, ()))
            self.slot_consumers[slot] = set()
        rdeps = set()
        for r in reads:
            e = self.res.get(r)
            if e and e["w"] is not None:
                rdeps.add(e["w"])
        for d in deps:
            dop = self.ops[d]
            if dop["dma_q"] is not None:
                self.slot_consumers.setdefault(dop["slot"], set()).add(idx)
            elif dop["eng"] == eng:
                if (eng != "tensor" and d in rdeps
                        and len(self.eng_ops[eng]) - dop["eng_pos"] <= 1):
                    pass
                else:
                    continue
            dop["need_inc"] = True
            o["waits"].add(d)
        o["eng_pos"] = len(self.eng_ops[eng])
        self.ops.append(o)
        self.eng_ops[eng].append(o)
        for r in reads:
            self.res.setdefault(r, {"w": None, "r": set()})["r"].add(idx)
        for w in writes:
            self.res[w] = {"w": idx, "r": set()}
        return idx

    def dma(self, eng, fn, reads=(), writes=(), tag=""):
        return self.op(eng, fn, reads, writes, dma=True, tag=tag)

    PARANOID = None

    def finalize(self, es):
        nc = self.nc
        if self.PARANOID:
            lo, hi = self.PARANOID
            for i in range(max(lo, 1), min(hi, len(self.ops))):
                self.ops[i - 1]["need_inc"] = True
                self.ops[i]["waits"].add(i - 1)
        self.esem = {e: es.enter_context(nc.semaphore("sem_" + e)) for e in ENGS}
        self.qsem = {}
        for q in set(DMA_QUEUES.values()):
            for si in range(R_DMA_SEMS):
                self.qsem[(q, si)] = es.enter_context(
                    nc.semaphore(f"sem_{q}_{si}"))
        for e in ENGS:
            n = 0
            for o in self.eng_ops[e]:
                if o["need_inc"] and o["dma_q"] is None:
                    n += 1
                    o["inc_seq"] = n
        for o in self.ops:
            wl = {}
            for d in o["waits"]:
                dop = self.ops[d]
                if dop["dma_q"] is not None:
                    sem, val = self.qsem[dop["slot"]], dop["sval"]
                else:
                    sem, val = self.esem[dop["eng"]], dop["inc_seq"]
                k = id(sem)
                if k not in wl or wl[k][1] < val:
                    wl[k] = (sem, val)
            o["wlist"] = list(wl.values())

    def emit(self, eng_name, engine):
        seen = {}
        for o in self.eng_ops[eng_name]:
            for sem, val in o["wlist"]:
                k = id(sem)
                if seen.get(k, -1) >= val:
                    continue
                seen[k] = val
                engine.wait_ge(sem, val)
            inst = o["fn"](engine)
            if o["need_inc"]:
                assert inst is not None, f"op {o['i']} {o['tag']} returned no inst"
                if o["dma_q"] is not None:
                    inst.then_inc(self.qsem[o["slot"]], 16)
                else:
                    inst.then_inc(self.esem[o["eng"]], 1)

    def stats(self):
        return {e: len(self.eng_ops[e]) for e in ENGS} | {
            "incs": sum(1 for o in self.ops if o["need_inc"]),
            "total": len(self.ops), "q": dict(self.qcount)}


class Tile:
    def __init__(self, ap, keys):
        self.ap = ap
        self.keys = tuple(keys)

    def __getitem__(self, idx):
        return self.ap[idx]


NSEQ = 4
S = 2048
D = 1024
NH, NKV, HD = 16, 4, 64
QKV = NH * HD + 2 * NKV * HD
II = 2816
R_SOFT = 1.442695041
EPS = 1e-5
BLK = 1024


class Build:
    def __init__(self, dbg=(), upto=99, nseq=NSEQ):
        self.dbg = set(dbg)
        self.upto = upto
        self.nseq = nseq
        self.nc = bass.Bass("TRN2", target_bir_lowering=False)
        self.es = contextlib.ExitStack()
        self.sc = Sched(self.nc)

    def dram(self, name, shape, dtype=F32, kind="ExternalInput"):
        return self.nc.dram_tensor(name, list(shape), dtype, kind=kind).ap()

    def sb(self, name, shape, dtype=F32):
        ap = self.es.enter_context(self.nc.sbuf_tensor(name, list(shape), dtype))
        return Tile(ap[...], [("t", name)])

    def carve(self, off_elems, shape, dtype=F32):
        n = int(np.prod(shape[1:]))
        width = n * (2 if dtype == F32 else 1)
        ap = self.pool[:, off_elems:off_elems + width]
        if dtype != BF16:
            ap = ap.bitcast(dtype)
        if len(shape) > 2:
            kw = {}
            names = []
            for i, s_ in enumerate(shape[1:]):
                names.append(chr(ord('a') + i))
                kw[names[i]] = s_
            ap = ap.rearrange(f"p ({' '.join(names)}) -> p {' '.join(names)}", **kw)
        b0, b1 = off_elems // BLK, (off_elems + width - 1) // BLK
        return Tile(ap, [("P", b) for b in range(b0, b1 + 1)])

    def ps(self, name, shape):
        ap = self.es.enter_context(self.nc.psum_tensor(name, list(shape), F32))
        return Tile(ap[...], [(name, 0), (name, 1)])

    def _keys(self, tiles):
        ks = []
        for t in tiles:
            if isinstance(t, Tile):
                ks.extend(t.keys)
            else:
                ks.append(t)
        return ks

    def v(self, fn, r=(), w=(), tag=""):
        return self.sc.op("vector", fn, self._keys(r), self._keys(w), tag=tag)

    def a(self, fn, r=(), w=(), tag=""):
        return self.sc.op("scalar", fn, self._keys(r), self._keys(w), tag=tag)

    def t(self, fn, r=(), w=(), tag=""):
        return self.sc.op("tensor", fn, self._keys(r), self._keys(w), tag=tag)

    def g(self, fn, r=(), w=(), tag=""):
        return self.sc.op("gpsimd", fn, self._keys(r), self._keys(w), tag=tag)

    def dget(self, fn, r=(), w=(), tag=""):
        return self.sc.dma("sync", fn, self._keys(r), self._keys(w), tag=tag)

    def dput(self, fn, r=(), w=(), tag=""):
        return self.sc.dma("gpsimd", fn, self._keys(r), self._keys(w), tag=tag)

    def dump(self, tag, ap, shape, dtype=F32, deps=()):
        if tag not in self.dbg:
            return
        t = self.dram("dbg_" + tag, shape, dtype, kind="ExternalOutput")
        self.dput(lambda g, t=t, ap=ap: g.dma_start(out=t[...], in_=ap),
                  r=list(deps), tag="dump_" + tag)


POOL_N = 69632

OFF_qT = 0
OFF_kvT = 16384
OFF_h1T = 24576
OFF_ktil = 24576
OFF_vT = 32768
OFF_paA0 = 37888
OFF_paA1 = 38912
OFF_paB0 = 39936
OFF_paB1 = 40960
OFF_xnb = 40960
OFF_attnT = 45056
OFF_wgs0 = 0
OFF_wgs1 = 4096
OFF_wus0 = 8192
OFF_wus1 = 12288
OFF_wds0 = 16384
OFF_wds1 = 19456
OFF_gd2 = 22528
OFF_hqT = 30720
OFF_cb1r = 38912
OFF_sbr = 40960
OFF_hq2 = 45056
OFF_actqT = 46080
OFF_wtmp0 = 0
OFF_wtmp1 = 5632
OFF_wbf0 = 11264
OFF_wbf1 = 14336
OFF_stg = 17408
OFF_cbuf = 18432
OFF_mslab = 20480
OFF_wmbuf0 = 24576
OFF_wmbuf1 = 32768
OFF_wmbf0 = 40960
OFF_wmbf1 = 45056
OFF_csb = 49152


class BuildK(Build):
    def build(self):
        nc, sc = self.nc, self.sc
        self.x_in = self.dram("x", (NSEQ, S, D))
        self.c_in = self.dram("c", (NSEQ, D))
        self.scaling = self.dram("scaling", (HD, 1))
        self.qkv_w = self.dram("qkv_w", (QKV, D))
        self.qkv_b = self.dram("qkv_b", (QKV,))
        self.o_w = self.dram("o_w", (D, D))
        self.o_b = self.dram("o_b", (D,))
        self.w_gate = self.dram("w_gate", (II, D))
        self.w_up = self.dram("w_up", (II, D))
        self.w_down = self.dram("w_down", (D, II))
        self.wmod = self.dram("wmod", (D, 6 * D))
        self.out = self.dram("out", (NSEQ, S, D), kind="ExternalOutput")
        self.wgqT_d = self.dram("wgqT_d", (D, II), BF16,
                                kind="ExternalOutput" if "wgq_d" in self.dbg else "Internal")
        self.wuqT_d = self.dram("wuqT_d", (D, II), BF16, kind="Internal")
        self.wdqT_d = self.dram("wdqT_d", (II, D), BF16, kind="Internal")
        self.xnew_d = self.dram("xnew_d", (S, D), F32, kind="Internal")
        self.mod_d = self.dram("mod_d", (4, 6 * D), F32, kind="Internal")
        self.bounce = self.dram("bounce", (8, 1024), F32, kind="Internal")
        if "xnew" in self.dbg:
            self.xnew_dbg = self.dram("dbg_xnew", (S, D), F32, kind="ExternalOutput")

        self.pool = self.es.enter_context(
            self.nc.sbuf_tensor("POOL", [128, POOL_N], BF16))[...]
        self.identb = self.sb("identb", (128, 128), BF16)
        self.onesb = self.sb("onesb", (128, 1), BF16)
        self.modT = self.sb("modT", (128, 192))
        self.qscT = self.sb("qscT", (128, 12))
        self.qbsT = self.sb("qbsT", (128, 12))
        self.obT = self.sb("obT", (128, 8))
        self.gobT = self.sb("gobT", (128, 32))
        self.sc64 = self.sb("sc64", (64, 1))
        self.epsT = self.sb("epsT", (128, 1))
        self.stgr = self.sb("stgr", (128, 2, 128), BF16)
        self.sc128 = self.sb("sc128", (128, 1))
        self.wqkvT = self.sb("wqkvT", (128, 8, QKV), BF16)
        self.owT = self.sb("owT", (128, 8, D), BF16)
        self.qU = self.sb("qU", (128, 512))
        self.qAU = self.sb("qAU", (128, 512))
        self.qT1 = self.sb("qT1", (128, 512), BF16)
        self.qT2 = self.sb("qT2", (128, 512), BF16)
        self.qT3 = self.sb("qT3", (128, 512), BF16)
        self.qT4 = self.sb("qT4", (128, 512), BF16)
        self.qBM = self.sb("qBM", (128, 32))
        self.qR = self.sb("qR", (128, 32))
        self.qS2 = self.sb("qS2", (128, 64))
        self._qpend = None
        self._qidx = 0
        self.st1 = self.sb("st_a", (128, 8))
        self.st2 = self.sb("st_b", (128, 8))
        self.chb = self.sb("chb", (128, 16))
        self.recb = self.sb("recb", (64, 2, 1024))
        self.ntmp = self.sb("ntmp", (64, 2, 1024), BF16)
        self.xio = self.sb("xio", (128, 2, D))
        self.pA = self.ps("pA", (128, 1024))
        self.pB = self.ps("pB", (128, 1024))
        self.pC = self.ps("pC", (128, 1024))
        self.pD = self.ps("pD", (128, 1024))

        cv = self.carve
        self.qT = cv(OFF_qT, (128, 8, S), BF16)
        self.kvT = cv(OFF_kvT, (128, 4, S), BF16)
        self.h1T = cv(OFF_h1T, (128, 8, S), BF16)
        self.xnb = cv(OFF_xnb, (128, 4, D), BF16)
        self.attnT = cv(OFF_attnT, (128, 8, S), BF16)
        self.ktil = cv(OFF_ktil, (128, 4, S), BF16)
        self.vT = cv(OFF_vT, (128, 4, 1040), BF16)
        self.paA0 = cv(OFF_paA0, (128, 1024), BF16)
        self.paA1 = cv(OFF_paA1, (128, 1024), BF16)
        self.paB0 = cv(OFF_paB0, (128, 1024), BF16)
        self.paB1 = cv(OFF_paB1, (128, 1024), BF16)
        self.wgs = [cv(OFF_wgs0, (128, 8, 512), BF16), cv(OFF_wgs1, (128, 8, 512), BF16)]
        self.wus = [cv(OFF_wus0, (128, 8, 512), BF16), cv(OFF_wus1, (128, 8, 512), BF16)]
        self.wds = [cv(OFF_wds0, (128, 22, 128), BF16), cv(OFF_wds1, (128, 22, 128), BF16)]
        self.gd2 = cv(OFF_gd2, (128, 2, 8, 512), BF16)
        self.hqT = cv(OFF_hqT, (128, 8, 1024), BF16)
        self.cb1r = cv(OFF_cb1r, (128, D), F32)
        self.sbr = cv(OFF_sbr, (128, D), F32)
        self.hq2 = cv(OFF_hq2, (128, D), BF16)
        self.actqT = cv(OFF_actqT, (128, 22, 1024), BF16)
        self.q2s = cv(16384, (128, 2048), BF16)
        self.h2t = cv(63488, (128, D), F32)
        self.sg = cv(43008, (128, 512), F32)
        self.acts = cv(44032, (128, 512), F32)
        self.sqd = cv(61440, (128, D), F32)
        self.wqt = [cv(65536, (128, 512), F32), cv(66560, (128, 512), F32)]
        self.wqb = [cv(67584, (128, 512), BF16), cv(68096, (128, 512), BF16)]
        self.identf = cv(45056, (128, 128), F32)
        self.iot1 = cv(45312, (128, 128), F32)
        self.iot2 = cv(45568, (128, 128), F32)
        self.wtmp = [cv(OFF_wtmp0, (128, II), F32), cv(OFF_wtmp1, (128, II), F32)]
        self.wbf = [cv(OFF_wbf0, (128, II), BF16), cv(OFF_wbf1, (128, II), BF16)]
        self.stg = cv(OFF_stg, (128, 4, 128), BF16)
        self.cbuf = cv(OFF_cbuf, (128, D), F32)
        self.mslab = cv(OFF_mslab, (128, 2048), F32)
        self.wmbuf = [cv(OFF_wmbuf0, (128, 8, 512), F32), cv(OFF_wmbuf1, (128, 8, 512), F32)]
        self.wmbf = [cv(OFF_wmbf0, (128, 8, 512), BF16), cv(OFF_wmbf1, (128, 8, 512), BF16)]
        self.csf = cv(OFF_csb, (128, 8, 4), F32)

        self.setup()
        for s in range(self.nseq):
            if self.upto >= 1: self.phaseA(s)
            if self.upto >= 2: self.phaseB(s)
            if self.upto >= 3: self.phaseC(s)
            if self.upto >= 4: self.phaseD(s)
            if self.upto >= 5:
                self.pump_wq(200)
                self.phaseE(s)
        for name in self.dbg:
            if not hasattr(self, name) or not isinstance(getattr(self, name), Tile):
                continue
            tile = getattr(self, name)
            ap = tile.ap
            sh = list(ap.shape)
            self.dump(name, ap, sh, ap.dtype, deps=[tile])
        self.finish()
        return self

    def run(self, in_map, n_cores=1, **kw):
        from concourse.bass_utils import run_bass_kernel_spmd
        maps = [in_map] * n_cores if isinstance(in_map, dict) else in_map
        return run_bass_kernel_spmd(self.nc, maps, core_ids=list(range(n_cores)), **kw)

    def emit_quant(self, X, Q, W, xsl=None, qsl=None, rr=(), tag="q", qout=False,
                   after=()):
        v, a = self.v, self.a
        NB = W // 16
        par = self._qidx % 2
        self._qidx += 1
        Xap = xsl(X.ap) if xsl else X.ap
        Qap = qsl(Q.ap) if qsl else Q.ap
        qU, qAU, qT1, qT2, qT3, qT4 = self.qU, self.qAU, self.qT1, self.qT2, self.qT3, self.qT4
        qBM, qR, qS2 = self.qBM, self.qR, self.qS2
        sc0 = par * 32
        x3 = Xap.rearrange("p (g k) -> p g k", k=16)
        S1 = [
            lambda: v(lambda e: e.tensor_reduce(qBM.ap[:, :NB], x3, AX.X, AO.max,
                                                apply_absolute_value=True),
                      r=[X] + list(rr), w=[qBM], tag=tag + "bm"),
            lambda: v(lambda e: e.tensor_scalar(qBM.ap[:, :NB], qBM.ap[:, :NB],
                                                1.0 / 6.0, 1e-12, AO.mult, AO.max),
                      r=[qBM], w=[qBM], tag=tag),
            lambda: v(lambda e: e.reciprocal(qR.ap[:, :NB], qBM.ap[:, :NB]),
                      r=[qBM], w=[qR]),
            lambda: v(lambda e: e.tensor_scalar(qS2.ap[:, sc0:sc0 + NB], qBM.ap[:, :NB],
                                                0.5, None, AO.mult),
                      r=[qBM], w=[("qS2", par)]),
            lambda: v(lambda e: e.tensor_tensor(
                qU.ap[:, :W].rearrange("p (g k) -> p g k", k=16), x3,
                qR.ap[:, :NB].broadcast_to([128, NB, 16]), AO.mult),
                      r=[X, qR], w=[qU], tag=tag + "u"),
        ]
        actAU = lambda: a(lambda e: e.activation(qAU.ap[:, :W], qU.ap[:, :W], AF.Abs),
                          r=[qU], w=[qAU])
        actSG = lambda: a(lambda e: e.activation(qT4.ap[:, :W], qU.ap[:, :W], AF.Sign),
                          r=[qU], w=[qT4])
        S2 = [
            lambda: v(lambda e: e.tensor_scalar(qT1.ap[:, :W], qAU.ap[:, :W], 0.25,
                                                None, AO.is_gt), r=[qAU], w=[qT1]),
            lambda: v(lambda e: e.tensor_scalar(qT2.ap[:, :W], qAU.ap[:, :W], 0.75,
                                                None, AO.is_gt), r=[qAU], w=[qT2]),
            lambda: v(lambda e: e.tensor_scalar(qT3.ap[:, :W], qAU.ap[:, :W], 1.25,
                                                None, AO.is_gt), r=[qAU], w=[qT3]),
            lambda: v(lambda e: e.tensor_add(qT1.ap[:, :W], qT1.ap[:, :W],
                                             qT2.ap[:, :W]), r=[qT1, qT2], w=[qT1]),
            lambda: v(lambda e: e.tensor_scalar(qT2.ap[:, :W], qAU.ap[:, :W], 1.75,
                                                None, AO.is_gt), r=[qAU], w=[qT2]),
            lambda: v(lambda e: e.tensor_add(qT1.ap[:, :W], qT1.ap[:, :W],
                                             qT3.ap[:, :W]), r=[qT1, qT3], w=[qT1]),
            lambda: v(lambda e: e.tensor_scalar(qT3.ap[:, :W], qAU.ap[:, :W], 2.5, 2.0,
                                                AO.is_gt, AO.mult), r=[qAU], w=[qT3]),
            lambda: v(lambda e: e.tensor_add(qT1.ap[:, :W], qT1.ap[:, :W],
                                             qT2.ap[:, :W]), r=[qT1, qT2], w=[qT1]),
            lambda: v(lambda e: e.tensor_scalar(qT2.ap[:, :W], qAU.ap[:, :W], 3.5, 2.0,
                                                AO.is_gt, AO.mult), r=[qAU], w=[qT2]),
            lambda: v(lambda e: e.tensor_add(qT1.ap[:, :W], qT1.ap[:, :W],
                                             qT3.ap[:, :W]), r=[qT1, qT3], w=[qT1]),
            lambda: v(lambda e: e.tensor_scalar(qT3.ap[:, :W], qAU.ap[:, :W], 5.0, 4.0,
                                                AO.is_gt, AO.mult), r=[qAU], w=[qT3]),
            lambda: v(lambda e: e.tensor_add(qT1.ap[:, :W], qT1.ap[:, :W],
                                             qT2.ap[:, :W]), r=[qT1, qT2], w=[qT1]),
            lambda: v(lambda e: e.tensor_tensor(
                qT4.ap[:, :W].rearrange("p (g k) -> p g k", k=16),
                qT4.ap[:, :W].rearrange("p (g k) -> p g k", k=16),
                qS2.ap[:, sc0:sc0 + NB].broadcast_to([128, NB, 16]), AO.mult),
                      r=[qT4, ("qS2", par)], w=[qT4]),
            lambda: v(lambda e: e.tensor_add(qT1.ap[:, :W], qT1.ap[:, :W],
                                             qT3.ap[:, :W]), r=[qT1, qT3], w=[qT1]),
            lambda: v(lambda e: e.tensor_mul(Qap, qT1.ap[:, :W], qT4.ap[:, :W]),
                      r=[qT1, qT4], w=[Q], tag=tag + "fin"),
        ]
        P, afterP = self._qpend if self._qpend is not None else ([], [])
        pad = [None] * max(0, 15 - len(P))
        P = P + pad
        for i in range(5):
            if P[i]: P[i]()
            S1[i]()
        for i in range(5, 11):
            if P[i]: P[i]()
        actAU()
        for i in range(11, 15):
            if P[i]: P[i]()
        actSG()
        for f in afterP:
            f()
        self._qpend = (S2, list(after))

    def quant_flush(self):
        if self._qpend is None:
            return
        P, afterP = self._qpend
        self._qpend = None
        for f in P:
            f()
        for f in afterP:
            f()

    def ln_stats(self, xap, xtile, tag=""):
        v, a = self.v, self.a
        st1, st2, sqd = self.st1, self.st2, self.sqd
        v(lambda e: e.reduce_sum(st1.ap[:, 0:1], xap, AX.X), r=[xtile], w=[st1], tag=tag + "sum")
        a(lambda e: e.activation(sqd.ap[...], xap, AF.Square,
                                 accum_out=st1.ap[:, 1:2]), r=[xtile, st1], w=[sqd, st1])
        a(lambda e: e.activation(st1.ap[:, 3:4], st1.ap[:, 0:1], AF.Square,
                                 scale=1.0 / D), r=[st1], w=[st1], tag=tag + "m2")
        v(lambda e: e.tensor_scalar(st1.ap[:, 2:3], st1.ap[:, 0:1], 1.0 / D, None,
                                    AO.mult), r=[st1], w=[st1])
        v(lambda e: e.scalar_tensor_tensor(st1.ap[:, 5:6], st1.ap[:, 1:2], 1.0 / D,
                                           st1.ap[:, 3:4], AO.mult, AO.subtract),
          r=[st1], w=[st1], tag=tag + "var")
        a(lambda e: e.activation(st2.ap[:, 0:1], st1.ap[:, 5:6], AF.Ln,
                                 bias=self.epsT.ap),
          r=[st1, st2, self.epsT], w=[st2])
        a(lambda e: e.activation(st2.ap[:, 4:5], st2.ap[:, 0:1], AF.Exp, scale=-0.5),
          r=[st2], w=[st2], tag=tag + "rs")
        v(lambda e: e.scalar_tensor_tensor(st2.ap[:, 5:6], st1.ap[:, 2:3], -1.0,
                                           st2.ap[:, 4:5], AO.mult, AO.mult),
          r=[st1, st2], w=[st2])
        return st2.ap[:, 4:5], st2.ap[:, 5:6]

    def setup(self):
        v, a, t, g = self.v, self.a, self.t, self.g
        dget, dput = self.dget, self.dput
        nc = self.nc
        g(lambda e: e.iota(self.iot1.ap, [[1, 128]], channel_multiplier=0,
                           allow_small_or_imprecise_dtypes=True), w=[self.iot1])
        g(lambda e: e.iota(self.iot2.ap, [[0, 128]], channel_multiplier=1,
                           allow_small_or_imprecise_dtypes=True), w=[self.iot2])
        v(lambda e: e.tensor_tensor(self.identf.ap, self.iot1.ap, self.iot2.ap,
                                    AO.is_equal), r=[self.iot1, self.iot2], w=[self.identf])
        v(lambda e: e.tensor_copy(self.identb.ap, self.identf.ap),
          r=[self.identf], w=[self.identb])
        v(lambda e: e.memset(self.onesb.ap, 1.0), w=[self.onesb])
        v(lambda e: e.memset(self.epsT.ap, EPS), w=[self.epsT])
        dget(lambda e: e.dma_start(out=self.sc64.ap, in_=self.scaling[...]),
             w=[self.sc64], tag="ld_scaling")
        a(lambda e: e.activation(self.sc64.ap, self.sc64.ap, AF.Exp),
          r=[self.sc64], w=[self.sc64])
        v(lambda e: e.tensor_scalar(self.sc64.ap, self.sc64.ap, 1.0, None, AO.add),
          r=[self.sc64], w=[self.sc64])
        a(lambda e: e.activation(self.sc64.ap, self.sc64.ap, AF.Ln),
          r=[self.sc64], w=[self.sc64])
        v(lambda e: e.tensor_scalar(self.sc64.ap, self.sc64.ap, R_SOFT / 8.0, None,
                                    AO.mult), r=[self.sc64], w=[self.sc64])
        dget(lambda e: e.dma_start(out=self.bounce[3:4, 0:64], in_=self.sc64.ap),
             r=[self.sc64], w=[("bnc", 3)], tag="sp_out")
        dget(lambda e: e.dma_start(out=self.sc128.ap[0:64, :], in_=self.bounce[3:4, 0:64]),
             r=[("bnc", 3)], w=[self.sc128])
        dget(lambda e: e.dma_start(out=self.sc128.ap[64:128, :], in_=self.bounce[3:4, 0:64]),
             r=[("bnc", 3)], w=[self.sc128])
        v(lambda e: e.tensor_copy(self.qscT.ap[:, 0:8].rearrange("p (a b) -> p a b", a=1),
                                  self.sc128.ap.broadcast_to([128, 1, 8])),
          r=[self.sc128], w=[self.qscT])
        v(lambda e: e.memset(self.qscT.ap[:, 8:12], 1.0), r=[self.qscT], w=[self.qscT])
        dget(lambda e: e.dma_start(out=self.qbsT.ap,
                                   in_=self.qkv_b.rearrange("(c p) -> p c", p=128),
                                   allow_slow_non_contiguous=True),
             w=[self.qbsT], tag="ld_qkvb")
        v(lambda e: e.tensor_mul(self.qbsT.ap, self.qbsT.ap, self.qscT.ap),
          r=[self.qbsT, self.qscT], w=[self.qbsT])
        dget(lambda e: e.dma_start(out=self.obT.ap,
                                   in_=self.o_b.rearrange("(c p) -> p c", p=128),
                                   allow_slow_non_contiguous=True),
             w=[self.obT], tag="ld_ob")
        dget(lambda e: e.dma_start(out=self.cbuf.ap[0:4, :], in_=self.c_in[...]),
             w=[self.cbuf], tag="ld_c")
        for c8 in range(8):
            t(lambda e, c8=c8: e.transpose(self.pC.ap[:, 0:4],
                                           self.cbuf.ap[0:4, c8 * 128:(c8 + 1) * 128],
                                           self.identf.ap[0:4, 0:4]),
              r=[self.cbuf, self.identf], w=[self.pC], tag="trc")
            a(lambda e, c8=c8: e.activation(self.csf.ap[:, c8, :], self.pC.ap[:, 0:4],
                                            AF.Silu), r=[self.pC], w=[self.csf])
        for j in range(12):
            wmb = self.wmbuf[j % 2]
            wmf = self.wmbf[j % 2]
            dget(lambda e, j=j, wmb=wmb: e.dma_start(
                out=wmb.ap,
                in_=self.wmod[:, j * 512:(j + 1) * 512]
                    .rearrange("(c p) n -> p c n", p=128)),
                 r=[wmb], w=[wmb], tag="ld_wmod")
            for c in range(8):
                t(lambda e, c=c, wmb=wmb: e.matmul(
                    self.pD.ap[0:4, 0:512], self.csf.ap[:, c, :],
                    wmb.ap[:, c, :],
                    start=(c == 0), stop=(c == 7)),
                  r=[self.csf, wmb], w=[self.pD], tag="mm_mod")
            a(lambda e: e.activation(self.mslab.ap[0:4, 0:512], self.pD.ap[0:4, 0:512],
                                     AF.Copy), r=[self.pD], w=[self.mslab])
            vec = (j * 512) // 1024
            if vec in (1, 4):
                v(lambda e: e.tensor_scalar(self.mslab.ap[0:4, 0:512],
                                            self.mslab.ap[0:4, 0:512], 1.0, None, AO.add),
                  r=[self.mslab], w=[self.mslab])
            if vec in (2, 5):
                sl = self.mslab.ap[0:4, 0:512]
                tm = self.mslab.ap[0:4, 512:1024]
                sn = self.mslab.ap[0:4, 1024:1536]
                v(lambda e, sl=sl, tm=tm: e.scalar_tensor_tensor(tm, sl, -1.0, sl,
                                                                 AO.mult, AO.max),
                  r=[self.mslab], w=[self.mslab])
                v(lambda e, tm=tm: e.tensor_scalar(tm, tm, 0.25, None, AO.max),
                  r=[self.mslab], w=[self.mslab])
                v(lambda e, sl=sl, sn=sn: e.tensor_scalar(sn, sl, 1e38, -1.0,
                                                          AO.mult, AO.max),
                  r=[self.mslab], w=[self.mslab])
                v(lambda e, sl=sl, tm=tm, sn=sn: e.scalar_tensor_tensor(
                    sl, sn, 1.0, tm, AO.min, AO.mult),
                  r=[self.mslab], w=[self.mslab])
            dput(lambda e, j=j: e.dma_start(out=self.mod_d[0:4, j * 512:(j + 1) * 512],
                                            in_=self.mslab.ap[0:4, 0:512]),
                 r=[self.mslab], w=[("mod_d", j)], tag="st_mod")
            for u in range(4):
                cn = 4 * j + u
                t(lambda e, u=u: e.transpose(self.pC.ap[:, 0:4],
                                             self.mslab.ap[0:4, u * 128:(u + 1) * 128],
                                             self.identf.ap[0:4, 0:4]),
                  r=[self.mslab, self.identf], w=[self.pC], tag="tr_mod")
                v(lambda e, cn=cn: e.tensor_copy(self.modT.ap[:, cn * 4:(cn + 1) * 4],
                                                 self.pC.ap[:, 0:4]),
                  r=[self.pC], w=[self.modT])
        for b in range(4):
            v(lambda e, b=b: e.tensor_mul(self.gobT.ap[:, b:32:4],
                                          self.modT.ap[:, 64 + b:96:4], self.obT.ap),
              r=[self.modT, self.obT], w=[self.gobT])
        for (src_w, dstT, nr) in ((self.qkv_w, self.wqkvT, 12), (self.o_w, self.owT, 8)):
            for r_ in range(nr):
                wt = self.wtmp[r_ % 2]
                wb = self.wbf[r_ % 2]
                dget(lambda e, src_w=src_w, r_=r_, wt=wt: e.dma_start(
                    out=wt.ap[:, 0:1024],
                    in_=src_w[r_ * 128:(r_ + 1) * 128, :]),
                     r=[wt], w=[wt], tag="ld_qow")
                v(lambda e, wt=wt, wb=wb: e.tensor_copy(wb.ap[:, 0:1024], wt.ap[:, 0:1024]),
                  r=[wt, wb], w=[wb])
                for c in range(8):
                    p = (self.pA, self.pB)[c % 2]
                    t(lambda e, wb=wb, c=c, p=p: e.transpose(
                        p.ap.bitcast(BF16)[:, 0:128], wb.ap[:, c * 128:(c + 1) * 128],
                        self.identb.ap), r=[wb, self.identb], w=[p], tag="tr_w")
                    v(lambda e, dstT=dstT, r_=r_, c=c, p=p: e.tensor_copy(
                        dstT.ap[:, c, r_ * 128:(r_ + 1) * 128], p.ap.bitcast(BF16)[:, 0:128]),
                      r=[p], w=[dstT])
        self._wq_gen = self._wq_generator()
        self._wq_done = False

    def _wq_generator(self):
        v, t, dget, dput = self.v, self.t, self.dget, self.dput
        for (w_in, w_out, wkey, nr, F) in (
                (self.w_gate, self.wgqT_d, "wgq", 22, 1024),
                (self.w_up, self.wuqT_d, "wuq", 22, 1024),
                (self.w_down, self.wdqT_d, "wdq", 8, II)):
            for r_ in range(nr):
                for so in range(0, F, 512):
                    w_ = min(512, F - so)
                    sl = (r_ * F + so) // 512 % 2
                    wt = self.wqt[sl]
                    wb = self.wqb[sl]
                    dget(lambda e, w_in=w_in, r_=r_, so=so, w_=w_, wt=wt: e.dma_start(
                        out=wt.ap[:, 0:w_],
                        in_=w_in[r_ * 128:(r_ + 1) * 128, so:so + w_]),
                         r=[wt], w=[wt], tag="ld_fw")
                    aft = []
                    for u in range(w_ // 128):
                        c = (so + u * 128) // 128
                        p = (self.pA, self.pB)[u % 2]
                        sg_ = u % 2
                        def mk(wb=wb, u=u, p=p, sg_=sg_, c=c, w_out=w_out, r_=r_,
                               wkey=wkey):
                            t(lambda e: e.transpose(
                                p.ap.bitcast(BF16)[:, 1024 + u % 2 * 128:1024 + u % 2 * 128 + 128],
                                wb.ap[:, u * 128:(u + 1) * 128],
                                self.identb.ap), r=[wb, self.identb], w=[p], tag="tr_fw")
                            v(lambda e: e.tensor_copy(
                                self.stgr.ap[:, sg_, :],
                                p.ap.bitcast(BF16)[:, 1024 + u % 2 * 128:1024 + u % 2 * 128 + 128]),
                              r=[p, ("stg", sg_)], w=[("stg", sg_)])
                            dput(lambda e: e.dma_start(
                                out=w_out[c * 128:(c + 1) * 128, r_ * 128:(r_ + 1) * 128],
                                in_=self.stgr.ap[:, sg_, :]),
                                 r=[("stg", sg_), (wkey,)], w=[(wkey,)], tag="st_fw")
                        aft.append(mk)
                    self.emit_quant(Tile(wt.ap[:, 0:w_], wt.keys),
                                    Tile(wb.ap[:, 0:w_], wb.keys), w_, after=aft)
                yield

    def pump_wq(self, n):
        if self._wq_done:
            return
        for _ in range(n):
            try:
                next(self._wq_gen)
            except StopIteration:
                self._wq_done = True
                self.quant_flush()
                return

    def emit_quant_sl(self, xap, xkey, qap, qkey, W):
        self.emit_quant(Tile(xap, [xkey] if not isinstance(xkey, list) else xkey),
                        Tile(qap, [qkey] if not isinstance(qkey, list) else qkey), W)

    def phaseA(self, s):
        v, a, t = self.v, self.a, self.t
        xio = self.xio
        for g_ in range(4):
            if s == 0 and g_:
                self.pump_wq(3)
            for t4 in range(4):
                tc = 4 * g_ + t4
                buf = tc % 2
                xk = ("xio", buf)
                self.dget(lambda e, s=s, tc=tc, buf=buf: e.dma_start(
                    out=xio.ap[:, buf, :],
                    in_=self.x_in[s, tc * 128:(tc + 1) * 128, :]),
                     r=[xk], w=[xk], tag="ldx")
                sc_, bi_ = self.ln_stats(xio.ap[:, buf, :], xk, tag="ln1")
                a(lambda e, t4=t4, buf=buf, sc_=sc_, bi_=bi_: e.activation(
                    self.xnb.ap[:, t4, :], xio.ap[:, buf, :], AF.Identity,
                    bias=bi_, scale=sc_),
                  r=[xk, self.st2], w=[self.xnb], tag="xn")
            for c in range(8):
                p = (self.pA, self.pB)[c % 2]
                for t4 in range(4):
                    t(lambda e, t4=t4, c=c, p=p: e.transpose(
                        p.ap.bitcast(BF16)[:, t4 * 128:(t4 + 1) * 128],
                        self.xnb.ap[:, t4, c * 128:(c + 1) * 128], self.identb.ap),
                      r=[self.xnb, self.identb], w=[p], tag="trA")
                csa = (0 * 8 + c) * 4 + s
                cca = (1 * 8 + c) * 4 + s
                a(lambda e, c=c, g_=g_, p=p, csa=csa, cca=cca: e.activation(
                    self.h1T.ap[:, c, g_ * 512:(g_ + 1) * 512],
                    p.ap.bitcast(BF16)[:, 0:512],
                    AF.Identity, bias=self.modT.ap[:, csa:csa + 1],
                    scale=self.modT.ap[:, cca:cca + 1]),
                  r=[p, self.modT], w=[self.h1T], tag="h1ev")
        for j in range(12):
            for g_ in range(4):
                p = (self.pC, self.pD)[g_ % 2]
                for c in range(8):
                    t(lambda e, j=j, g_=g_, c=c, p=p: e.matmul(
                        p.ap[:, 0:512], self.wqkvT.ap[:, c, j * 128:(j + 1) * 128],
                        self.h1T.ap[:, c, g_ * 512:(g_ + 1) * 512],
                        start=(c == 0), stop=(c == 7)),
                      r=[self.wqkvT, self.h1T], w=[p], tag="mmqkv")
                dst = self.qT if j < 8 else self.kvT
                jj = j if j < 8 else j - 8
                a(lambda e, dst=dst, jj=jj, g_=g_, p=p, j=j: e.activation(
                    dst.ap[:, jj, g_ * 512:(g_ + 1) * 512], p.ap[:, 0:512],
                    AF.Identity, bias=self.qbsT.ap[:, j:j + 1],
                    scale=self.qscT.ap[:, j:j + 1]),
                  r=[p, self.qbsT, self.qscT], w=[dst], tag="qkvev")

    def phaseB(self, s):
        v, t, a = self.v, self.t, self.a
        for g_ in range(4):
            half = g_ % 2
            ch = g_ // 2
            for dsthalf in range(2):
                self.dget(lambda e, g_=g_, half=half, ch=ch, dh=dsthalf: e.dma_start(
                    out=self.ktil.ap[dh * 64:(dh + 1) * 64, g_, :],
                    in_=self.kvT.ap[half * 64:(half + 1) * 64, ch, :]),
                     r=[self.kvT], w=[self.ktil], tag="ktil")
            v(lambda e, g_=g_: e.memset(self.vT.ap[:, g_, 64:1040:65], 1.0),
              w=[self.vT], tag="vones")
            for kc in range(16):
                p = (self.pA, self.pB)[kc % 2]
                t(lambda e, g_=g_, half=half, ch=ch, kc=kc, p=p: e.transpose(
                    p.ap.bitcast(BF16)[:, 0:64],
                    self.kvT.ap[half * 64:(half + 1) * 64, 2 + ch, kc * 128:(kc + 1) * 128],
                    self.identb.ap[half * 64:(half + 1) * 64, half * 64:(half + 1) * 64]),
                  r=[self.kvT, self.identb], w=[p], tag="trv")
                v(lambda e, g_=g_, kc=kc, p=p: e.tensor_copy(
                    self.vT.ap[:, g_, kc * 65:kc * 65 + 64], p.ap.bitcast(BF16)[:, 0:64]),
                  r=[p], w=[self.vT], tag="vcp")
            v(lambda e, g_=g_: e.tensor_mul(self.q2s.ap[0:64, :],
                                            self.ktil.ap[0:64, g_, :],
                                            self.ktil.ap[0:64, g_, :]),
              r=[self.ktil, self.q2s], w=[self.q2s], tag="k2")
            for u in range(4):
                p = (self.pC, self.pD)[u % 2]
                t(lambda e, u=u, p=p: e.matmul(
                    p.ap[0:1, 0:512], self.onesb.ap[0:64, :],
                    self.q2s.ap[0:64, u * 512:(u + 1) * 512], start=True, stop=True),
                  r=[self.onesb, self.q2s], w=[p], tag="knrm")
                v(lambda e, u=u, p=p: e.reduce_max(self.st1.ap[0:1, u:u + 1],
                                                   p.ap[0:1, 0:512], AX.X),
                  r=[p, self.st1], w=[self.st1])
            v(lambda e, g_=g_: e.reduce_max(self.st2.ap[0:1, g_:g_ + 1],
                                            self.st1.ap[0:1, 0:4], AX.X),
              r=[self.st1, ("km2",)], w=[("km2",), self.st2], tag="km2")

        for hp in range(8):
            kv = hp // 2
            v(lambda e, hp=hp: e.tensor_mul(self.q2s.ap, self.qT.ap[:, hp, :],
                                            self.qT.ap[:, hp, :]),
              r=[self.qT, self.q2s], w=[self.q2s], tag="q2")
            for hh in range(2):
                base = 64 * hh
                for u in range(4):
                    p = (self.pC, self.pD)[u % 2]
                    t(lambda e, base=base, u=u, p=p: e.matmul(
                        p.ap[0:1, 0:512], self.onesb.ap[base:base + 64, :],
                        self.q2s.ap[base:base + 64, u * 512:(u + 1) * 512],
                        start=True, stop=True,
                        tile_position=(base, 0)),
                      r=[self.onesb, self.q2s], w=[p], tag="qnrm")
                    v(lambda e, u=u, p=p: e.reduce_max(self.st1.ap[0:1, u:u + 1],
                                                       p.ap[0:1, 0:512], AX.X),
                      r=[p, self.st1], w=[self.st1])
                v(lambda e, hh=hh: e.reduce_max(self.st1.ap[0:1, 4 + hh:5 + hh],
                                                self.st1.ap[0:1, 0:4], AX.X),
                  r=[self.st1], w=[self.st1])
                v(lambda e, hh=hh, kv=kv: e.tensor_mul(
                    self.st1.ap[0:1, 4 + hh:5 + hh], self.st1.ap[0:1, 4 + hh:5 + hh],
                    self.st2.ap[0:1, kv:kv + 1]),
                  r=[self.st1, self.st2], w=[self.st1])
                a(lambda e, hh=hh: e.activation(self.st1.ap[0:1, 6 + hh:7 + hh],
                                                self.st1.ap[0:1, 4 + hh:5 + hh], AF.Sqrt),
                  r=[self.st1], w=[self.st1])
            v(lambda e: e.tensor_scalar(self.st1.ap[0:1, 6:8], self.st1.ap[0:1, 6:8],
                                        -1.0, None, AO.mult),
              r=[self.st1], w=[self.st1], tag="negch")
            self.dget(lambda e, hp=hp: e.dma_start(
                out=self.bounce[2:3, 2 * hp:2 * hp + 2], in_=self.st1.ap[0:1, 6:8]),
                 r=[self.st1], w=[("bnc", 2, hp)], tag="ch_out")
            self.dget(lambda e, hp=hp: e.dma_start(
                out=self.chb.ap[:, 2 * hp:2 * hp + 2],
                in_=self.bounce[2:3, 2 * hp:2 * hp + 2].broadcast_to([128, 2])),
                 r=[("bnc", 2, hp), self.chb], w=[self.chb], tag="ch_in")

    def phaseC(self, s):
        v, a, t = self.v, self.a, self.t
        for hp in range(8):
            if s == 0:
                self.pump_wq(9)
            kv = hp // 2
            for qg in range(2):
                pas = ((self.paA0, self.paA1), (self.paB0, self.paB1))

                def emit_qk(i, hp=hp, kv=kv, qg=qg):
                    kc, colh = i // 2, i % 2
                    q0 = qg * 1024 + colh * 512
                    t(lambda e, hp=hp, kv=kv, kc=kc, q0=q0, colh=colh: e.matmul(
                        self.pA.ap[:, colh * 512:(colh + 1) * 512],
                        self.ktil.ap[0:64, kv, kc * 128:(kc + 1) * 128],
                        self.qT.ap[0:64, hp, q0:q0 + 512],
                        start=True, stop=True, tile_position=(0, 0)),
                      r=[self.ktil, self.qT], w=[("pA", colh)], tag="qkA")
                    t(lambda e, hp=hp, kv=kv, kc=kc, q0=q0, colh=colh: e.matmul(
                        self.pB.ap[:, colh * 512:(colh + 1) * 512],
                        self.ktil.ap[64:128, kv, kc * 128:(kc + 1) * 128],
                        self.qT.ap[64:128, hp, q0:q0 + 512],
                        start=True, stop=True, tile_position=(64, 0)),
                      r=[self.ktil, self.qT], w=[("pB", colh)], tag="qkB")

                emit_qk(0)
                for i in range(32):
                    kc, colh = i // 2, i % 2
                    bA = pas[0][kc % 2]
                    bB = pas[1][kc % 2]
                    if i + 1 < 32:
                        emit_qk(i + 1)
                    a(lambda e, hp=hp, bA=bA, colh=colh: e.activation(
                        bA.ap[:, colh * 512:(colh + 1) * 512],
                        self.pA.ap[:, colh * 512:(colh + 1) * 512], AF.Exp,
                        bias=self.chb.ap[:, 2 * hp:2 * hp + 1]),
                      r=[("pA", colh), self.chb], w=[(id(bA), colh)] + list(bA.keys),
                      tag="expA")
                    a(lambda e, hp=hp, bB=bB, colh=colh: e.activation(
                        bB.ap[:, colh * 512:(colh + 1) * 512],
                        self.pB.ap[:, colh * 512:(colh + 1) * 512], AF.Exp,
                        bias=self.chb.ap[:, 2 * hp + 1:2 * hp + 2]),
                      r=[("pB", colh), self.chb], w=[(id(bB), colh)] + list(bB.keys),
                      tag="expB")
                    t(lambda e, kv=kv, kc=kc, colh=colh, bA=bA: e.matmul(
                        self.pC.ap[0:65, colh * 512:(colh + 1) * 512],
                        self.vT.ap[:, kv, kc * 65:kc * 65 + 65],
                        bA.ap[:, colh * 512:(colh + 1) * 512],
                        start=(kc == 0), stop=(kc == 15)),
                      r=[self.vT, (id(bA), colh)], w=[("pC", colh)], tag="pvA")
                    t(lambda e, kv=kv, kc=kc, colh=colh, bB=bB: e.matmul(
                        self.pD.ap[0:65, colh * 512:(colh + 1) * 512],
                        self.vT.ap[:, kv, kc * 65:kc * 65 + 65],
                        bB.ap[:, colh * 512:(colh + 1) * 512],
                        start=(kc == 0), stop=(kc == 15)),
                      r=[self.vT, (id(bB), colh)], w=[("pD", colh)], tag="pvB")
                for hh, pv in ((0, self.pC), (1, self.pD)):
                    pvk = [("pC", 0), ("pC", 1)] if hh == 0 else [("pD", 0), ("pD", 1)]
                    v(lambda e, hh=hh, pv=pv: e.tensor_copy(
                        self.q2s.ap[0:65, hh * 1024:(hh + 1) * 1024],
                        pv.ap[0:65, 0:1024]),
                      r=pvk + [self.q2s], w=[self.q2s], tag="pvev")
                for hh in range(2):
                    dent = (self.h2t, self.sqd)[hh]
                    v(lambda e, hh=hh, dent=dent: e.reciprocal(
                        dent.ap[64:65, :],
                        self.q2s.ap[64:65, hh * 1024:(hh + 1) * 1024]),
                      r=[self.q2s, dent], w=[dent], tag="denrec")
                    self.dget(lambda e, hh=hh, dent=dent: e.dma_start(
                        out=self.bounce[hh:hh + 1, :],
                        in_=dent.ap[64:65, :]),
                         r=[dent], w=[("bnc", hh)], tag="den_out")
                    self.dget(lambda e, hh=hh: e.dma_start(
                        out=self.recb.ap[:, hh, :],
                        in_=self.bounce[hh:hh + 1, :].broadcast_to([64, 1024])),
                         r=[("bnc", hh), self.recb], w=[self.recb], tag="den_in")
                    if hh == 0:
                        v(lambda e, hp=hp, qg=qg: e.tensor_tensor(
                            self.attnT.ap[0:64, hp, qg * 1024:(qg + 1) * 1024],
                            self.q2s.ap[0:64, 0:1024], self.recb.ap[:, 0, :], AO.mult),
                          r=[self.q2s, self.recb], w=[self.attnT], tag="nrmA")
                    else:
                        v(lambda e, qg=qg: e.tensor_tensor(
                            self.ntmp.ap[:, qg, :], self.q2s.ap[0:64, 1024:2048],
                            self.recb.ap[:, 1, :], AO.mult),
                          r=[self.q2s, self.recb, self.ntmp], w=[self.ntmp], tag="nrmB")
                        self.dget(lambda e, hp=hp, qg=qg: e.dma_start(
                            out=self.attnT.ap[64:128, hp, qg * 1024:(qg + 1) * 1024],
                            in_=self.ntmp.ap[:, qg, :]),
                             r=[self.ntmp], w=[self.attnT], tag="nrmBmv")

    def phaseD(self, s):
        v, a, t = self.v, self.a, self.t
        for g_ in range(4):
            if s == 0:
                self.pump_wq(4)
            for dc in range(8):
                p = (self.pA, self.pB)[dc % 2]
                for hp in range(8):
                    t(lambda e, g_=g_, dc=dc, hp=hp, p=p: e.matmul(
                        p.ap[:, 0:512], self.owT.ap[:, hp, dc * 128:(dc + 1) * 128],
                        self.attnT.ap[:, hp, g_ * 512:(g_ + 1) * 512],
                        start=(hp == 0), stop=(hp == 7)),
                      r=[self.owT, self.attnT], w=[p], tag="mmo")
                cga = (2 * 8 + dc) * 4 + s
                a(lambda e, dc=dc, p=p, cga=cga, s=s: e.activation(
                    self.gd2.ap[:, 0, dc, :], p.ap[:, 0:512], AF.Identity,
                    bias=self.gobT.ap[:, dc * 4 + s:dc * 4 + s + 1],
                    scale=self.modT.ap[:, cga:cga + 1]),
                  r=[p, self.gobT, self.modT], w=[self.gd2], tag="gdev")
            for t4 in range(4):
                tc = g_ * 4 + t4
                buf = t4 % 2
                xk = ("xio", buf)
                self.dget(lambda e, s=s, tc=tc, buf=buf: e.dma_start(
                    out=self.xio.ap[:, buf, :],
                    in_=self.x_in[s, tc * 128:(tc + 1) * 128, :]),
                     r=[xk], w=[xk], tag="ldx2")
                p2 = (self.pC, self.pD)[t4 % 2]
                for dc in range(8):
                    t(lambda e, dc=dc, t4=t4, p2=p2: e.transpose(
                        p2.ap.bitcast(BF16)[:, dc * 128:(dc + 1) * 128],
                        self.gd2.ap[:, 0, dc, t4 * 128:(t4 + 1) * 128], self.identb.ap),
                      r=[self.gd2, self.identb], w=[p2], tag="trD")
                v(lambda e, buf=buf, p2=p2: e.tensor_tensor(
                    self.xio.ap[:, buf, :], self.xio.ap[:, buf, :],
                    p2.ap.bitcast(BF16)[:, 0:1024], AO.add),
                  r=[xk, p2], w=[xk], tag="resadd")
                self.dput(lambda e, tc=tc, buf=buf: e.dma_start(
                    out=self.xnew_d[tc * 128:(tc + 1) * 128, :],
                    in_=self.xio.ap[:, buf, :]),
                     r=[xk], w=[("xn_d", tc)], tag="stxn")
                if "xnew" in self.dbg and s == 0:
                    self.dput(lambda e, tc=tc, buf=buf: e.dma_start(
                        out=self.xnew_dbg[tc * 128:(tc + 1) * 128, :],
                        in_=self.xio.ap[:, buf, :]),
                         r=[xk], w=[("xnd", tc)], tag="dbgxn")

    def phaseE(self, s):
        v, a, t = self.v, self.a, self.t
        self.dget(lambda e, s=s: e.dma_start(
            out=self.cb1r.ap, in_=self.mod_d[s:s + 1, 4 * D:5 * D].broadcast_to([128, D])),
             r=[("mod_d", 8), ("mod_d", 9), self.cb1r], w=[self.cb1r], tag="cb1r")
        self.dget(lambda e, s=s: e.dma_start(
            out=self.sbr.ap, in_=self.mod_d[s:s + 1, 3 * D:4 * D].broadcast_to([128, D])),
             r=[("mod_d", 6), ("mod_d", 7), self.sbr], w=[self.sbr], tag="sbr")
        for half in range(2):
            for tcl in range(8):
                tc = half * 8 + tcl
                buf = tcl % 2
                xk = ("xio", buf)
                self.dget(lambda e, tc=tc, buf=buf: e.dma_start(
                    out=self.xio.ap[:, buf, :],
                    in_=self.xnew_d[tc * 128:(tc + 1) * 128, :]),
                     r=[xk, ("xn_d", tc)], w=[xk], tag="ldxn")
                sc_, bi_ = self.ln_stats(self.xio.ap[:, buf, :], xk, tag="ln2")
                a(lambda e, buf=buf, sc_=sc_, bi_=bi_: e.activation(
                    self.h2t.ap, self.xio.ap[:, buf, :], AF.Identity,
                    bias=bi_, scale=sc_),
                  r=[xk, self.st2, self.h2t], w=[self.h2t], tag="xn2")
                v(lambda e: e.tensor_mul(self.h2t.ap, self.h2t.ap, self.cb1r.ap),
                  r=[self.h2t, self.cb1r], w=[self.h2t])
                v(lambda e: e.tensor_add(self.h2t.ap, self.h2t.ap, self.sbr.ap),
                  r=[self.h2t, self.sbr], w=[self.h2t])
                for so in (0, 512):
                    aft = []
                    for c in range(so // 128, so // 128 + 4):
                        def mk(c=c, tcl=tcl):
                            pbk = ("pB", c % 2)
                            t(lambda e: e.transpose(
                                self.pB.ap.bitcast(BF16)[:, c % 2 * 1024:c % 2 * 1024 + 128],
                                self.hq2.ap[:, c * 128:(c + 1) * 128],
                                self.identb.ap), r=[self.hq2, self.identb], w=[pbk],
                              tag="trh")
                            a(lambda e: e.activation(
                                self.hqT.ap[:, c, tcl * 128:(tcl + 1) * 128],
                                self.pB.ap.bitcast(BF16)[:, c % 2 * 1024:c % 2 * 1024 + 128],
                                AF.Copy),
                              r=[pbk], w=[self.hqT], tag="hqcp")
                        aft.append(mk)
                    self.emit_quant(
                        Tile(self.h2t.ap[:, so:so + 512], [self.h2t.keys[0]]),
                        Tile(self.hq2.ap[:, so:so + 512], [self.hq2.keys[0]]),
                        512, after=aft)
            self.quant_flush()
            chunks = [(0, 512), (512, 512), (1024, 512), (1536, 512), (2048, 512),
                      (2560, 256)]
            for ici, (woff, w_) in enumerate(chunks):
                wgt = self.wgs[ici % 2]
                wut = self.wus[ici % 2]
                self.dget(lambda e, woff=woff, w_=w_, wgt=wgt: e.dma_start(
                    out=wgt.ap[:, :, 0:w_],
                    in_=self.wgqT_d[:, woff:woff + w_].rearrange("(c p) n -> p c n", p=128)),
                     r=[wgt, ("wgq",)], w=[wgt], tag="ldwg")
                self.dget(lambda e, woff=woff, w_=w_, wut=wut: e.dma_start(
                    out=wut.ap[:, :, 0:w_],
                    in_=self.wuqT_d[:, woff:woff + w_].rearrange("(c p) n -> p c n", p=128)),
                     r=[wut, ("wuq",)], w=[wut], tag="ldwu")
                for tcl in range(8):
                    p = (self.pC, self.pD, self.pA)[(ici * 8 + tcl) % 3]
                    pk = [(p.keys[0][0], 0), (p.keys[0][0], 1)]
                    for c in range(8):
                        t(lambda e, c=c, tcl=tcl, wgt=wgt, w_=w_, p=p: e.matmul(
                            p.ap[:, 0:w_], self.hqT.ap[:, c, tcl * 128:(tcl + 1) * 128],
                            wgt.ap[:, c, 0:w_], start=(c == 0), stop=(c == 7)),
                          r=[self.hqT, wgt], w=[pk[0]], tag="mmg")
                    for c in range(8):
                        t(lambda e, c=c, tcl=tcl, wut=wut, w_=w_, p=p: e.matmul(
                            p.ap[:, 512:512 + w_],
                            self.hqT.ap[:, c, tcl * 128:(tcl + 1) * 128],
                            wut.ap[:, c, 0:w_], start=(c == 0), stop=(c == 7)),
                          r=[self.hqT, wut], w=[pk[1]], tag="mmu")
                    a(lambda e, w_=w_, p=p: e.activation(self.sg.ap[:, 0:w_],
                                                         p.ap[:, 0:w_], AF.Silu),
                      r=[pk[0], self.sg], w=[self.sg], tag="silu")
                    v(lambda e, w_=w_, p=p: e.tensor_tensor(
                        self.acts.ap[:, 0:w_], self.sg.ap[:, 0:w_],
                        p.ap[:, 512:512 + w_], AO.mult),
                      r=[self.sg, pk[1], self.acts], w=[self.acts], tag="actm")
                    aft = []
                    for u in range(w_ // 128):
                        c2 = (woff + u * 128) // 128
                        def mk(u=u, c2=c2, tcl=tcl):
                            pbk = ("pB", u % 2)
                            t(lambda e: e.transpose(
                                self.pB.ap.bitcast(BF16)[:, u % 2 * 1024:u % 2 * 1024 + 128],
                                self.qT3.ap[:, u * 128:(u + 1) * 128],
                                self.identb.ap), r=[self.qT3, self.identb], w=[pbk],
                              tag="tra")
                            a(lambda e: e.activation(
                                self.actqT.ap[:, c2, tcl * 128:(tcl + 1) * 128],
                                self.pB.ap.bitcast(BF16)[:, u % 2 * 1024:u % 2 * 1024 + 128],
                                AF.Copy),
                              r=[pbk], w=[self.actqT], tag="aqcp")
                        aft.append(mk)
                    self.emit_quant(Tile(self.acts.ap[:, 0:w_], self.acts.keys),
                                    Tile(self.qT3.ap[:, 0:w_], self.qT3.keys), w_,
                                    qout=True, after=aft)
            self.quant_flush()
            for dc in range(8):
                wdt = self.wds[dc % 2]
                self.dget(lambda e, dc=dc, wdt=wdt: e.dma_start(
                    out=wdt.ap,
                    in_=self.wdqT_d[:, dc * 128:(dc + 1) * 128]
                        .rearrange("(c p) n -> p c n", p=128)),
                     r=[wdt, ("wdq",)], w=[wdt], tag="ldwd")
                for g2 in range(2):
                    p = (self.pC, self.pD)[g2]
                    for c2 in range(22):
                        t(lambda e, c2=c2, g2=g2, wdt=wdt, p=p: e.matmul(
                            p.ap[:, 0:512], wdt.ap[:, c2, :],
                            self.actqT.ap[:, c2, g2 * 512:(g2 + 1) * 512],
                            start=(c2 == 0), stop=(c2 == 21)),
                          r=[wdt, self.actqT], w=[p], tag="mmd")
                    cgb = (5 * 8 + dc) * 4 + s
                    a(lambda e, dc=dc, g2=g2, p=p, cgb=cgb: e.activation(
                        self.gd2.ap[:, g2, dc, :], p.ap[:, 0:512], AF.Identity,
                        scale=self.modT.ap[:, cgb:cgb + 1]),
                      r=[p, self.modT], w=[self.gd2], tag="fdev")
            for g2 in range(2):
                for t4 in range(4):
                    tcl = g2 * 4 + t4
                    tc = half * 8 + tcl
                    buf = t4 % 2
                    xk = ("xio", buf)
                    self.dget(lambda e, tc=tc, buf=buf: e.dma_start(
                        out=self.xio.ap[:, buf, :],
                        in_=self.xnew_d[tc * 128:(tc + 1) * 128, :]),
                         r=[xk, ("xn_d", tc)], w=[xk], tag="ldxn2")
                    p2 = (self.pC, self.pD)[t4 % 2]
                    for dc in range(8):
                        t(lambda e, dc=dc, g2=g2, t4=t4, p2=p2: e.transpose(
                            p2.ap.bitcast(BF16)[:, dc * 128:(dc + 1) * 128],
                            self.gd2.ap[:, g2, dc, t4 * 128:(t4 + 1) * 128],
                            self.identb.ap),
                          r=[self.gd2, self.identb], w=[p2], tag="trE")
                    v(lambda e, buf=buf, p2=p2: e.tensor_tensor(
                        self.xio.ap[:, buf, :], self.xio.ap[:, buf, :],
                        p2.ap.bitcast(BF16)[:, 0:1024],
                        AO.add), r=[xk, p2], w=[xk], tag="fres")
                    self.dput(lambda e, s=s, tc=tc, buf=buf: e.dma_start(
                        out=self.out[s, tc * 128:(tc + 1) * 128, :],
                        in_=self.xio.ap[:, buf, :]),
                         r=[xk], w=[("out", s, tc)], tag="stout")

    def finish(self):
        sc = self.sc

        def final_wait(g):
            last = None
            for slot, n in sc.slot_n.items():
                last = g.wait_ge(sc.qsem[slot], 16 * n)
            return last
        sc.op("gpsimd", final_wait, tag="final_wait")
        sc.finalize(self.es)
        blocks = self.es.enter_context(self.nc.Block())
        for e in ENGS:
            getattr(blocks, e)(lambda eng, e=e: sc.emit(e, eng))
        self.es.close()



def kernel(L0_attn_hidden_states, L0_attn_scaling, L0_attn_qkv_w, L0_attn_qkv_b,
           L0_attn_o_w, L0_attn_o_b, L0_ffn_w_gate, L0_ffn_w_up, L0_ffn_w_down,
           L0_c, L0_wmod):
    from concourse.bass_utils import run_bass_kernel_spmd
    x = np.ascontiguousarray(np.asarray(L0_attn_hidden_states, dtype=np.float32))
    c = np.ascontiguousarray(np.asarray(L0_c, dtype=np.float32))
    common = {
        "scaling": np.ascontiguousarray(np.asarray(L0_attn_scaling, np.float32).reshape(HD, 1)),
        "qkv_w": np.ascontiguousarray(np.asarray(L0_attn_qkv_w, np.float32)),
        "qkv_b": np.ascontiguousarray(np.asarray(L0_attn_qkv_b, np.float32)),
        "o_w": np.ascontiguousarray(np.asarray(L0_attn_o_w, np.float32)),
        "o_b": np.ascontiguousarray(np.asarray(L0_attn_o_b, np.float32)),
        "w_gate": np.ascontiguousarray(np.asarray(L0_ffn_w_gate, np.float32)),
        "w_up": np.ascontiguousarray(np.asarray(L0_ffn_w_up, np.float32)),
        "w_down": np.ascontiguousarray(np.asarray(L0_ffn_w_down, np.float32)),
        "wmod": np.ascontiguousarray(np.asarray(L0_wmod, np.float32)),
    }
    b = BuildK().build()
    in_maps = [dict(common, x=np.ascontiguousarray(x[4 * i:4 * i + 4]),
                    c=np.ascontiguousarray(c[4 * i:4 * i + 4])) for i in range(8)]
    res = run_bass_kernel_spmd(b.nc, in_maps, core_ids=list(range(8)))
    return np.concatenate([np.asarray(r["out"]) for r in res.results], axis=0)


# version 07
# speedup vs baseline: 1.5948x; 1.1233x over previous
import contextlib
import numpy as np
import concourse.bass as bass
import concourse.mybir as mybir
from concourse.alu_op_type import AluOpType as AO

F32 = mybir.dt.float32
BF16 = mybir.dt.bfloat16
AF = mybir.ActivationFunctionType
AX = mybir.AxisListType

ENGS = ("sync", "tensor", "vector", "scalar", "gpsimd")
DMA_QUEUES = {"sync": "qs", "gpsimd": "qg", "scalar": "qa"}


R_DMA_SEMS = 10


class Sched:
    def __init__(self, nc):
        self.nc = nc
        self.ops = []
        self.eng_ops = {e: [] for e in ENGS}
        self.res = {}
        self.qcount = {q: 0 for q in DMA_QUEUES.values()}
        self.slot_n = {}
        self.slot_consumers = {}

    def op(self, eng, fn, reads=(), writes=(), dma=False, tag=""):
        idx = len(self.ops)
        q = DMA_QUEUES[eng] if dma else None
        o = {"i": idx, "eng": eng, "fn": fn, "dma_q": q, "need_inc": dma,
             "waits": set(), "tag": tag}
        deps = set()
        for r in reads:
            e = self.res.get(r)
            if e and e["w"] is not None:
                deps.add(e["w"])
        for w in writes:
            e = self.res.get(w)
            if e:
                if e["w"] is not None:
                    deps.add(e["w"])
                deps.update(e["r"])
        if dma:
            slot = (q, self.qcount[q] % R_DMA_SEMS)
            self.qcount[q] += 1
            n = self.slot_n.get(slot, 0)
            o["slot"], o["sval"] = slot, 16 * (n + 1)
            self.slot_n[slot] = n + 1
            deps.update(self.slot_consumers.get(slot, ()))
            self.slot_consumers[slot] = set()
        rdeps = set()
        for r in reads:
            e = self.res.get(r)
            if e and e["w"] is not None:
                rdeps.add(e["w"])
        for d in deps:
            dop = self.ops[d]
            if dop["dma_q"] is not None:
                self.slot_consumers.setdefault(dop["slot"], set()).add(idx)
            elif dop["eng"] == eng:
                if (eng != "tensor" and d in rdeps
                        and len(self.eng_ops[eng]) - dop["eng_pos"] <= 1):
                    pass
                else:
                    continue
            dop["need_inc"] = True
            o["waits"].add(d)
        o["eng_pos"] = len(self.eng_ops[eng])
        self.ops.append(o)
        self.eng_ops[eng].append(o)
        for r in reads:
            self.res.setdefault(r, {"w": None, "r": set()})["r"].add(idx)
        for w in writes:
            self.res[w] = {"w": idx, "r": set()}
        return idx

    def dma(self, eng, fn, reads=(), writes=(), tag=""):
        return self.op(eng, fn, reads, writes, dma=True, tag=tag)

    PARANOID = None

    def finalize(self, es):
        nc = self.nc
        if self.PARANOID:
            lo, hi = self.PARANOID
            for i in range(max(lo, 1), min(hi, len(self.ops))):
                self.ops[i - 1]["need_inc"] = True
                self.ops[i]["waits"].add(i - 1)
        self.esem = {e: es.enter_context(nc.semaphore("sem_" + e)) for e in ENGS}
        self.qsem = {}
        for q in set(DMA_QUEUES.values()):
            for si in range(R_DMA_SEMS):
                self.qsem[(q, si)] = es.enter_context(
                    nc.semaphore(f"sem_{q}_{si}"))
        for e in ENGS:
            n = 0
            for o in self.eng_ops[e]:
                if o["need_inc"] and o["dma_q"] is None:
                    n += 1
                    o["inc_seq"] = n
        for o in self.ops:
            wl = {}
            for d in o["waits"]:
                dop = self.ops[d]
                if dop["dma_q"] is not None:
                    sem, val = self.qsem[dop["slot"]], dop["sval"]
                else:
                    sem, val = self.esem[dop["eng"]], dop["inc_seq"]
                k = id(sem)
                if k not in wl or wl[k][1] < val:
                    wl[k] = (sem, val)
            o["wlist"] = list(wl.values())

    def emit(self, eng_name, engine):
        seen = {}
        for o in self.eng_ops[eng_name]:
            for sem, val in o["wlist"]:
                k = id(sem)
                if seen.get(k, -1) >= val:
                    continue
                seen[k] = val
                engine.wait_ge(sem, val)
            inst = o["fn"](engine)
            if o["need_inc"]:
                assert inst is not None, f"op {o['i']} {o['tag']} returned no inst"
                if o["dma_q"] is not None:
                    inst.then_inc(self.qsem[o["slot"]], 16)
                else:
                    inst.then_inc(self.esem[o["eng"]], 1)

    def stats(self):
        return {e: len(self.eng_ops[e]) for e in ENGS} | {
            "incs": sum(1 for o in self.ops if o["need_inc"]),
            "total": len(self.ops), "q": dict(self.qcount)}


class Tile:
    def __init__(self, ap, keys):
        self.ap = ap
        self.keys = tuple(keys)

    def __getitem__(self, idx):
        return self.ap[idx]


NSEQ = 4
S = 2048
D = 1024
NH, NKV, HD = 16, 4, 64
QKV = NH * HD + 2 * NKV * HD
II = 2816
R_SOFT = 1.442695041
EPS = 1e-5
BLK = 1024


class Build:
    def __init__(self, dbg=(), upto=99, nseq=NSEQ):
        self.dbg = set(dbg)
        self.upto = upto
        self.nseq = nseq
        self.nc = bass.Bass("TRN2", target_bir_lowering=False)
        self.es = contextlib.ExitStack()
        self.sc = Sched(self.nc)

    def dram(self, name, shape, dtype=F32, kind="ExternalInput"):
        return self.nc.dram_tensor(name, list(shape), dtype, kind=kind).ap()

    def sb(self, name, shape, dtype=F32):
        ap = self.es.enter_context(self.nc.sbuf_tensor(name, list(shape), dtype))
        return Tile(ap[...], [("t", name)])

    def carve(self, off_elems, shape, dtype=F32):
        n = int(np.prod(shape[1:]))
        width = n * (2 if dtype == F32 else 1)
        ap = self.pool[:, off_elems:off_elems + width]
        if dtype != BF16:
            ap = ap.bitcast(dtype)
        if len(shape) > 2:
            kw = {}
            names = []
            for i, s_ in enumerate(shape[1:]):
                names.append(chr(ord('a') + i))
                kw[names[i]] = s_
            ap = ap.rearrange(f"p ({' '.join(names)}) -> p {' '.join(names)}", **kw)
        b0, b1 = off_elems // BLK, (off_elems + width - 1) // BLK
        return Tile(ap, [("P", b) for b in range(b0, b1 + 1)])

    def ps(self, name, shape):
        ap = self.es.enter_context(self.nc.psum_tensor(name, list(shape), F32))
        return Tile(ap[...], [(name, 0), (name, 1)])

    def _keys(self, tiles):
        ks = []
        for t in tiles:
            if isinstance(t, Tile):
                ks.extend(t.keys)
            else:
                ks.append(t)
        return ks

    def v(self, fn, r=(), w=(), tag=""):
        return self.sc.op("vector", fn, self._keys(r), self._keys(w), tag=tag)

    def a(self, fn, r=(), w=(), tag=""):
        return self.sc.op("scalar", fn, self._keys(r), self._keys(w), tag=tag)

    def t(self, fn, r=(), w=(), tag=""):
        return self.sc.op("tensor", fn, self._keys(r), self._keys(w), tag=tag)

    def g(self, fn, r=(), w=(), tag=""):
        return self.sc.op("gpsimd", fn, self._keys(r), self._keys(w), tag=tag)

    def dget(self, fn, r=(), w=(), tag=""):
        return self.sc.dma("sync", fn, self._keys(r), self._keys(w), tag=tag)

    def dput(self, fn, r=(), w=(), tag=""):
        return self.sc.dma("gpsimd", fn, self._keys(r), self._keys(w), tag=tag)

    def dump(self, tag, ap, shape, dtype=F32, deps=()):
        if tag not in self.dbg:
            return
        t = self.dram("dbg_" + tag, shape, dtype, kind="ExternalOutput")
        self.dput(lambda g, t=t, ap=ap: g.dma_start(out=t[...], in_=ap),
                  r=list(deps), tag="dump_" + tag)


POOL_N = 69632

OFF_qT = 0
OFF_kvT = 16384
OFF_h1T = 24576
OFF_ktil = 24576
OFF_vT = 32768
OFF_paA0 = 37888
OFF_paA1 = 38912
OFF_paB0 = 39936
OFF_paB1 = 40960
OFF_xnb = 40960
OFF_attnT = 45056
OFF_wgs0 = 0
OFF_wgs1 = 4096
OFF_wus0 = 8192
OFF_wus1 = 12288
OFF_wds0 = 16384
OFF_wds1 = 19456
OFF_gd2 = 22528
OFF_hqT = 30720
OFF_cb1r = 38912
OFF_sbr = 40960
OFF_hq2 = 45056
OFF_actqT = 46080
OFF_wtmp0 = 0
OFF_wtmp1 = 5632
OFF_wbf0 = 11264
OFF_wbf1 = 14336
OFF_stg = 17408
OFF_cbuf = 18432
OFF_mslab = 20480
OFF_wmbuf0 = 24576
OFF_wmbuf1 = 32768
OFF_wmbf0 = 40960
OFF_wmbf1 = 45056
OFF_csb = 49152


class BuildK(Build):
    def build(self):
        nc, sc = self.nc, self.sc
        self.x_in = self.dram("x", (NSEQ, S, D))
        self.c_in = self.dram("c", (NSEQ, D))
        self.scaling = self.dram("scaling", (HD, 1))
        self.qkv_w = self.dram("qkv_w", (QKV, D))
        self.qkv_b = self.dram("qkv_b", (QKV,))
        self.o_w = self.dram("o_w", (D, D))
        self.o_b = self.dram("o_b", (D,))
        self.w_gate = self.dram("w_gate", (II, D))
        self.w_up = self.dram("w_up", (II, D))
        self.w_down = self.dram("w_down", (D, II))
        self.wmod = self.dram("wmod", (D, 6 * D))
        self.out = self.dram("out", (NSEQ, S, D), kind="ExternalOutput")
        self.wgqT_d = self.dram("wgqT_d", (D, II), BF16,
                                kind="ExternalOutput" if "wgq_d" in self.dbg else "Internal")
        self.wuqT_d = self.dram("wuqT_d", (D, II), BF16, kind="Internal")
        self.wdqT_d = self.dram("wdqT_d", (II, D), BF16, kind="Internal")
        self.xnew_d = self.dram("xnew_d", (S, D), F32, kind="Internal")
        self.mod_d = self.dram("mod_d", (4, 6 * D), F32, kind="Internal")
        self.bounce = self.dram("bounce", (8, 1024), F32, kind="Internal")
        if "xnew" in self.dbg:
            self.xnew_dbg = self.dram("dbg_xnew", (S, D), F32, kind="ExternalOutput")

        self.pool = self.es.enter_context(
            self.nc.sbuf_tensor("POOL", [128, POOL_N], BF16))[...]
        self.identb = self.sb("identb", (128, 128), BF16)
        self.onesb = self.sb("onesb", (128, 1), BF16)
        self.modT = self.sb("modT", (128, 192))
        self.qscT = self.sb("qscT", (128, 12))
        self.qbsT = self.sb("qbsT", (128, 12))
        self.obT = self.sb("obT", (128, 8))
        self.gobT = self.sb("gobT", (128, 32))
        self.sc64 = self.sb("sc64", (64, 1))
        self.epsT = self.sb("epsT", (128, 1))
        self.stgr = self.sb("stgr", (128, 2, 128), BF16)
        self.sc128 = self.sb("sc128", (128, 1))
        self.wqkvT = self.sb("wqkvT", (128, 8, QKV), BF16)
        self.owT = self.sb("owT", (128, 8, D), BF16)
        self.qU = self.sb("qU", (128, 512))
        self.qAU = self.sb("qAU", (128, 512))
        self.qT1 = self.sb("qT1", (128, 512), BF16)
        self.qT2 = self.sb("qT2", (128, 512), BF16)
        self.qT3 = self.sb("qT3", (128, 512), BF16)
        self.qT4 = self.sb("qT4", (128, 512), BF16)
        self.qBM = self.sb("qBM", (128, 32))
        self.qR = self.sb("qR", (128, 32))
        self.qS2 = self.sb("qS2", (128, 64))
        self._qpend = None
        self._qidx = 0
        self.st1 = self.sb("st_a", (128, 8))
        self.st2 = self.sb("st_b", (128, 8))
        self.chb = self.sb("chb", (128, 16))
        self.recb = self.sb("recb", (64, 2, 1024))
        self.ntmp = self.sb("ntmp", (64, 2, 1024), BF16)
        self.xio = self.sb("xio", (128, 2, D))
        self.pA = self.ps("pA", (128, 1024))
        self.pB = self.ps("pB", (128, 1024))
        self.pC = self.ps("pC", (128, 1024))
        self.pD = self.ps("pD", (128, 1024))

        cv = self.carve
        self.qT = cv(OFF_qT, (128, 8, S), BF16)
        self.kvT = cv(OFF_kvT, (128, 4, S), BF16)
        self.h1T = cv(OFF_h1T, (128, 8, S), BF16)
        self.xnb = cv(OFF_xnb, (128, 4, D), BF16)
        self.attnT = cv(OFF_attnT, (128, 8, S), BF16)
        self.ktil = cv(OFF_ktil, (128, 4, S), BF16)
        self.vT = cv(OFF_vT, (128, 4, 1040), BF16)
        self.paA0 = cv(OFF_paA0, (128, 1024), BF16)
        self.paA1 = cv(OFF_paA1, (128, 1024), BF16)
        self.paB0 = cv(OFF_paB0, (128, 1024), BF16)
        self.paB1 = cv(OFF_paB1, (128, 1024), BF16)
        self.wgs = [cv(OFF_wgs0, (128, 8, 512), BF16), cv(OFF_wgs1, (128, 8, 512), BF16)]
        self.wus = [cv(OFF_wus0, (128, 8, 512), BF16), cv(OFF_wus1, (128, 8, 512), BF16)]
        self.wds = [cv(OFF_wds0, (128, 22, 128), BF16), cv(OFF_wds1, (128, 22, 128), BF16)]
        self.gd2 = cv(OFF_gd2, (128, 2, 8, 512), BF16)
        self.hqT = cv(OFF_hqT, (128, 8, 1024), BF16)
        self.cb1r = cv(OFF_cb1r, (128, D), F32)
        self.sbr = cv(OFF_sbr, (128, D), F32)
        self.hq2 = cv(OFF_hq2, (128, D), BF16)
        self.actqT = cv(OFF_actqT, (128, 22, 1024), BF16)
        self.q2s = cv(16384, (128, 2048), BF16)
        self.h2t = cv(63488, (128, D), F32)
        self.sg = cv(43008, (128, 512), F32)
        self.acts = cv(44032, (128, 512), F32)
        self.sqd = cv(61440, (128, D), F32)
        self.h2e = cv(43008, (128, D), F32)
        self.wqt = [cv(65536, (128, 512), F32), cv(66560, (128, 512), F32)]
        self.wqb = [cv(67584, (128, 512), BF16), cv(68096, (128, 512), BF16)]
        self.identf = cv(45056, (128, 128), F32)
        self.iot1 = cv(45312, (128, 128), F32)
        self.iot2 = cv(45568, (128, 128), F32)
        self.wtmp = [cv(OFF_wtmp0, (128, II), F32), cv(OFF_wtmp1, (128, II), F32)]
        self.wbf = [cv(OFF_wbf0, (128, II), BF16), cv(OFF_wbf1, (128, II), BF16)]
        self.stg = cv(OFF_stg, (128, 4, 128), BF16)
        self.cbuf = cv(OFF_cbuf, (128, D), F32)
        self.mslab = cv(OFF_mslab, (128, 2048), F32)
        self.wmbuf = [cv(OFF_wmbuf0, (128, 8, 512), F32), cv(OFF_wmbuf1, (128, 8, 512), F32)]
        self.wmbf = [cv(OFF_wmbf0, (128, 8, 512), BF16), cv(OFF_wmbf1, (128, 8, 512), BF16)]
        self.csf = cv(OFF_csb, (128, 8, 4), F32)

        self.setup()
        for s in range(self.nseq):
            if self.upto >= 1: self.phaseA(s)
            if self.upto >= 2: self.phaseB(s)
            if self.upto >= 3: self.phaseC(s)
            if self.upto >= 4: self.phaseD(s)
            if self.upto >= 5:
                self.pump_wq(200)
                self.phaseE(s)
        for name in self.dbg:
            if not hasattr(self, name) or not isinstance(getattr(self, name), Tile):
                continue
            tile = getattr(self, name)
            ap = tile.ap
            sh = list(ap.shape)
            self.dump(name, ap, sh, ap.dtype, deps=[tile])
        self.finish()
        return self

    def run(self, in_map, n_cores=1, **kw):
        from concourse.bass_utils import run_bass_kernel_spmd
        maps = [in_map] * n_cores if isinstance(in_map, dict) else in_map
        return run_bass_kernel_spmd(self.nc, maps, core_ids=list(range(n_cores)), **kw)

    def emit_quant(self, X, Q, W, xsl=None, qsl=None, rr=(), tag="q", qout=False,
                   after=(), pre=()):
        v, a = self.v, self.a
        NB = W // 16
        par = self._qidx % 2
        self._qidx += 1
        Xap = xsl(X.ap) if xsl else X.ap
        Qap = qsl(Q.ap) if qsl else Q.ap
        qU, qAU, qT1, qT2, qT3, qT4 = self.qU, self.qAU, self.qT1, self.qT2, self.qT3, self.qT4
        qBM, qR, qS2 = self.qBM, self.qR, self.qS2
        sc0 = par * 32
        x3 = Xap.rearrange("p (g k) -> p g k", k=16)
        S1 = [
            lambda: v(lambda e: e.tensor_reduce(qBM.ap[:, :NB], x3, AX.X, AO.max,
                                                apply_absolute_value=True),
                      r=[X] + list(rr), w=[qBM], tag=tag + "bm"),
            lambda: v(lambda e: e.tensor_scalar(qBM.ap[:, :NB], qBM.ap[:, :NB],
                                                1.0 / 6.0, 1e-12, AO.mult, AO.max),
                      r=[qBM], w=[qBM], tag=tag),
            lambda: v(lambda e: e.reciprocal(qR.ap[:, :NB], qBM.ap[:, :NB]),
                      r=[qBM], w=[qR]),
            lambda: v(lambda e: e.tensor_scalar(qS2.ap[:, sc0:sc0 + NB], qBM.ap[:, :NB],
                                                0.5, None, AO.mult),
                      r=[qBM], w=[("qS2", par)]),
            lambda: v(lambda e: e.tensor_tensor(
                qU.ap[:, :W].rearrange("p (g k) -> p g k", k=16), x3,
                qR.ap[:, :NB].broadcast_to([128, NB, 16]), AO.mult),
                      r=[X, qR], w=[qU], tag=tag + "u"),
        ]
        actAU = lambda: a(lambda e: e.activation(qAU.ap[:, :W], qU.ap[:, :W], AF.Abs),
                          r=[qU], w=[qAU])
        actSG = lambda: a(lambda e: e.activation(qT4.ap[:, :W], qU.ap[:, :W], AF.Sign),
                          r=[qU], w=[qT4])
        S2 = [
            lambda: v(lambda e: e.tensor_scalar(qT1.ap[:, :W], qAU.ap[:, :W], 0.25,
                                                None, AO.is_gt), r=[qAU], w=[qT1]),
            lambda: v(lambda e: e.tensor_scalar(qT2.ap[:, :W], qAU.ap[:, :W], 0.75,
                                                None, AO.is_gt), r=[qAU], w=[qT2]),
            lambda: v(lambda e: e.tensor_scalar(qT3.ap[:, :W], qAU.ap[:, :W], 1.25,
                                                None, AO.is_gt), r=[qAU], w=[qT3]),
            lambda: v(lambda e: e.tensor_add(qT1.ap[:, :W], qT1.ap[:, :W],
                                             qT2.ap[:, :W]), r=[qT1, qT2], w=[qT1]),
            lambda: v(lambda e: e.tensor_scalar(qT2.ap[:, :W], qAU.ap[:, :W], 1.75,
                                                None, AO.is_gt), r=[qAU], w=[qT2]),
            lambda: v(lambda e: e.tensor_add(qT1.ap[:, :W], qT1.ap[:, :W],
                                             qT3.ap[:, :W]), r=[qT1, qT3], w=[qT1]),
            lambda: v(lambda e: e.tensor_scalar(qT3.ap[:, :W], qAU.ap[:, :W], 2.5, 2.0,
                                                AO.is_gt, AO.mult), r=[qAU], w=[qT3]),
            lambda: v(lambda e: e.tensor_add(qT1.ap[:, :W], qT1.ap[:, :W],
                                             qT2.ap[:, :W]), r=[qT1, qT2], w=[qT1]),
            lambda: v(lambda e: e.tensor_scalar(qT2.ap[:, :W], qAU.ap[:, :W], 3.5, 2.0,
                                                AO.is_gt, AO.mult), r=[qAU], w=[qT2]),
            lambda: v(lambda e: e.tensor_add(qT1.ap[:, :W], qT1.ap[:, :W],
                                             qT3.ap[:, :W]), r=[qT1, qT3], w=[qT1]),
            lambda: v(lambda e: e.tensor_scalar(qT3.ap[:, :W], qAU.ap[:, :W], 5.0, 4.0,
                                                AO.is_gt, AO.mult), r=[qAU], w=[qT3]),
            lambda: v(lambda e: e.tensor_add(qT1.ap[:, :W], qT1.ap[:, :W],
                                             qT2.ap[:, :W]), r=[qT1, qT2], w=[qT1]),
            lambda: v(lambda e: e.tensor_tensor(
                qT4.ap[:, :W].rearrange("p (g k) -> p g k", k=16),
                qT4.ap[:, :W].rearrange("p (g k) -> p g k", k=16),
                qS2.ap[:, sc0:sc0 + NB].broadcast_to([128, NB, 16]), AO.mult),
                      r=[qT4, ("qS2", par)], w=[qT4]),
            lambda: v(lambda e: e.tensor_add(qT1.ap[:, :W], qT1.ap[:, :W],
                                             qT3.ap[:, :W]), r=[qT1, qT3], w=[qT1]),
            lambda: v(lambda e: e.tensor_mul(Qap, qT1.ap[:, :W], qT4.ap[:, :W]),
                      r=[qT1, qT4], w=[Q], tag=tag + "fin"),
        ]
        P, afterP = self._qpend if self._qpend is not None else ([], [])
        for f in P:
            f()
        for f in afterP:
            f()
        for f in pre:
            f()
        for f in S1:
            f()
        actAU()
        actSG()
        self._qpend = (S2, list(after))

    def quant_flush(self):
        if self._qpend is None:
            return
        P, afterP = self._qpend
        self._qpend = None
        for f in P:
            f()
        for f in afterP:
            f()

    def ln_stats(self, xap, xtile, tag="", dump=None):
        v, a = self.v, self.a
        st1, st2 = self.st1, self.st2
        sqd = dump if dump is not None else self.sqd
        v(lambda e: e.reduce_sum(st1.ap[:, 0:1], xap, AX.X), r=[xtile], w=[st1], tag=tag + "sum")
        a(lambda e: e.activation(sqd.ap[...], xap, AF.Square,
                                 accum_out=st1.ap[:, 1:2]), r=[xtile, st1], w=[sqd, st1])
        a(lambda e: e.activation(st1.ap[:, 3:4], st1.ap[:, 0:1], AF.Square,
                                 scale=1.0 / D), r=[st1], w=[st1], tag=tag + "m2")
        v(lambda e: e.tensor_scalar(st1.ap[:, 2:3], st1.ap[:, 0:1], 1.0 / D, None,
                                    AO.mult), r=[st1], w=[st1])
        v(lambda e: e.scalar_tensor_tensor(st1.ap[:, 5:6], st1.ap[:, 1:2], 1.0 / D,
                                           st1.ap[:, 3:4], AO.mult, AO.subtract),
          r=[st1], w=[st1], tag=tag + "var")
        a(lambda e: e.activation(st2.ap[:, 0:1], st1.ap[:, 5:6], AF.Ln,
                                 bias=self.epsT.ap),
          r=[st1, st2, self.epsT], w=[st2])
        a(lambda e: e.activation(st2.ap[:, 4:5], st2.ap[:, 0:1], AF.Exp, scale=-0.5),
          r=[st2], w=[st2], tag=tag + "rs")
        v(lambda e: e.scalar_tensor_tensor(st2.ap[:, 5:6], st1.ap[:, 2:3], -1.0,
                                           st2.ap[:, 4:5], AO.mult, AO.mult),
          r=[st1, st2], w=[st2])
        return st2.ap[:, 4:5], st2.ap[:, 5:6]

    def setup(self):
        v, a, t, g = self.v, self.a, self.t, self.g
        dget, dput = self.dget, self.dput
        nc = self.nc
        g(lambda e: e.iota(self.iot1.ap, [[1, 128]], channel_multiplier=0,
                           allow_small_or_imprecise_dtypes=True), w=[self.iot1])
        g(lambda e: e.iota(self.iot2.ap, [[0, 128]], channel_multiplier=1,
                           allow_small_or_imprecise_dtypes=True), w=[self.iot2])
        v(lambda e: e.tensor_tensor(self.identf.ap, self.iot1.ap, self.iot2.ap,
                                    AO.is_equal), r=[self.iot1, self.iot2], w=[self.identf])
        v(lambda e: e.tensor_copy(self.identb.ap, self.identf.ap),
          r=[self.identf], w=[self.identb])
        v(lambda e: e.memset(self.onesb.ap, 1.0), w=[self.onesb])
        v(lambda e: e.memset(self.epsT.ap, EPS), w=[self.epsT])
        dget(lambda e: e.dma_start(out=self.sc64.ap, in_=self.scaling[...]),
             w=[self.sc64], tag="ld_scaling")
        a(lambda e: e.activation(self.sc64.ap, self.sc64.ap, AF.Exp),
          r=[self.sc64], w=[self.sc64])
        v(lambda e: e.tensor_scalar(self.sc64.ap, self.sc64.ap, 1.0, None, AO.add),
          r=[self.sc64], w=[self.sc64])
        a(lambda e: e.activation(self.sc64.ap, self.sc64.ap, AF.Ln),
          r=[self.sc64], w=[self.sc64])
        v(lambda e: e.tensor_scalar(self.sc64.ap, self.sc64.ap, R_SOFT / 8.0, None,
                                    AO.mult), r=[self.sc64], w=[self.sc64])
        dget(lambda e: e.dma_start(out=self.bounce[3:4, 0:64], in_=self.sc64.ap),
             r=[self.sc64], w=[("bnc", 3)], tag="sp_out")
        dget(lambda e: e.dma_start(out=self.sc128.ap[0:64, :], in_=self.bounce[3:4, 0:64]),
             r=[("bnc", 3)], w=[self.sc128])
        dget(lambda e: e.dma_start(out=self.sc128.ap[64:128, :], in_=self.bounce[3:4, 0:64]),
             r=[("bnc", 3)], w=[self.sc128])
        v(lambda e: e.tensor_copy(self.qscT.ap[:, 0:8].rearrange("p (a b) -> p a b", a=1),
                                  self.sc128.ap.broadcast_to([128, 1, 8])),
          r=[self.sc128], w=[self.qscT])
        v(lambda e: e.memset(self.qscT.ap[:, 8:12], 1.0), r=[self.qscT], w=[self.qscT])
        dget(lambda e: e.dma_start(out=self.qbsT.ap,
                                   in_=self.qkv_b.rearrange("(c p) -> p c", p=128),
                                   allow_slow_non_contiguous=True),
             w=[self.qbsT], tag="ld_qkvb")
        v(lambda e: e.tensor_mul(self.qbsT.ap, self.qbsT.ap, self.qscT.ap),
          r=[self.qbsT, self.qscT], w=[self.qbsT])
        dget(lambda e: e.dma_start(out=self.obT.ap,
                                   in_=self.o_b.rearrange("(c p) -> p c", p=128),
                                   allow_slow_non_contiguous=True),
             w=[self.obT], tag="ld_ob")
        dget(lambda e: e.dma_start(out=self.cbuf.ap[0:4, :], in_=self.c_in[...]),
             w=[self.cbuf], tag="ld_c")
        for c8 in range(8):
            t(lambda e, c8=c8: e.transpose(self.pC.ap[:, 0:4],
                                           self.cbuf.ap[0:4, c8 * 128:(c8 + 1) * 128],
                                           self.identf.ap[0:4, 0:4]),
              r=[self.cbuf, self.identf], w=[self.pC], tag="trc")
            a(lambda e, c8=c8: e.activation(self.csf.ap[:, c8, :], self.pC.ap[:, 0:4],
                                            AF.Silu), r=[self.pC], w=[self.csf])
        for j in range(12):
            wmb = self.wmbuf[j % 2]
            wmf = self.wmbf[j % 2]
            dget(lambda e, j=j, wmb=wmb: e.dma_start(
                out=wmb.ap,
                in_=self.wmod[:, j * 512:(j + 1) * 512]
                    .rearrange("(c p) n -> p c n", p=128)),
                 r=[wmb], w=[wmb], tag="ld_wmod")
            for c in range(8):
                t(lambda e, c=c, wmb=wmb: e.matmul(
                    self.pD.ap[0:4, 0:512], self.csf.ap[:, c, :],
                    wmb.ap[:, c, :],
                    start=(c == 0), stop=(c == 7)),
                  r=[self.csf, wmb], w=[self.pD], tag="mm_mod")
            a(lambda e: e.activation(self.mslab.ap[0:4, 0:512], self.pD.ap[0:4, 0:512],
                                     AF.Copy), r=[self.pD], w=[self.mslab])
            vec = (j * 512) // 1024
            if vec in (1, 4):
                v(lambda e: e.tensor_scalar(self.mslab.ap[0:4, 0:512],
                                            self.mslab.ap[0:4, 0:512], 1.0, None, AO.add),
                  r=[self.mslab], w=[self.mslab])
            if vec in (2, 5):
                sl = self.mslab.ap[0:4, 0:512]
                tm = self.mslab.ap[0:4, 512:1024]
                sn = self.mslab.ap[0:4, 1024:1536]
                v(lambda e, sl=sl, tm=tm: e.scalar_tensor_tensor(tm, sl, -1.0, sl,
                                                                 AO.mult, AO.max),
                  r=[self.mslab], w=[self.mslab])
                v(lambda e, tm=tm: e.tensor_scalar(tm, tm, 0.25, None, AO.max),
                  r=[self.mslab], w=[self.mslab])
                v(lambda e, sl=sl, sn=sn: e.tensor_scalar(sn, sl, 1e38, -1.0,
                                                          AO.mult, AO.max),
                  r=[self.mslab], w=[self.mslab])
                v(lambda e, sl=sl, tm=tm, sn=sn: e.scalar_tensor_tensor(
                    sl, sn, 1.0, tm, AO.min, AO.mult),
                  r=[self.mslab], w=[self.mslab])
            dput(lambda e, j=j: e.dma_start(out=self.mod_d[0:4, j * 512:(j + 1) * 512],
                                            in_=self.mslab.ap[0:4, 0:512]),
                 r=[self.mslab], w=[("mod_d", j)], tag="st_mod")
            for u in range(4):
                cn = 4 * j + u
                t(lambda e, u=u: e.transpose(self.pC.ap[:, 0:4],
                                             self.mslab.ap[0:4, u * 128:(u + 1) * 128],
                                             self.identf.ap[0:4, 0:4]),
                  r=[self.mslab, self.identf], w=[self.pC], tag="tr_mod")
                v(lambda e, cn=cn: e.tensor_copy(self.modT.ap[:, cn * 4:(cn + 1) * 4],
                                                 self.pC.ap[:, 0:4]),
                  r=[self.pC], w=[self.modT])
        for b in range(4):
            v(lambda e, b=b: e.tensor_mul(self.gobT.ap[:, b:32:4],
                                          self.modT.ap[:, 64 + b:96:4], self.obT.ap),
              r=[self.modT, self.obT], w=[self.gobT])
        for (src_w, dstT, nr) in ((self.qkv_w, self.wqkvT, 12), (self.o_w, self.owT, 8)):
            for r_ in range(nr):
                wt = self.wtmp[r_ % 2]
                wb = self.wbf[r_ % 2]
                dget(lambda e, src_w=src_w, r_=r_, wt=wt: e.dma_start(
                    out=wt.ap[:, 0:1024],
                    in_=src_w[r_ * 128:(r_ + 1) * 128, :]),
                     r=[wt], w=[wt], tag="ld_qow")
                v(lambda e, wt=wt, wb=wb: e.tensor_copy(wb.ap[:, 0:1024], wt.ap[:, 0:1024]),
                  r=[wt, wb], w=[wb])
                for c in range(8):
                    p = (self.pA, self.pB)[c % 2]
                    t(lambda e, wb=wb, c=c, p=p: e.transpose(
                        p.ap.bitcast(BF16)[:, 0:128], wb.ap[:, c * 128:(c + 1) * 128],
                        self.identb.ap), r=[wb, self.identb], w=[p], tag="tr_w")
                    v(lambda e, dstT=dstT, r_=r_, c=c, p=p: e.tensor_copy(
                        dstT.ap[:, c, r_ * 128:(r_ + 1) * 128], p.ap.bitcast(BF16)[:, 0:128]),
                      r=[p], w=[dstT])
        self._wq_gen = self._wq_generator()
        self._wq_done = False

    def _wq_generator(self):
        v, t, dget, dput = self.v, self.t, self.dget, self.dput
        for (w_in, w_out, wkey, nr, F) in (
                (self.w_gate, self.wgqT_d, "wgq", 22, 1024),
                (self.w_up, self.wuqT_d, "wuq", 22, 1024),
                (self.w_down, self.wdqT_d, "wdq", 8, II)):
            for r_ in range(nr):
                for so in range(0, F, 512):
                    w_ = min(512, F - so)
                    sl = (r_ * F + so) // 512 % 2
                    wt = self.wqt[sl]
                    wb = self.wqb[sl]
                    dget(lambda e, w_in=w_in, r_=r_, so=so, w_=w_, wt=wt: e.dma_start(
                        out=wt.ap[:, 0:w_],
                        in_=w_in[r_ * 128:(r_ + 1) * 128, so:so + w_]),
                         r=[wt], w=[wt], tag="ld_fw")
                    aft = []
                    for u in range(w_ // 128):
                        c = (so + u * 128) // 128
                        p = (self.pA, self.pB)[u % 2]
                        sg_ = u % 2
                        def mk(wb=wb, u=u, p=p, sg_=sg_, c=c, w_out=w_out, r_=r_,
                               wkey=wkey):
                            t(lambda e: e.transpose(
                                p.ap.bitcast(BF16)[:, 1024 + u % 2 * 128:1024 + u % 2 * 128 + 128],
                                wb.ap[:, u * 128:(u + 1) * 128],
                                self.identb.ap), r=[wb, self.identb], w=[p], tag="tr_fw")
                            v(lambda e: e.tensor_copy(
                                self.stgr.ap[:, sg_, :],
                                p.ap.bitcast(BF16)[:, 1024 + u % 2 * 128:1024 + u % 2 * 128 + 128]),
                              r=[p, ("stg", sg_)], w=[("stg", sg_)])
                            dput(lambda e: e.dma_start(
                                out=w_out[c * 128:(c + 1) * 128, r_ * 128:(r_ + 1) * 128],
                                in_=self.stgr.ap[:, sg_, :]),
                                 r=[("stg", sg_), (wkey,)], w=[(wkey,)], tag="st_fw")
                        aft.append(mk)
                    self.emit_quant(Tile(wt.ap[:, 0:w_], wt.keys),
                                    Tile(wb.ap[:, 0:w_], wb.keys), w_, after=aft)
                yield

    def pump_wq(self, n):
        if self._wq_done:
            return
        for _ in range(n):
            try:
                next(self._wq_gen)
            except StopIteration:
                self._wq_done = True
                self.quant_flush()
                return

    def emit_quant_sl(self, xap, xkey, qap, qkey, W):
        self.emit_quant(Tile(xap, [xkey] if not isinstance(xkey, list) else xkey),
                        Tile(qap, [qkey] if not isinstance(qkey, list) else qkey), W)

    def phaseA(self, s):
        v, a, t = self.v, self.a, self.t
        xio = self.xio
        for g_ in range(4):
            if s == 0 and g_:
                self.pump_wq(3)
            for t4 in range(4):
                tc = 4 * g_ + t4
                buf = tc % 2
                xk = ("xio", buf)
                self.dget(lambda e, s=s, tc=tc, buf=buf: e.dma_start(
                    out=xio.ap[:, buf, :],
                    in_=self.x_in[s, tc * 128:(tc + 1) * 128, :]),
                     r=[xk], w=[xk], tag="ldx")
                sc_, bi_ = self.ln_stats(xio.ap[:, buf, :], xk, tag="ln1")
                a(lambda e, t4=t4, buf=buf, sc_=sc_, bi_=bi_: e.activation(
                    self.xnb.ap[:, t4, :], xio.ap[:, buf, :], AF.Identity,
                    bias=bi_, scale=sc_),
                  r=[xk, self.st2], w=[self.xnb], tag="xn")
            for c in range(8):
                p = (self.pA, self.pB)[c % 2]
                for t4 in range(4):
                    t(lambda e, t4=t4, c=c, p=p: e.transpose(
                        p.ap.bitcast(BF16)[:, t4 * 128:(t4 + 1) * 128],
                        self.xnb.ap[:, t4, c * 128:(c + 1) * 128], self.identb.ap),
                      r=[self.xnb, self.identb], w=[p], tag="trA")
                csa = (0 * 8 + c) * 4 + s
                cca = (1 * 8 + c) * 4 + s
                a(lambda e, c=c, g_=g_, p=p, csa=csa, cca=cca: e.activation(
                    self.h1T.ap[:, c, g_ * 512:(g_ + 1) * 512],
                    p.ap.bitcast(BF16)[:, 0:512],
                    AF.Identity, bias=self.modT.ap[:, csa:csa + 1],
                    scale=self.modT.ap[:, cca:cca + 1]),
                  r=[p, self.modT], w=[self.h1T], tag="h1ev")
        for j in range(12):
            for g_ in range(4):
                p = (self.pC, self.pD)[g_ % 2]
                for c in range(8):
                    t(lambda e, j=j, g_=g_, c=c, p=p: e.matmul(
                        p.ap[:, 0:512], self.wqkvT.ap[:, c, j * 128:(j + 1) * 128],
                        self.h1T.ap[:, c, g_ * 512:(g_ + 1) * 512],
                        start=(c == 0), stop=(c == 7)),
                      r=[self.wqkvT, self.h1T], w=[p], tag="mmqkv")
                dst = self.qT if j < 8 else self.kvT
                jj = j if j < 8 else j - 8
                a(lambda e, dst=dst, jj=jj, g_=g_, p=p, j=j: e.activation(
                    dst.ap[:, jj, g_ * 512:(g_ + 1) * 512], p.ap[:, 0:512],
                    AF.Identity, bias=self.qbsT.ap[:, j:j + 1],
                    scale=self.qscT.ap[:, j:j + 1]),
                  r=[p, self.qbsT, self.qscT], w=[dst], tag="qkvev")

    def phaseB(self, s):
        v, t, a = self.v, self.t, self.a
        for g_ in range(4):
            if s == 0:
                self.pump_wq(10)
            half = g_ % 2
            ch = g_ // 2
            for dsthalf in range(2):
                self.dget(lambda e, g_=g_, half=half, ch=ch, dh=dsthalf: e.dma_start(
                    out=self.ktil.ap[dh * 64:(dh + 1) * 64, g_, :],
                    in_=self.kvT.ap[half * 64:(half + 1) * 64, ch, :]),
                     r=[self.kvT], w=[self.ktil], tag="ktil")
            v(lambda e, g_=g_: e.memset(self.vT.ap[:, g_, 64:1040:65], 1.0),
              w=[self.vT], tag="vones")
            for kc in range(16):
                p = (self.pA, self.pB)[kc % 2]
                t(lambda e, g_=g_, half=half, ch=ch, kc=kc, p=p: e.transpose(
                    p.ap.bitcast(BF16)[:, 0:64],
                    self.kvT.ap[half * 64:(half + 1) * 64, 2 + ch, kc * 128:(kc + 1) * 128],
                    self.identb.ap[half * 64:(half + 1) * 64, half * 64:(half + 1) * 64]),
                  r=[self.kvT, self.identb], w=[p], tag="trv")
                v(lambda e, g_=g_, kc=kc, p=p: e.tensor_copy(
                    self.vT.ap[:, g_, kc * 65:kc * 65 + 64], p.ap.bitcast(BF16)[:, 0:64]),
                  r=[p], w=[self.vT], tag="vcp")
            v(lambda e, g_=g_: e.tensor_mul(self.q2s.ap[0:64, :],
                                            self.ktil.ap[0:64, g_, :],
                                            self.ktil.ap[0:64, g_, :]),
              r=[self.ktil, self.q2s], w=[self.q2s], tag="k2")
            for u in range(4):
                p = (self.pC, self.pD)[u % 2]
                t(lambda e, u=u, p=p: e.matmul(
                    p.ap[0:1, 0:512], self.onesb.ap[0:64, :],
                    self.q2s.ap[0:64, u * 512:(u + 1) * 512], start=True, stop=True),
                  r=[self.onesb, self.q2s], w=[p], tag="knrm")
                v(lambda e, u=u, p=p: e.reduce_max(self.st1.ap[0:1, u:u + 1],
                                                   p.ap[0:1, 0:512], AX.X),
                  r=[p, self.st1], w=[self.st1])
            v(lambda e, g_=g_: e.reduce_max(self.st2.ap[0:1, g_:g_ + 1],
                                            self.st1.ap[0:1, 0:4], AX.X),
              r=[self.st1, ("km2",)], w=[("km2",), self.st2], tag="km2")

        for hp in range(8):
            if s == 0:
                self.pump_wq(5)
            kv = hp // 2
            v(lambda e, hp=hp: e.tensor_mul(self.q2s.ap, self.qT.ap[:, hp, :],
                                            self.qT.ap[:, hp, :]),
              r=[self.qT, self.q2s], w=[self.q2s], tag="q2")
            for hh in range(2):
                base = 64 * hh
                for u in range(4):
                    p = (self.pC, self.pD)[u % 2]
                    t(lambda e, base=base, u=u, p=p: e.matmul(
                        p.ap[0:1, 0:512], self.onesb.ap[base:base + 64, :],
                        self.q2s.ap[base:base + 64, u * 512:(u + 1) * 512],
                        start=True, stop=True,
                        tile_position=(base, 0)),
                      r=[self.onesb, self.q2s], w=[p], tag="qnrm")
                    v(lambda e, u=u, p=p: e.reduce_max(self.st1.ap[0:1, u:u + 1],
                                                       p.ap[0:1, 0:512], AX.X),
                      r=[p, self.st1], w=[self.st1])
                v(lambda e, hh=hh: e.reduce_max(self.st1.ap[0:1, 4 + hh:5 + hh],
                                                self.st1.ap[0:1, 0:4], AX.X),
                  r=[self.st1], w=[self.st1])
                v(lambda e, hh=hh, kv=kv: e.tensor_mul(
                    self.st1.ap[0:1, 4 + hh:5 + hh], self.st1.ap[0:1, 4 + hh:5 + hh],
                    self.st2.ap[0:1, kv:kv + 1]),
                  r=[self.st1, self.st2], w=[self.st1])
                a(lambda e, hh=hh: e.activation(self.st1.ap[0:1, 6 + hh:7 + hh],
                                                self.st1.ap[0:1, 4 + hh:5 + hh], AF.Sqrt),
                  r=[self.st1], w=[self.st1])
            v(lambda e: e.tensor_scalar(self.st1.ap[0:1, 6:8], self.st1.ap[0:1, 6:8],
                                        -1.0, None, AO.mult),
              r=[self.st1], w=[self.st1], tag="negch")
            self.dget(lambda e, hp=hp: e.dma_start(
                out=self.bounce[2:3, 2 * hp:2 * hp + 2], in_=self.st1.ap[0:1, 6:8]),
                 r=[self.st1], w=[("bnc", 2, hp)], tag="ch_out")
            self.dget(lambda e, hp=hp: e.dma_start(
                out=self.chb.ap[:, 2 * hp:2 * hp + 2],
                in_=self.bounce[2:3, 2 * hp:2 * hp + 2].broadcast_to([128, 2])),
                 r=[("bnc", 2, hp), self.chb], w=[self.chb], tag="ch_in")

    def phaseC(self, s):
        v, a, t = self.v, self.a, self.t
        for hp in range(8):
            kv = hp // 2
            for qg in range(2):
                pas = ((self.paA0, self.paA1), (self.paB0, self.paB1))

                def emit_qk(i, hp=hp, kv=kv, qg=qg):
                    kc, colh = i // 2, i % 2
                    q0 = qg * 1024 + colh * 512
                    t(lambda e, hp=hp, kv=kv, kc=kc, q0=q0, colh=colh: e.matmul(
                        self.pA.ap[:, colh * 512:(colh + 1) * 512],
                        self.ktil.ap[0:64, kv, kc * 128:(kc + 1) * 128],
                        self.qT.ap[0:64, hp, q0:q0 + 512],
                        start=True, stop=True, tile_position=(0, 0)),
                      r=[self.ktil, self.qT], w=[("pA", colh)], tag="qkA")
                    t(lambda e, hp=hp, kv=kv, kc=kc, q0=q0, colh=colh: e.matmul(
                        self.pB.ap[:, colh * 512:(colh + 1) * 512],
                        self.ktil.ap[64:128, kv, kc * 128:(kc + 1) * 128],
                        self.qT.ap[64:128, hp, q0:q0 + 512],
                        start=True, stop=True, tile_position=(64, 0)),
                      r=[self.ktil, self.qT], w=[("pB", colh)], tag="qkB")

                emit_qk(0)
                for i in range(32):
                    kc, colh = i // 2, i % 2
                    bA = pas[0][kc % 2]
                    bB = pas[1][kc % 2]
                    if i + 1 < 32:
                        emit_qk(i + 1)
                    a(lambda e, hp=hp, bA=bA, colh=colh: e.activation(
                        bA.ap[:, colh * 512:(colh + 1) * 512],
                        self.pA.ap[:, colh * 512:(colh + 1) * 512], AF.Exp,
                        bias=self.chb.ap[:, 2 * hp:2 * hp + 1]),
                      r=[("pA", colh), self.chb], w=[(id(bA), colh)] + list(bA.keys),
                      tag="expA")
                    a(lambda e, hp=hp, bB=bB, colh=colh: e.activation(
                        bB.ap[:, colh * 512:(colh + 1) * 512],
                        self.pB.ap[:, colh * 512:(colh + 1) * 512], AF.Exp,
                        bias=self.chb.ap[:, 2 * hp + 1:2 * hp + 2]),
                      r=[("pB", colh), self.chb], w=[(id(bB), colh)] + list(bB.keys),
                      tag="expB")
                    t(lambda e, kv=kv, kc=kc, colh=colh, bA=bA: e.matmul(
                        self.pC.ap[0:65, colh * 512:(colh + 1) * 512],
                        self.vT.ap[:, kv, kc * 65:kc * 65 + 65],
                        bA.ap[:, colh * 512:(colh + 1) * 512],
                        start=(kc == 0), stop=(kc == 15)),
                      r=[self.vT, (id(bA), colh)], w=[("pC", colh)], tag="pvA")
                    t(lambda e, kv=kv, kc=kc, colh=colh, bB=bB: e.matmul(
                        self.pD.ap[0:65, colh * 512:(colh + 1) * 512],
                        self.vT.ap[:, kv, kc * 65:kc * 65 + 65],
                        bB.ap[:, colh * 512:(colh + 1) * 512],
                        start=(kc == 0), stop=(kc == 15)),
                      r=[self.vT, (id(bB), colh)], w=[("pD", colh)], tag="pvB")
                for hh, pv in ((0, self.pC), (1, self.pD)):
                    pvk = [("pC", 0), ("pC", 1)] if hh == 0 else [("pD", 0), ("pD", 1)]
                    v(lambda e, hh=hh, pv=pv: e.tensor_copy(
                        self.q2s.ap[0:65, hh * 1024:(hh + 1) * 1024],
                        pv.ap[0:65, 0:1024]),
                      r=pvk + [self.q2s], w=[self.q2s], tag="pvev")
                for hh in range(2):
                    dent = (self.h2t, self.sqd)[hh]
                    v(lambda e, hh=hh, dent=dent: e.reciprocal(
                        dent.ap[64:65, :],
                        self.q2s.ap[64:65, hh * 1024:(hh + 1) * 1024]),
                      r=[self.q2s, dent], w=[dent], tag="denrec")
                    self.dget(lambda e, hh=hh, dent=dent: e.dma_start(
                        out=self.bounce[hh:hh + 1, :],
                        in_=dent.ap[64:65, :]),
                         r=[dent], w=[("bnc", hh)], tag="den_out")
                    self.dget(lambda e, hh=hh: e.dma_start(
                        out=self.recb.ap[:, hh, :],
                        in_=self.bounce[hh:hh + 1, :].broadcast_to([64, 1024])),
                         r=[("bnc", hh), self.recb], w=[self.recb], tag="den_in")
                    if hh == 0:
                        v(lambda e, hp=hp, qg=qg: e.tensor_tensor(
                            self.attnT.ap[0:64, hp, qg * 1024:(qg + 1) * 1024],
                            self.q2s.ap[0:64, 0:1024], self.recb.ap[:, 0, :], AO.mult),
                          r=[self.q2s, self.recb], w=[self.attnT], tag="nrmA")
                    else:
                        v(lambda e, qg=qg: e.tensor_tensor(
                            self.ntmp.ap[:, qg, :], self.q2s.ap[0:64, 1024:2048],
                            self.recb.ap[:, 1, :], AO.mult),
                          r=[self.q2s, self.recb, self.ntmp], w=[self.ntmp], tag="nrmB")
                        self.dget(lambda e, hp=hp, qg=qg: e.dma_start(
                            out=self.attnT.ap[64:128, hp, qg * 1024:(qg + 1) * 1024],
                            in_=self.ntmp.ap[:, qg, :]),
                             r=[self.ntmp], w=[self.attnT], tag="nrmBmv")

    def phaseD(self, s):
        v, a, t = self.v, self.a, self.t
        for g_ in range(4):
            if s == 0:
                self.pump_wq(8)
            for dc in range(8):
                p = (self.pA, self.pB)[dc % 2]
                for hp in range(8):
                    t(lambda e, g_=g_, dc=dc, hp=hp, p=p: e.matmul(
                        p.ap[:, 0:512], self.owT.ap[:, hp, dc * 128:(dc + 1) * 128],
                        self.attnT.ap[:, hp, g_ * 512:(g_ + 1) * 512],
                        start=(hp == 0), stop=(hp == 7)),
                      r=[self.owT, self.attnT], w=[p], tag="mmo")
                cga = (2 * 8 + dc) * 4 + s
                a(lambda e, dc=dc, p=p, cga=cga, s=s: e.activation(
                    self.gd2.ap[:, 0, dc, :], p.ap[:, 0:512], AF.Identity,
                    bias=self.gobT.ap[:, dc * 4 + s:dc * 4 + s + 1],
                    scale=self.modT.ap[:, cga:cga + 1]),
                  r=[p, self.gobT, self.modT], w=[self.gd2], tag="gdev")
            for t4 in range(4):
                tc = g_ * 4 + t4
                buf = t4 % 2
                xk = ("xio", buf)
                self.dget(lambda e, s=s, tc=tc, buf=buf: e.dma_start(
                    out=self.xio.ap[:, buf, :],
                    in_=self.x_in[s, tc * 128:(tc + 1) * 128, :]),
                     r=[xk], w=[xk], tag="ldx2")
                p2 = (self.pC, self.pD)[t4 % 2]
                for dc in range(8):
                    t(lambda e, dc=dc, t4=t4, p2=p2: e.transpose(
                        p2.ap.bitcast(BF16)[:, dc * 128:(dc + 1) * 128],
                        self.gd2.ap[:, 0, dc, t4 * 128:(t4 + 1) * 128], self.identb.ap),
                      r=[self.gd2, self.identb], w=[p2], tag="trD")
                v(lambda e, buf=buf, p2=p2: e.tensor_tensor(
                    self.xio.ap[:, buf, :], self.xio.ap[:, buf, :],
                    p2.ap.bitcast(BF16)[:, 0:1024], AO.add),
                  r=[xk, p2], w=[xk], tag="resadd")
                self.dput(lambda e, tc=tc, buf=buf: e.dma_start(
                    out=self.xnew_d[tc * 128:(tc + 1) * 128, :],
                    in_=self.xio.ap[:, buf, :]),
                     r=[xk], w=[("xn_d", tc)], tag="stxn")
                if "xnew" in self.dbg and s == 0:
                    self.dput(lambda e, tc=tc, buf=buf: e.dma_start(
                        out=self.xnew_dbg[tc * 128:(tc + 1) * 128, :],
                        in_=self.xio.ap[:, buf, :]),
                         r=[xk], w=[("xnd", tc)], tag="dbgxn")

    def phaseE(self, s):
        v, a, t = self.v, self.a, self.t
        self.dget(lambda e, s=s: e.dma_start(
            out=self.cb1r.ap, in_=self.mod_d[s:s + 1, 4 * D:5 * D].broadcast_to([128, D])),
             r=[("mod_d", 8), ("mod_d", 9), self.cb1r], w=[self.cb1r], tag="cb1r")
        self.dget(lambda e, s=s: e.dma_start(
            out=self.sbr.ap, in_=self.mod_d[s:s + 1, 3 * D:4 * D].broadcast_to([128, D])),
             r=[("mod_d", 6), ("mod_d", 7), self.sbr], w=[self.sbr], tag="sbr")
        self._e4u = []
        for half in range(2):
            for tcl in range(8):
                for _ in range(2):
                    if self._e4u:
                        self._e4u.pop(0)()
                tc = half * 8 + tcl
                buf = tcl % 2
                xk = ("xio", buf)
                self.dget(lambda e, tc=tc, buf=buf: e.dma_start(
                    out=self.xio.ap[:, buf, :],
                    in_=self.xnew_d[tc * 128:(tc + 1) * 128, :]),
                     r=[xk, ("xn_d", tc)], w=[xk], tag="ldxn")
                self.quant_flush()
                sc_, bi_ = self.ln_stats(self.xio.ap[:, buf, :], xk, tag="ln2",
                                         dump=self.h2e)
                a(lambda e, buf=buf, sc_=sc_, bi_=bi_: e.activation(
                    self.h2e.ap, self.xio.ap[:, buf, :], AF.Identity,
                    bias=bi_, scale=sc_),
                  r=[xk, self.st2, self.h2e], w=[self.h2e], tag="xn2")
                v(lambda e: e.tensor_mul(self.h2e.ap, self.h2e.ap, self.cb1r.ap),
                  r=[self.h2e, self.cb1r], w=[self.h2e])
                v(lambda e: e.tensor_add(self.h2e.ap, self.h2e.ap, self.sbr.ap),
                  r=[self.h2e, self.sbr], w=[self.h2e])
                for so in (0, 512):
                    aft = []
                    for c in range(so // 128, so // 128 + 4):
                        def mk(c=c, tcl=tcl):
                            pbk = ("pB", c % 2)
                            t(lambda e: e.transpose(
                                self.pB.ap.bitcast(BF16)[:, c % 2 * 1024:c % 2 * 1024 + 128],
                                self.hq2.ap[:, c * 128:(c + 1) * 128],
                                self.identb.ap), r=[self.hq2, self.identb], w=[pbk],
                              tag="trh")
                            a(lambda e: e.activation(
                                self.hqT.ap[:, c, tcl * 128:(tcl + 1) * 128],
                                self.pB.ap.bitcast(BF16)[:, c % 2 * 1024:c % 2 * 1024 + 128],
                                AF.Copy),
                              r=[pbk], w=[self.hqT], tag="hqcp")
                        aft.append(mk)
                    self.emit_quant(
                        Tile(self.h2e.ap[:, so:so + 512], list(self.h2e.keys)),
                        Tile(self.hq2.ap[:, so:so + 512], [self.hq2.keys[0]]),
                        512, after=aft)
            self.quant_flush()
            chunks = [(0, 512), (512, 512), (1024, 512), (1536, 512), (2048, 512),
                      (2560, 256)]
            for ici, (woff, w_) in enumerate(chunks):
                wgt = self.wgs[ici % 2]
                wut = self.wus[ici % 2]
                self.dget(lambda e, woff=woff, w_=w_, wgt=wgt: e.dma_start(
                    out=wgt.ap[:, :, 0:w_],
                    in_=self.wgqT_d[:, woff:woff + w_].rearrange("(c p) n -> p c n", p=128)),
                     r=[wgt, ("wgq",)], w=[wgt], tag="ldwg")
                self.dget(lambda e, woff=woff, w_=w_, wut=wut: e.dma_start(
                    out=wut.ap[:, :, 0:w_],
                    in_=self.wuqT_d[:, woff:woff + w_].rearrange("(c p) n -> p c n", p=128)),
                     r=[wut, ("wuq",)], w=[wut], tag="ldwu")
                for tcl in range(8):
                    p = (self.pC, self.pD, self.pA)[(ici * 8 + tcl) % 3]
                    pk = [(p.keys[0][0], 0), (p.keys[0][0], 1)]
                    for c in range(8):
                        t(lambda e, c=c, tcl=tcl, wgt=wgt, w_=w_, p=p: e.matmul(
                            p.ap[:, 0:w_], self.hqT.ap[:, c, tcl * 128:(tcl + 1) * 128],
                            wgt.ap[:, c, 0:w_], start=(c == 0), stop=(c == 7)),
                          r=[self.hqT, wgt], w=[pk[0]], tag="mmg")
                    for c in range(8):
                        t(lambda e, c=c, tcl=tcl, wut=wut, w_=w_, p=p: e.matmul(
                            p.ap[:, 512:512 + w_],
                            self.hqT.ap[:, c, tcl * 128:(tcl + 1) * 128],
                            wut.ap[:, c, 0:w_], start=(c == 0), stop=(c == 7)),
                          r=[self.hqT, wut], w=[pk[1]], tag="mmu")
                    a(lambda e, w_=w_, p=p: e.activation(self.sg.ap[:, 0:w_],
                                                         p.ap[:, 0:w_], AF.Silu),
                      r=[pk[0], self.sg], w=[self.sg], tag="silu")
                    pre = [lambda w_=w_, p=p, pk=pk: v(
                        lambda e: e.tensor_tensor(
                            self.acts.ap[:, 0:w_], self.sg.ap[:, 0:w_],
                            p.ap[:, 512:512 + w_], AO.mult),
                        r=[self.sg, pk[1], self.acts], w=[self.acts], tag="actm")]
                    aft = []
                    for u in range(w_ // 128):
                        c2 = (woff + u * 128) // 128
                        def mk(u=u, c2=c2, tcl=tcl):
                            pbk = ("pB", u % 2)
                            t(lambda e: e.transpose(
                                self.pB.ap.bitcast(BF16)[:, u % 2 * 1024:u % 2 * 1024 + 128],
                                self.qT3.ap[:, u * 128:(u + 1) * 128],
                                self.identb.ap), r=[self.qT3, self.identb], w=[pbk],
                              tag="tra")
                            a(lambda e: e.activation(
                                self.actqT.ap[:, c2, tcl * 128:(tcl + 1) * 128],
                                self.pB.ap.bitcast(BF16)[:, u % 2 * 1024:u % 2 * 1024 + 128],
                                AF.Copy),
                              r=[pbk], w=[self.actqT], tag="aqcp")
                        aft.append(mk)
                    self.emit_quant(Tile(self.acts.ap[:, 0:w_], self.acts.keys),
                                    Tile(self.qT3.ap[:, 0:w_], self.qT3.keys), w_,
                                    qout=True, after=aft, pre=pre)
            self.quant_flush()
            def mk_e4_units(half=half, s=s):
                units = []
                for dc in range(8):
                    def u_dc(dc=dc, s=s):
                        wdt = self.wds[dc % 2]
                        self.dget(lambda e, dc=dc, wdt=wdt: e.dma_start(
                            out=wdt.ap,
                            in_=self.wdqT_d[:, dc * 128:(dc + 1) * 128]
                                .rearrange("(c p) n -> p c n", p=128)),
                             r=[wdt, ("wdq",)], w=[wdt], tag="ldwd")
                        for g2 in range(2):
                            p = (self.pC, self.pD)[g2]
                            for c2 in range(22):
                                t(lambda e, c2=c2, g2=g2, wdt=wdt, p=p: e.matmul(
                                    p.ap[:, 0:512], wdt.ap[:, c2, :],
                                    self.actqT.ap[:, c2, g2 * 512:(g2 + 1) * 512],
                                    start=(c2 == 0), stop=(c2 == 21)),
                                  r=[wdt, self.actqT], w=[p], tag="mmd")
                            cgb = (5 * 8 + dc) * 4 + s
                            a(lambda e, dc=dc, g2=g2, p=p, cgb=cgb: e.activation(
                                self.gd2.ap[:, g2, dc, :], p.ap[:, 0:512], AF.Identity,
                                scale=self.modT.ap[:, cgb:cgb + 1]),
                              r=[p, self.modT], w=[self.gd2], tag="fdev")
                    units.append(u_dc)
                for g2 in range(2):
                    for t4 in range(4):
                        def u_res(g2=g2, t4=t4, half=half, s=s):
                            tc = half * 8 + g2 * 4 + t4
                            rb = self.q2s.ap.bitcast(F32)[:, 0:1024]
                            self.dget(lambda e, tc=tc, rb=rb: e.dma_start(
                                out=rb,
                                in_=self.xnew_d[tc * 128:(tc + 1) * 128, :]),
                                 r=[self.q2s, ("xn_d", tc)], w=[self.q2s], tag="ldxn2")
                            p2 = (self.pC, self.pD)[t4 % 2]
                            for dc in range(8):
                                t(lambda e, dc=dc, g2=g2, t4=t4, p2=p2: e.transpose(
                                    p2.ap.bitcast(BF16)[:, dc * 128:(dc + 1) * 128],
                                    self.gd2.ap[:, g2, dc, t4 * 128:(t4 + 1) * 128],
                                    self.identb.ap),
                                  r=[self.gd2, self.identb], w=[p2], tag="trE")
                            v(lambda e, rb=rb, p2=p2: e.tensor_tensor(
                                rb, rb, p2.ap.bitcast(BF16)[:, 0:1024],
                                AO.add), r=[self.q2s, p2], w=[self.q2s], tag="fres")
                            self.dput(lambda e, s=s, tc=tc, rb=rb: e.dma_start(
                                out=self.out[s, tc * 128:(tc + 1) * 128, :],
                                in_=rb),
                                 r=[self.q2s], w=[("out", s, tc)], tag="stout")
                        units.append(u_res)
                return units

            self._e4u.extend(mk_e4_units())
        while self._e4u:
            self._e4u.pop(0)()

    def finish(self):
        sc = self.sc

        def final_wait(g):
            last = None
            for slot, n in sc.slot_n.items():
                last = g.wait_ge(sc.qsem[slot], 16 * n)
            return last
        sc.op("gpsimd", final_wait, tag="final_wait")
        sc.finalize(self.es)
        blocks = self.es.enter_context(self.nc.Block())
        for e in ENGS:
            getattr(blocks, e)(lambda eng, e=e: sc.emit(e, eng))
        self.es.close()



def kernel(L0_attn_hidden_states, L0_attn_scaling, L0_attn_qkv_w, L0_attn_qkv_b,
           L0_attn_o_w, L0_attn_o_b, L0_ffn_w_gate, L0_ffn_w_up, L0_ffn_w_down,
           L0_c, L0_wmod):
    from concourse.bass_utils import run_bass_kernel_spmd
    x = np.ascontiguousarray(np.asarray(L0_attn_hidden_states, dtype=np.float32))
    c = np.ascontiguousarray(np.asarray(L0_c, dtype=np.float32))
    common = {
        "scaling": np.ascontiguousarray(np.asarray(L0_attn_scaling, np.float32).reshape(HD, 1)),
        "qkv_w": np.ascontiguousarray(np.asarray(L0_attn_qkv_w, np.float32)),
        "qkv_b": np.ascontiguousarray(np.asarray(L0_attn_qkv_b, np.float32)),
        "o_w": np.ascontiguousarray(np.asarray(L0_attn_o_w, np.float32)),
        "o_b": np.ascontiguousarray(np.asarray(L0_attn_o_b, np.float32)),
        "w_gate": np.ascontiguousarray(np.asarray(L0_ffn_w_gate, np.float32)),
        "w_up": np.ascontiguousarray(np.asarray(L0_ffn_w_up, np.float32)),
        "w_down": np.ascontiguousarray(np.asarray(L0_ffn_w_down, np.float32)),
        "wmod": np.ascontiguousarray(np.asarray(L0_wmod, np.float32)),
    }
    b = BuildK().build()
    in_maps = [dict(common, x=np.ascontiguousarray(x[4 * i:4 * i + 4]),
                    c=np.ascontiguousarray(c[4 * i:4 * i + 4])) for i in range(8)]
    res = run_bass_kernel_spmd(b.nc, in_maps, core_ids=list(range(8)))
    return np.concatenate([np.asarray(r["out"]) for r in res.results], axis=0)


# version 08
# speedup vs baseline: 1.6519x; 1.0358x over previous
import contextlib
import numpy as np
import concourse.bass as bass
import concourse.mybir as mybir
from concourse.alu_op_type import AluOpType as AO

F32 = mybir.dt.float32
BF16 = mybir.dt.bfloat16
AF = mybir.ActivationFunctionType
AX = mybir.AxisListType

ENGS = ("sync", "tensor", "vector", "scalar", "gpsimd")
DMA_QUEUES = {"sync": "qs", "gpsimd": "qg", "scalar": "qa"}


R_DMA_SEMS = 10


class Sched:
    def __init__(self, nc):
        self.nc = nc
        self.ops = []
        self.eng_ops = {e: [] for e in ENGS}
        self.res = {}
        self.qcount = {q: 0 for q in DMA_QUEUES.values()}
        self.slot_n = {}
        self.slot_consumers = {}

    def op(self, eng, fn, reads=(), writes=(), dma=False, tag=""):
        idx = len(self.ops)
        q = DMA_QUEUES[eng] if dma else None
        o = {"i": idx, "eng": eng, "fn": fn, "dma_q": q, "need_inc": dma,
             "waits": set(), "tag": tag}
        deps = set()
        for r in reads:
            e = self.res.get(r)
            if e and e["w"] is not None:
                deps.add(e["w"])
        for w in writes:
            e = self.res.get(w)
            if e:
                if e["w"] is not None:
                    deps.add(e["w"])
                deps.update(e["r"])
        if dma:
            slot = (q, self.qcount[q] % R_DMA_SEMS)
            self.qcount[q] += 1
            n = self.slot_n.get(slot, 0)
            o["slot"], o["sval"] = slot, 16 * (n + 1)
            self.slot_n[slot] = n + 1
            deps.update(self.slot_consumers.get(slot, ()))
            self.slot_consumers[slot] = set()
        rdeps = set()
        for r in reads:
            e = self.res.get(r)
            if e and e["w"] is not None:
                rdeps.add(e["w"])
        for d in deps:
            dop = self.ops[d]
            if dop["dma_q"] is not None:
                self.slot_consumers.setdefault(dop["slot"], set()).add(idx)
            elif dop["eng"] == eng:
                if (eng != "tensor" and d in rdeps
                        and len(self.eng_ops[eng]) - dop["eng_pos"] <= 1):
                    pass
                else:
                    continue
            dop["need_inc"] = True
            o["waits"].add(d)
        o["eng_pos"] = len(self.eng_ops[eng])
        self.ops.append(o)
        self.eng_ops[eng].append(o)
        for r in reads:
            self.res.setdefault(r, {"w": None, "r": set()})["r"].add(idx)
        for w in writes:
            self.res[w] = {"w": idx, "r": set()}
        return idx

    def dma(self, eng, fn, reads=(), writes=(), tag=""):
        return self.op(eng, fn, reads, writes, dma=True, tag=tag)

    PARANOID = None

    def finalize(self, es):
        nc = self.nc
        if self.PARANOID:
            lo, hi = self.PARANOID
            for i in range(max(lo, 1), min(hi, len(self.ops))):
                self.ops[i - 1]["need_inc"] = True
                self.ops[i]["waits"].add(i - 1)
        self.esem = {e: es.enter_context(nc.semaphore("sem_" + e)) for e in ENGS}
        self.qsem = {}
        for q in set(DMA_QUEUES.values()):
            for si in range(R_DMA_SEMS):
                self.qsem[(q, si)] = es.enter_context(
                    nc.semaphore(f"sem_{q}_{si}"))
        for e in ENGS:
            n = 0
            for o in self.eng_ops[e]:
                if o["need_inc"] and o["dma_q"] is None:
                    n += 1
                    o["inc_seq"] = n
        for o in self.ops:
            wl = {}
            for d in o["waits"]:
                dop = self.ops[d]
                if dop["dma_q"] is not None:
                    sem, val = self.qsem[dop["slot"]], dop["sval"]
                else:
                    sem, val = self.esem[dop["eng"]], dop["inc_seq"]
                k = id(sem)
                if k not in wl or wl[k][1] < val:
                    wl[k] = (sem, val)
            o["wlist"] = list(wl.values())

    def emit(self, eng_name, engine):
        seen = {}
        for o in self.eng_ops[eng_name]:
            for sem, val in o["wlist"]:
                k = id(sem)
                if seen.get(k, -1) >= val:
                    continue
                seen[k] = val
                engine.wait_ge(sem, val)
            inst = o["fn"](engine)
            if o["need_inc"]:
                assert inst is not None, f"op {o['i']} {o['tag']} returned no inst"
                if o["dma_q"] is not None:
                    inst.then_inc(self.qsem[o["slot"]], 16)
                else:
                    inst.then_inc(self.esem[o["eng"]], 1)

    def stats(self):
        return {e: len(self.eng_ops[e]) for e in ENGS} | {
            "incs": sum(1 for o in self.ops if o["need_inc"]),
            "total": len(self.ops), "q": dict(self.qcount)}


class Tile:
    def __init__(self, ap, keys):
        self.ap = ap
        self.keys = tuple(keys)

    def __getitem__(self, idx):
        return self.ap[idx]


NSEQ = 4
S = 2048
D = 1024
NH, NKV, HD = 16, 4, 64
QKV = NH * HD + 2 * NKV * HD
II = 2816
R_SOFT = 1.442695041
EPS = 1e-5
BLK = 1024


class Build:
    def __init__(self, dbg=(), upto=99, nseq=NSEQ):
        self.dbg = set(dbg)
        self.upto = upto
        self.nseq = nseq
        self.nc = bass.Bass("TRN2", target_bir_lowering=False)
        self.es = contextlib.ExitStack()
        self.sc = Sched(self.nc)

    def dram(self, name, shape, dtype=F32, kind="ExternalInput"):
        return self.nc.dram_tensor(name, list(shape), dtype, kind=kind).ap()

    def sb(self, name, shape, dtype=F32):
        ap = self.es.enter_context(self.nc.sbuf_tensor(name, list(shape), dtype))
        return Tile(ap[...], [("t", name)])

    def carve(self, off_elems, shape, dtype=F32):
        n = int(np.prod(shape[1:]))
        width = n * (2 if dtype == F32 else 1)
        ap = self.pool[:, off_elems:off_elems + width]
        if dtype != BF16:
            ap = ap.bitcast(dtype)
        if len(shape) > 2:
            kw = {}
            names = []
            for i, s_ in enumerate(shape[1:]):
                names.append(chr(ord('a') + i))
                kw[names[i]] = s_
            ap = ap.rearrange(f"p ({' '.join(names)}) -> p {' '.join(names)}", **kw)
        b0, b1 = off_elems // BLK, (off_elems + width - 1) // BLK
        return Tile(ap, [("P", b) for b in range(b0, b1 + 1)])

    def ps(self, name, shape):
        ap = self.es.enter_context(self.nc.psum_tensor(name, list(shape), F32))
        return Tile(ap[...], [(name, 0), (name, 1)])

    def _keys(self, tiles):
        ks = []
        for t in tiles:
            if isinstance(t, Tile):
                ks.extend(t.keys)
            else:
                ks.append(t)
        return ks

    def v(self, fn, r=(), w=(), tag=""):
        return self.sc.op("vector", fn, self._keys(r), self._keys(w), tag=tag)

    def a(self, fn, r=(), w=(), tag=""):
        return self.sc.op("scalar", fn, self._keys(r), self._keys(w), tag=tag)

    def t(self, fn, r=(), w=(), tag=""):
        return self.sc.op("tensor", fn, self._keys(r), self._keys(w), tag=tag)

    def g(self, fn, r=(), w=(), tag=""):
        return self.sc.op("gpsimd", fn, self._keys(r), self._keys(w), tag=tag)

    def dget(self, fn, r=(), w=(), tag=""):
        return self.sc.dma("sync", fn, self._keys(r), self._keys(w), tag=tag)

    def dput(self, fn, r=(), w=(), tag=""):
        return self.sc.dma("gpsimd", fn, self._keys(r), self._keys(w), tag=tag)

    def dump(self, tag, ap, shape, dtype=F32, deps=()):
        if tag not in self.dbg:
            return
        t = self.dram("dbg_" + tag, shape, dtype, kind="ExternalOutput")
        self.dput(lambda g, t=t, ap=ap: g.dma_start(out=t[...], in_=ap),
                  r=list(deps), tag="dump_" + tag)


POOL_N = 69632

OFF_qT = 0
OFF_kvT = 16384
OFF_h1T = 24576
OFF_ktil = 24576
OFF_vT = 32768
OFF_paA0 = 37888
OFF_paA1 = 38912
OFF_paB0 = 39936
OFF_paB1 = 40960
OFF_xnb = 40960
OFF_attnT = 45056
OFF_wgs0 = 0
OFF_wgs1 = 4096
OFF_wus0 = 8192
OFF_wus1 = 12288
OFF_wds0 = 16384
OFF_wds1 = 19456
OFF_gd2 = 22528
OFF_hqT = 30720
OFF_cb1r = 38912
OFF_sbr = 40960
OFF_hq2 = 45056
OFF_actqT = 46080
OFF_wtmp0 = 0
OFF_wtmp1 = 5632
OFF_wbf0 = 11264
OFF_wbf1 = 14336
OFF_stg = 17408
OFF_cbuf = 18432
OFF_mslab = 20480
OFF_wmbuf0 = 24576
OFF_wmbuf1 = 32768
OFF_wmbf0 = 40960
OFF_wmbf1 = 45056
OFF_csb = 49152


class BuildK(Build):
    def build(self):
        nc, sc = self.nc, self.sc
        self.x_in = self.dram("x", (NSEQ, S, D))
        self.c_in = self.dram("c", (NSEQ, D))
        self.scaling = self.dram("scaling", (HD, 1))
        self.qkv_w = self.dram("qkv_w", (QKV, D))
        self.qkv_b = self.dram("qkv_b", (QKV,))
        self.o_w = self.dram("o_w", (D, D))
        self.o_b = self.dram("o_b", (D,))
        self.w_gate = self.dram("w_gate", (II, D))
        self.w_up = self.dram("w_up", (II, D))
        self.w_down = self.dram("w_down", (D, II))
        self.wmod = self.dram("wmod", (D, 6 * D))
        self.out = self.dram("out", (NSEQ, S, D), kind="ExternalOutput")
        self.wgqT_d = self.dram("wgqT_d", (D, II), BF16,
                                kind="ExternalOutput" if "wgq_d" in self.dbg else "Internal")
        self.wuqT_d = self.dram("wuqT_d", (D, II), BF16, kind="Internal")
        self.wdqT_d = self.dram("wdqT_d", (II, D), BF16, kind="Internal")
        self.xnew_d = self.dram("xnew_d", (S, D), F32, kind="Internal")
        self.mod_d = self.dram("mod_d", (4, 6 * D), F32, kind="Internal")
        self.bounce = self.dram("bounce", (8, 1024), F32, kind="Internal")
        if "xnew" in self.dbg:
            self.xnew_dbg = self.dram("dbg_xnew", (S, D), F32, kind="ExternalOutput")

        self.pool = self.es.enter_context(
            self.nc.sbuf_tensor("POOL", [128, POOL_N], BF16))[...]
        self.identb = self.sb("identb", (128, 128), BF16)
        self.onesb = self.sb("onesb", (128, 1), BF16)
        self.modT = self.sb("modT", (128, 192))
        self.qscT = self.sb("qscT", (128, 12))
        self.qbsT = self.sb("qbsT", (128, 12))
        self.obT = self.sb("obT", (128, 8))
        self.gobT = self.sb("gobT", (128, 32))
        self.sc64 = self.sb("sc64", (64, 1))
        self.epsT = self.sb("epsT", (128, 1))
        self.stgr = self.sb("stgr", (128, 4, 128), BF16)
        self.sc128 = self.sb("sc128", (128, 1))
        self.wqkvT = self.sb("wqkvT", (128, 8, QKV), BF16)
        self.owT = self.sb("owT", (128, 8, D), BF16)
        self.qU = self.sb("qU", (128, 512))
        self.qAU = self.sb("qAU", (128, 512))
        self.qT1 = self.sb("qT1", (128, 512), BF16)
        self.qT2 = self.sb("qT2", (128, 512), BF16)
        self.qT3 = self.sb("qT3", (128, 512), BF16)
        self.qT4 = self.sb("qT4", (128, 512), BF16)
        self.qBM = self.sb("qBM", (128, 32))
        self.qR = self.sb("qR", (128, 32))
        self.qS2 = self.sb("qS2", (128, 64))
        self._qpend = None
        self._qidx = 0
        self.st1 = self.sb("st_a", (128, 8))
        self.st2 = self.sb("st_b", (128, 8))
        self.chb = self.sb("chb", (128, 16))
        self.recb = self.sb("recb", (64, 2, 1024))
        self.ntmp = self.sb("ntmp", (64, 2, 1024), BF16)
        self.xio = self.sb("xio", (128, 2, D))
        self.pA = self.ps("pA", (128, 1024))
        self.pB = self.ps("pB", (128, 1024))
        self.pC = self.ps("pC", (128, 1024))
        self.pD = self.ps("pD", (128, 1024))

        cv = self.carve
        self.qT = cv(OFF_qT, (128, 8, S), BF16)
        self.kvT = cv(OFF_kvT, (128, 4, S), BF16)
        self.h1T = cv(OFF_h1T, (128, 8, S), BF16)
        self.xnb = cv(OFF_xnb, (128, 4, D), BF16)
        self.attnT = cv(OFF_attnT, (128, 8, S), BF16)
        self.ktil = cv(OFF_ktil, (128, 4, S), BF16)
        self.vT = cv(OFF_vT, (128, 4, 1040), BF16)
        self.paA0 = cv(OFF_paA0, (128, 1024), BF16)
        self.paA1 = cv(OFF_paA1, (128, 1024), BF16)
        self.paB0 = cv(OFF_paB0, (128, 1024), BF16)
        self.paB1 = cv(OFF_paB1, (128, 1024), BF16)
        self.wgs = [cv(OFF_wgs0, (128, 8, 512), BF16), cv(OFF_wgs1, (128, 8, 512), BF16)]
        self.wus = [cv(OFF_wus0, (128, 8, 512), BF16), cv(OFF_wus1, (128, 8, 512), BF16)]
        self.wds = [cv(OFF_wds0, (128, 22, 128), BF16), cv(OFF_wds1, (128, 22, 128), BF16)]
        self.gd2 = cv(OFF_gd2, (128, 2, 8, 512), BF16)
        self.hqT = cv(OFF_hqT, (128, 8, 1024), BF16)
        self.cb1r = cv(OFF_cb1r, (128, D), F32)
        self.sbr = cv(OFF_sbr, (128, D), F32)
        self.hq2 = cv(OFF_hq2, (128, D), BF16)
        self.actqT = cv(OFF_actqT, (128, 22, 1024), BF16)
        self.q2s = cv(16384, (128, 2048), BF16)
        self.h2t = cv(63488, (128, D), F32)
        self.sg = cv(43008, (128, 512), F32)
        self.acts = cv(44032, (128, 512), F32)
        self.sqd = cv(61440, (128, D), F32)
        self.h2e = cv(43008, (128, D), F32)
        self.wqt = [cv(65536, (128, 512), F32), cv(66560, (128, 512), F32)]
        self.wqb = [cv(67584, (128, 512), BF16), cv(68096, (128, 512), BF16)]
        self.identf = cv(45056, (128, 128), F32)
        self.iot1 = cv(45312, (128, 128), F32)
        self.iot2 = cv(45568, (128, 128), F32)
        self.wtmp = [cv(OFF_wtmp0, (128, II), F32), cv(OFF_wtmp1, (128, II), F32)]
        self.wbf = [cv(OFF_wbf0, (128, II), BF16), cv(OFF_wbf1, (128, II), BF16)]
        self.stg = cv(OFF_stg, (128, 4, 128), BF16)
        self.cbuf = cv(OFF_cbuf, (128, D), F32)
        self.mslab = cv(OFF_mslab, (128, 2048), F32)
        self.wmbuf = [cv(OFF_wmbuf0, (128, 8, 512), F32), cv(OFF_wmbuf1, (128, 8, 512), F32)]
        self.wmbf = [cv(OFF_wmbf0, (128, 8, 512), BF16), cv(OFF_wmbf1, (128, 8, 512), BF16)]
        self.csf = cv(OFF_csb, (128, 8, 4), F32)

        self.setup()
        for s in range(self.nseq):
            if self.upto >= 1: self.phaseA(s)
            if self.upto >= 2: self.phaseB(s)
            if self.upto >= 3: self.phaseC(s)
            if self.upto >= 4: self.phaseD(s)
            if self.upto >= 5:
                self.pump_wq(200)
                self.phaseE(s)
        for name in self.dbg:
            if not hasattr(self, name) or not isinstance(getattr(self, name), Tile):
                continue
            tile = getattr(self, name)
            ap = tile.ap
            sh = list(ap.shape)
            self.dump(name, ap, sh, ap.dtype, deps=[tile])
        self.finish()
        return self

    def run(self, in_map, n_cores=1, **kw):
        from concourse.bass_utils import run_bass_kernel_spmd
        maps = [in_map] * n_cores if isinstance(in_map, dict) else in_map
        return run_bass_kernel_spmd(self.nc, maps, core_ids=list(range(n_cores)), **kw)

    def emit_quant(self, X, Q, W, xsl=None, qsl=None, rr=(), tag="q", qout=False,
                   after=(), pre=()):
        v, a = self.v, self.a
        NB = W // 16
        par = self._qidx % 2
        self._qidx += 1
        Xap = xsl(X.ap) if xsl else X.ap
        Qap = qsl(Q.ap) if qsl else Q.ap
        qU, qAU, qT1, qT2, qT3, qT4 = self.qU, self.qAU, self.qT1, self.qT2, self.qT3, self.qT4
        qBM, qR, qS2 = self.qBM, self.qR, self.qS2
        sc0 = par * 32
        x3 = Xap.rearrange("p (g k) -> p g k", k=16)
        S1 = [
            lambda: v(lambda e: e.tensor_reduce(qBM.ap[:, :NB], x3, AX.X, AO.max,
                                                apply_absolute_value=True),
                      r=[X] + list(rr), w=[qBM], tag=tag + "bm"),
            lambda: v(lambda e: e.tensor_scalar(qBM.ap[:, :NB], qBM.ap[:, :NB],
                                                1.0 / 6.0, 1e-12, AO.mult, AO.max),
                      r=[qBM], w=[qBM], tag=tag),
            lambda: v(lambda e: e.reciprocal(qR.ap[:, :NB], qBM.ap[:, :NB]),
                      r=[qBM], w=[qR]),
            lambda: v(lambda e: e.tensor_scalar(qS2.ap[:, sc0:sc0 + NB], qBM.ap[:, :NB],
                                                0.5, None, AO.mult),
                      r=[qBM], w=[("qS2", par)]),
            lambda: v(lambda e: e.tensor_tensor(
                qU.ap[:, :W].rearrange("p (g k) -> p g k", k=16), x3,
                qR.ap[:, :NB].broadcast_to([128, NB, 16]), AO.mult),
                      r=[X, qR], w=[qU], tag=tag + "u"),
        ]
        actAU = lambda: a(lambda e: e.activation(qAU.ap[:, :W], qU.ap[:, :W], AF.Abs),
                          r=[qU], w=[qAU])
        actSG = lambda: a(lambda e: e.activation(qT4.ap[:, :W], qU.ap[:, :W], AF.Sign),
                          r=[qU], w=[qT4])
        S2 = [
            lambda: v(lambda e: e.tensor_scalar(qT1.ap[:, :W], qAU.ap[:, :W], 0.25,
                                                None, AO.is_gt), r=[qAU], w=[qT1]),
            lambda: v(lambda e: e.tensor_scalar(qT2.ap[:, :W], qAU.ap[:, :W], 0.75,
                                                None, AO.is_gt), r=[qAU], w=[qT2]),
            lambda: v(lambda e: e.tensor_scalar(qT3.ap[:, :W], qAU.ap[:, :W], 1.25,
                                                None, AO.is_gt), r=[qAU], w=[qT3]),
            lambda: v(lambda e: e.tensor_add(qT1.ap[:, :W], qT1.ap[:, :W],
                                             qT2.ap[:, :W]), r=[qT1, qT2], w=[qT1]),
            lambda: v(lambda e: e.tensor_scalar(qT2.ap[:, :W], qAU.ap[:, :W], 1.75,
                                                None, AO.is_gt), r=[qAU], w=[qT2]),
            lambda: v(lambda e: e.tensor_add(qT1.ap[:, :W], qT1.ap[:, :W],
                                             qT3.ap[:, :W]), r=[qT1, qT3], w=[qT1]),
            lambda: v(lambda e: e.tensor_scalar(qT3.ap[:, :W], qAU.ap[:, :W], 2.5, 2.0,
                                                AO.is_gt, AO.mult), r=[qAU], w=[qT3]),
            lambda: v(lambda e: e.tensor_add(qT1.ap[:, :W], qT1.ap[:, :W],
                                             qT2.ap[:, :W]), r=[qT1, qT2], w=[qT1]),
            lambda: v(lambda e: e.tensor_scalar(qT2.ap[:, :W], qAU.ap[:, :W], 3.5, 2.0,
                                                AO.is_gt, AO.mult), r=[qAU], w=[qT2]),
            lambda: v(lambda e: e.tensor_add(qT1.ap[:, :W], qT1.ap[:, :W],
                                             qT3.ap[:, :W]), r=[qT1, qT3], w=[qT1]),
            lambda: v(lambda e: e.tensor_scalar(qT3.ap[:, :W], qAU.ap[:, :W], 5.0, 4.0,
                                                AO.is_gt, AO.mult), r=[qAU], w=[qT3]),
            lambda: v(lambda e: e.tensor_add(qT1.ap[:, :W], qT1.ap[:, :W],
                                             qT2.ap[:, :W]), r=[qT1, qT2], w=[qT1]),
            lambda: v(lambda e: e.tensor_tensor(
                qT4.ap[:, :W].rearrange("p (g k) -> p g k", k=16),
                qT4.ap[:, :W].rearrange("p (g k) -> p g k", k=16),
                qS2.ap[:, sc0:sc0 + NB].broadcast_to([128, NB, 16]), AO.mult),
                      r=[qT4, ("qS2", par)], w=[qT4]),
            lambda: v(lambda e: e.tensor_add(qT1.ap[:, :W], qT1.ap[:, :W],
                                             qT3.ap[:, :W]), r=[qT1, qT3], w=[qT1]),
            lambda: v(lambda e: e.tensor_mul(Qap, qT1.ap[:, :W], qT4.ap[:, :W]),
                      r=[qT1, qT4], w=[Q], tag=tag + "fin"),
        ]
        P, afterP = self._qpend if self._qpend is not None else ([], [])
        for f in P:
            f()
        for f in afterP:
            f()
        for f in pre:
            f()
        for f in S1:
            f()
        actAU()
        actSG()
        self._qpend = (S2, list(after))

    def quant_flush(self):
        if self._qpend is None:
            return
        P, afterP = self._qpend
        self._qpend = None
        for f in P:
            f()
        for f in afterP:
            f()

    def ln_stats(self, xap, xtile, tag="", dump=None):
        v, a = self.v, self.a
        st1, st2 = self.st1, self.st2
        sqd = dump if dump is not None else self.sqd
        v(lambda e: e.reduce_sum(st1.ap[:, 0:1], xap, AX.X), r=[xtile], w=[st1], tag=tag + "sum")
        a(lambda e: e.activation(sqd.ap[...], xap, AF.Square,
                                 accum_out=st1.ap[:, 1:2]), r=[xtile, st1], w=[sqd, st1])
        a(lambda e: e.activation(st1.ap[:, 3:4], st1.ap[:, 0:1], AF.Square,
                                 scale=1.0 / D), r=[st1], w=[st1], tag=tag + "m2")
        v(lambda e: e.tensor_scalar(st1.ap[:, 2:3], st1.ap[:, 0:1], 1.0 / D, None,
                                    AO.mult), r=[st1], w=[st1])
        v(lambda e: e.scalar_tensor_tensor(st1.ap[:, 5:6], st1.ap[:, 1:2], 1.0 / D,
                                           st1.ap[:, 3:4], AO.mult, AO.subtract),
          r=[st1], w=[st1], tag=tag + "var")
        a(lambda e: e.activation(st2.ap[:, 0:1], st1.ap[:, 5:6], AF.Ln,
                                 bias=self.epsT.ap),
          r=[st1, st2, self.epsT], w=[st2])
        a(lambda e: e.activation(st2.ap[:, 4:5], st2.ap[:, 0:1], AF.Exp, scale=-0.5),
          r=[st2], w=[st2], tag=tag + "rs")
        v(lambda e: e.scalar_tensor_tensor(st2.ap[:, 5:6], st1.ap[:, 2:3], -1.0,
                                           st2.ap[:, 4:5], AO.mult, AO.mult),
          r=[st1, st2], w=[st2])
        return st2.ap[:, 4:5], st2.ap[:, 5:6]

    def setup(self):
        v, a, t, g = self.v, self.a, self.t, self.g
        dget, dput = self.dget, self.dput
        nc = self.nc
        g(lambda e: e.iota(self.iot1.ap, [[1, 128]], channel_multiplier=0,
                           allow_small_or_imprecise_dtypes=True), w=[self.iot1])
        g(lambda e: e.iota(self.iot2.ap, [[0, 128]], channel_multiplier=1,
                           allow_small_or_imprecise_dtypes=True), w=[self.iot2])
        v(lambda e: e.tensor_tensor(self.identf.ap, self.iot1.ap, self.iot2.ap,
                                    AO.is_equal), r=[self.iot1, self.iot2], w=[self.identf])
        v(lambda e: e.tensor_copy(self.identb.ap, self.identf.ap),
          r=[self.identf], w=[self.identb])
        v(lambda e: e.memset(self.onesb.ap, 1.0), w=[self.onesb])
        v(lambda e: e.memset(self.epsT.ap, EPS), w=[self.epsT])
        dget(lambda e: e.dma_start(out=self.sc64.ap, in_=self.scaling[...]),
             w=[self.sc64], tag="ld_scaling")
        a(lambda e: e.activation(self.sc64.ap, self.sc64.ap, AF.Exp),
          r=[self.sc64], w=[self.sc64])
        v(lambda e: e.tensor_scalar(self.sc64.ap, self.sc64.ap, 1.0, None, AO.add),
          r=[self.sc64], w=[self.sc64])
        a(lambda e: e.activation(self.sc64.ap, self.sc64.ap, AF.Ln),
          r=[self.sc64], w=[self.sc64])
        v(lambda e: e.tensor_scalar(self.sc64.ap, self.sc64.ap, R_SOFT / 8.0, None,
                                    AO.mult), r=[self.sc64], w=[self.sc64])
        dget(lambda e: e.dma_start(out=self.bounce[3:4, 0:64], in_=self.sc64.ap),
             r=[self.sc64], w=[("bnc", 3)], tag="sp_out")
        dget(lambda e: e.dma_start(out=self.sc128.ap[0:64, :], in_=self.bounce[3:4, 0:64]),
             r=[("bnc", 3)], w=[self.sc128])
        dget(lambda e: e.dma_start(out=self.sc128.ap[64:128, :], in_=self.bounce[3:4, 0:64]),
             r=[("bnc", 3)], w=[self.sc128])
        v(lambda e: e.tensor_copy(self.qscT.ap[:, 0:8].rearrange("p (a b) -> p a b", a=1),
                                  self.sc128.ap.broadcast_to([128, 1, 8])),
          r=[self.sc128], w=[self.qscT])
        v(lambda e: e.memset(self.qscT.ap[:, 8:12], 1.0), r=[self.qscT], w=[self.qscT])
        dget(lambda e: e.dma_start(out=self.qbsT.ap,
                                   in_=self.qkv_b.rearrange("(c p) -> p c", p=128),
                                   allow_slow_non_contiguous=True),
             w=[self.qbsT], tag="ld_qkvb")
        v(lambda e: e.tensor_mul(self.qbsT.ap, self.qbsT.ap, self.qscT.ap),
          r=[self.qbsT, self.qscT], w=[self.qbsT])
        dget(lambda e: e.dma_start(out=self.obT.ap,
                                   in_=self.o_b.rearrange("(c p) -> p c", p=128),
                                   allow_slow_non_contiguous=True),
             w=[self.obT], tag="ld_ob")
        dget(lambda e: e.dma_start(out=self.cbuf.ap[0:4, :], in_=self.c_in[...]),
             w=[self.cbuf], tag="ld_c")
        for c8 in range(8):
            t(lambda e, c8=c8: e.transpose(self.pC.ap[:, 0:4],
                                           self.cbuf.ap[0:4, c8 * 128:(c8 + 1) * 128],
                                           self.identf.ap[0:4, 0:4]),
              r=[self.cbuf, self.identf], w=[self.pC], tag="trc")
            a(lambda e, c8=c8: e.activation(self.csf.ap[:, c8, :], self.pC.ap[:, 0:4],
                                            AF.Silu), r=[self.pC], w=[self.csf])
        for j in range(12):
            wmb = self.wmbuf[j % 2]
            wmf = self.wmbf[j % 2]
            dget(lambda e, j=j, wmb=wmb: e.dma_start(
                out=wmb.ap,
                in_=self.wmod[:, j * 512:(j + 1) * 512]
                    .rearrange("(c p) n -> p c n", p=128)),
                 r=[wmb], w=[wmb], tag="ld_wmod")
            for c in range(8):
                t(lambda e, c=c, wmb=wmb: e.matmul(
                    self.pD.ap[0:4, 0:512], self.csf.ap[:, c, :],
                    wmb.ap[:, c, :],
                    start=(c == 0), stop=(c == 7)),
                  r=[self.csf, wmb], w=[self.pD], tag="mm_mod")
            a(lambda e: e.activation(self.mslab.ap[0:4, 0:512], self.pD.ap[0:4, 0:512],
                                     AF.Copy), r=[self.pD], w=[self.mslab])
            vec = (j * 512) // 1024
            if vec in (1, 4):
                v(lambda e: e.tensor_scalar(self.mslab.ap[0:4, 0:512],
                                            self.mslab.ap[0:4, 0:512], 1.0, None, AO.add),
                  r=[self.mslab], w=[self.mslab])
            if vec in (2, 5):
                sl = self.mslab.ap[0:4, 0:512]
                tm = self.mslab.ap[0:4, 512:1024]
                sn = self.mslab.ap[0:4, 1024:1536]
                v(lambda e, sl=sl, tm=tm: e.scalar_tensor_tensor(tm, sl, -1.0, sl,
                                                                 AO.mult, AO.max),
                  r=[self.mslab], w=[self.mslab])
                v(lambda e, tm=tm: e.tensor_scalar(tm, tm, 0.25, None, AO.max),
                  r=[self.mslab], w=[self.mslab])
                v(lambda e, sl=sl, sn=sn: e.tensor_scalar(sn, sl, 1e38, -1.0,
                                                          AO.mult, AO.max),
                  r=[self.mslab], w=[self.mslab])
                v(lambda e, sl=sl, tm=tm, sn=sn: e.scalar_tensor_tensor(
                    sl, sn, 1.0, tm, AO.min, AO.mult),
                  r=[self.mslab], w=[self.mslab])
            dput(lambda e, j=j: e.dma_start(out=self.mod_d[0:4, j * 512:(j + 1) * 512],
                                            in_=self.mslab.ap[0:4, 0:512]),
                 r=[self.mslab], w=[("mod_d", j)], tag="st_mod")
            for u in range(4):
                cn = 4 * j + u
                t(lambda e, u=u: e.transpose(self.pC.ap[:, 0:4],
                                             self.mslab.ap[0:4, u * 128:(u + 1) * 128],
                                             self.identf.ap[0:4, 0:4]),
                  r=[self.mslab, self.identf], w=[self.pC], tag="tr_mod")
                v(lambda e, cn=cn: e.tensor_copy(self.modT.ap[:, cn * 4:(cn + 1) * 4],
                                                 self.pC.ap[:, 0:4]),
                  r=[self.pC], w=[self.modT])
        for b in range(4):
            v(lambda e, b=b: e.tensor_mul(self.gobT.ap[:, b:32:4],
                                          self.modT.ap[:, 64 + b:96:4], self.obT.ap),
              r=[self.modT, self.obT], w=[self.gobT])
        for (src_w, dstT, nr) in ((self.qkv_w, self.wqkvT, 12), (self.o_w, self.owT, 8)):
            for r_ in range(nr):
                wt = self.wtmp[r_ % 2]
                wb = self.wbf[r_ % 2]
                dget(lambda e, src_w=src_w, r_=r_, wt=wt: e.dma_start(
                    out=wt.ap[:, 0:1024],
                    in_=src_w[r_ * 128:(r_ + 1) * 128, :]),
                     r=[wt], w=[wt], tag="ld_qow")
                v(lambda e, wt=wt, wb=wb: e.tensor_copy(wb.ap[:, 0:1024], wt.ap[:, 0:1024]),
                  r=[wt, wb], w=[wb])
                for c in range(8):
                    p = (self.pA, self.pB)[c % 2]
                    t(lambda e, wb=wb, c=c, p=p: e.transpose(
                        p.ap.bitcast(BF16)[:, 0:128], wb.ap[:, c * 128:(c + 1) * 128],
                        self.identb.ap), r=[wb, self.identb], w=[p], tag="tr_w")
                    v(lambda e, dstT=dstT, r_=r_, c=c, p=p: e.tensor_copy(
                        dstT.ap[:, c, r_ * 128:(r_ + 1) * 128], p.ap.bitcast(BF16)[:, 0:128]),
                      r=[p], w=[dstT])
        self._wq_gen = self._wq_generator()
        self._wq_done = False

    def _wq_generator(self):
        v, t, dget, dput = self.v, self.t, self.dget, self.dput
        for (w_in, w_out, wkey, nr, F) in (
                (self.w_gate, self.wgqT_d, "wgq", 22, 1024),
                (self.w_up, self.wuqT_d, "wuq", 22, 1024),
                (self.w_down, self.wdqT_d, "wdq", 8, II)):
            for r_ in range(nr):
                for so in range(0, F, 512):
                    w_ = min(512, F - so)
                    sl = (r_ * F + so) // 512 % 2
                    wt = self.wqt[sl]
                    wb = self.wqb[sl]
                    dget(lambda e, w_in=w_in, r_=r_, so=so, w_=w_, wt=wt: e.dma_start(
                        out=wt.ap[:, 0:w_],
                        in_=w_in[r_ * 128:(r_ + 1) * 128, so:so + w_]),
                         r=[wt], w=[wt], tag="ld_fw")
                    aft = []
                    for u in range(w_ // 128):
                        c = (so + u * 128) // 128
                        p = (self.pA, self.pB)[u % 2]
                        sg_ = u % 4
                        def mk(wb=wb, u=u, p=p, sg_=sg_, c=c, w_out=w_out, r_=r_,
                               wkey=wkey):
                            t(lambda e: e.transpose(
                                p.ap.bitcast(BF16)[:, 1024 + u % 2 * 128:1024 + u % 2 * 128 + 128],
                                wb.ap[:, u * 128:(u + 1) * 128],
                                self.identb.ap), r=[wb, self.identb], w=[p], tag="tr_fw")
                            v(lambda e: e.tensor_copy(
                                self.stgr.ap[:, sg_, :],
                                p.ap.bitcast(BF16)[:, 1024 + u % 2 * 128:1024 + u % 2 * 128 + 128]),
                              r=[p, ("stg", sg_)], w=[("stg", sg_)])
                            dput(lambda e: e.dma_start(
                                out=w_out[c * 128:(c + 1) * 128, r_ * 128:(r_ + 1) * 128],
                                in_=self.stgr.ap[:, sg_, :]),
                                 r=[("stg", sg_), (wkey,)], w=[(wkey,)], tag="st_fw")
                        aft.append(mk)
                    self.emit_quant(Tile(wt.ap[:, 0:w_], wt.keys),
                                    Tile(wb.ap[:, 0:w_], wb.keys), w_, after=aft)
                yield

    def pump_wq(self, n):
        if self._wq_done:
            return
        for _ in range(n):
            try:
                next(self._wq_gen)
            except StopIteration:
                self._wq_done = True
                self.quant_flush()
                return

    def emit_quant_sl(self, xap, xkey, qap, qkey, W):
        self.emit_quant(Tile(xap, [xkey] if not isinstance(xkey, list) else xkey),
                        Tile(qap, [qkey] if not isinstance(qkey, list) else qkey), W)

    def phaseA(self, s):
        v, a, t = self.v, self.a, self.t
        xio = self.xio
        for g_ in range(4):
            if s == 0 and g_:
                self.pump_wq(3)
            for t4 in range(4):
                tc = 4 * g_ + t4
                buf = tc % 2
                xk = ("xio", buf)
                self.dget(lambda e, s=s, tc=tc, buf=buf: e.dma_start(
                    out=xio.ap[:, buf, :],
                    in_=self.x_in[s, tc * 128:(tc + 1) * 128, :]),
                     r=[xk], w=[xk], tag="ldx")
                sc_, bi_ = self.ln_stats(xio.ap[:, buf, :], xk, tag="ln1")
                a(lambda e, t4=t4, buf=buf, sc_=sc_, bi_=bi_: e.activation(
                    self.xnb.ap[:, t4, :], xio.ap[:, buf, :], AF.Identity,
                    bias=bi_, scale=sc_),
                  r=[xk, self.st2], w=[self.xnb], tag="xn")
            for c in range(8):
                p = (self.pA, self.pB)[c % 2]
                for t4 in range(4):
                    t(lambda e, t4=t4, c=c, p=p: e.transpose(
                        p.ap.bitcast(BF16)[:, t4 * 128:(t4 + 1) * 128],
                        self.xnb.ap[:, t4, c * 128:(c + 1) * 128], self.identb.ap),
                      r=[self.xnb, self.identb], w=[p], tag="trA")
                csa = (0 * 8 + c) * 4 + s
                cca = (1 * 8 + c) * 4 + s
                a(lambda e, c=c, g_=g_, p=p, csa=csa, cca=cca: e.activation(
                    self.h1T.ap[:, c, g_ * 512:(g_ + 1) * 512],
                    p.ap.bitcast(BF16)[:, 0:512],
                    AF.Identity, bias=self.modT.ap[:, csa:csa + 1],
                    scale=self.modT.ap[:, cca:cca + 1]),
                  r=[p, self.modT], w=[self.h1T], tag="h1ev")
        for j in range(12):
            for g_ in range(4):
                p = (self.pC, self.pD)[g_ % 2]
                for c in range(8):
                    t(lambda e, j=j, g_=g_, c=c, p=p: e.matmul(
                        p.ap[:, 0:512], self.wqkvT.ap[:, c, j * 128:(j + 1) * 128],
                        self.h1T.ap[:, c, g_ * 512:(g_ + 1) * 512],
                        start=(c == 0), stop=(c == 7)),
                      r=[self.wqkvT, self.h1T], w=[p], tag="mmqkv")
                dst = self.qT if j < 8 else self.kvT
                jj = j if j < 8 else j - 8
                a(lambda e, dst=dst, jj=jj, g_=g_, p=p, j=j: e.activation(
                    dst.ap[:, jj, g_ * 512:(g_ + 1) * 512], p.ap[:, 0:512],
                    AF.Identity, bias=self.qbsT.ap[:, j:j + 1],
                    scale=self.qscT.ap[:, j:j + 1]),
                  r=[p, self.qbsT, self.qscT], w=[dst], tag="qkvev")

    def phaseB(self, s):
        v, t, a = self.v, self.t, self.a
        for g_ in range(4):
            if s == 0:
                self.pump_wq(10)
            half = g_ % 2
            ch = g_ // 2
            for dsthalf in range(2):
                self.dget(lambda e, g_=g_, half=half, ch=ch, dh=dsthalf: e.dma_start(
                    out=self.ktil.ap[dh * 64:(dh + 1) * 64, g_, :],
                    in_=self.kvT.ap[half * 64:(half + 1) * 64, ch, :]),
                     r=[self.kvT], w=[self.ktil], tag="ktil")
            v(lambda e, g_=g_: e.memset(self.vT.ap[:, g_, 64:1040:65], 1.0),
              w=[self.vT], tag="vones")
            for kc in range(16):
                p = (self.pA, self.pB)[kc % 2]
                t(lambda e, g_=g_, half=half, ch=ch, kc=kc, p=p: e.transpose(
                    p.ap.bitcast(BF16)[:, 0:64],
                    self.kvT.ap[half * 64:(half + 1) * 64, 2 + ch, kc * 128:(kc + 1) * 128],
                    self.identb.ap[half * 64:(half + 1) * 64, half * 64:(half + 1) * 64]),
                  r=[self.kvT, self.identb], w=[p], tag="trv")
                v(lambda e, g_=g_, kc=kc, p=p: e.tensor_copy(
                    self.vT.ap[:, g_, kc * 65:kc * 65 + 64], p.ap.bitcast(BF16)[:, 0:64]),
                  r=[p], w=[self.vT], tag="vcp")
            v(lambda e, g_=g_: e.tensor_mul(self.q2s.ap[0:64, :],
                                            self.ktil.ap[0:64, g_, :],
                                            self.ktil.ap[0:64, g_, :]),
              r=[self.ktil, self.q2s], w=[self.q2s], tag="k2")
            for u in range(4):
                p = (self.pC, self.pD)[u % 2]
                t(lambda e, u=u, p=p: e.matmul(
                    p.ap[0:1, 0:512], self.onesb.ap[0:64, :],
                    self.q2s.ap[0:64, u * 512:(u + 1) * 512], start=True, stop=True),
                  r=[self.onesb, self.q2s], w=[p], tag="knrm")
                v(lambda e, u=u, p=p: e.reduce_max(self.st1.ap[0:1, u:u + 1],
                                                   p.ap[0:1, 0:512], AX.X),
                  r=[p, self.st1], w=[self.st1])
            v(lambda e, g_=g_: e.reduce_max(self.st2.ap[0:1, g_:g_ + 1],
                                            self.st1.ap[0:1, 0:4], AX.X),
              r=[self.st1, ("km2",)], w=[("km2",), self.st2], tag="km2")

        for hp in range(8):
            if s == 0:
                self.pump_wq(5)
            kv = hp // 2
            v(lambda e, hp=hp: e.tensor_mul(self.q2s.ap, self.qT.ap[:, hp, :],
                                            self.qT.ap[:, hp, :]),
              r=[self.qT, self.q2s], w=[self.q2s], tag="q2")
            for hh in range(2):
                base = 64 * hh
                for u in range(4):
                    p = (self.pC, self.pD)[u % 2]
                    t(lambda e, base=base, u=u, p=p: e.matmul(
                        p.ap[0:1, 0:512], self.onesb.ap[base:base + 64, :],
                        self.q2s.ap[base:base + 64, u * 512:(u + 1) * 512],
                        start=True, stop=True,
                        tile_position=(base, 0)),
                      r=[self.onesb, self.q2s], w=[p], tag="qnrm")
                    v(lambda e, u=u, p=p: e.reduce_max(self.st1.ap[0:1, u:u + 1],
                                                       p.ap[0:1, 0:512], AX.X),
                      r=[p, self.st1], w=[self.st1])
                v(lambda e, hh=hh: e.reduce_max(self.st1.ap[0:1, 4 + hh:5 + hh],
                                                self.st1.ap[0:1, 0:4], AX.X),
                  r=[self.st1], w=[self.st1])
                v(lambda e, hh=hh, kv=kv: e.tensor_mul(
                    self.st1.ap[0:1, 4 + hh:5 + hh], self.st1.ap[0:1, 4 + hh:5 + hh],
                    self.st2.ap[0:1, kv:kv + 1]),
                  r=[self.st1, self.st2], w=[self.st1])
                a(lambda e, hh=hh: e.activation(self.st1.ap[0:1, 6 + hh:7 + hh],
                                                self.st1.ap[0:1, 4 + hh:5 + hh], AF.Sqrt),
                  r=[self.st1], w=[self.st1])
            v(lambda e: e.tensor_scalar(self.st1.ap[0:1, 6:8], self.st1.ap[0:1, 6:8],
                                        -1.0, None, AO.mult),
              r=[self.st1], w=[self.st1], tag="negch")
            self.dget(lambda e, hp=hp: e.dma_start(
                out=self.bounce[2:3, 2 * hp:2 * hp + 2], in_=self.st1.ap[0:1, 6:8]),
                 r=[self.st1], w=[("bnc", 2, hp)], tag="ch_out")
            self.dget(lambda e, hp=hp: e.dma_start(
                out=self.chb.ap[:, 2 * hp:2 * hp + 2],
                in_=self.bounce[2:3, 2 * hp:2 * hp + 2].broadcast_to([128, 2])),
                 r=[("bnc", 2, hp), self.chb], w=[self.chb], tag="ch_in")

    def phaseC(self, s):
        v, a, t = self.v, self.a, self.t
        for hp in range(8):
            kv = hp // 2
            for qg in range(2):
                pas = ((self.paA0, self.paA1), (self.paB0, self.paB1))

                def emit_qk(i, hp=hp, kv=kv, qg=qg):
                    kc, colh = i // 2, i % 2
                    q0 = qg * 1024 + colh * 512
                    t(lambda e, hp=hp, kv=kv, kc=kc, q0=q0, colh=colh: e.matmul(
                        self.pA.ap[:, colh * 512:(colh + 1) * 512],
                        self.ktil.ap[0:64, kv, kc * 128:(kc + 1) * 128],
                        self.qT.ap[0:64, hp, q0:q0 + 512],
                        start=True, stop=True, tile_position=(0, 0)),
                      r=[self.ktil, self.qT], w=[("pA", colh)], tag="qkA")
                    t(lambda e, hp=hp, kv=kv, kc=kc, q0=q0, colh=colh: e.matmul(
                        self.pB.ap[:, colh * 512:(colh + 1) * 512],
                        self.ktil.ap[64:128, kv, kc * 128:(kc + 1) * 128],
                        self.qT.ap[64:128, hp, q0:q0 + 512],
                        start=True, stop=True, tile_position=(64, 0)),
                      r=[self.ktil, self.qT], w=[("pB", colh)], tag="qkB")

                emit_qk(0)
                for i in range(32):
                    kc, colh = i // 2, i % 2
                    bA = pas[0][kc % 2]
                    bB = pas[1][kc % 2]
                    if i + 1 < 32:
                        emit_qk(i + 1)
                    a(lambda e, hp=hp, bA=bA, colh=colh: e.activation(
                        bA.ap[:, colh * 512:(colh + 1) * 512],
                        self.pA.ap[:, colh * 512:(colh + 1) * 512], AF.Exp,
                        bias=self.chb.ap[:, 2 * hp:2 * hp + 1]),
                      r=[("pA", colh), self.chb], w=[(id(bA), colh)] + list(bA.keys),
                      tag="expA")
                    a(lambda e, hp=hp, bB=bB, colh=colh: e.activation(
                        bB.ap[:, colh * 512:(colh + 1) * 512],
                        self.pB.ap[:, colh * 512:(colh + 1) * 512], AF.Exp,
                        bias=self.chb.ap[:, 2 * hp + 1:2 * hp + 2]),
                      r=[("pB", colh), self.chb], w=[(id(bB), colh)] + list(bB.keys),
                      tag="expB")
                    t(lambda e, kv=kv, kc=kc, colh=colh, bA=bA: e.matmul(
                        self.pC.ap[0:65, colh * 512:(colh + 1) * 512],
                        self.vT.ap[:, kv, kc * 65:kc * 65 + 65],
                        bA.ap[:, colh * 512:(colh + 1) * 512],
                        start=(kc == 0), stop=(kc == 15)),
                      r=[self.vT, (id(bA), colh)], w=[("pC", colh)], tag="pvA")
                    t(lambda e, kv=kv, kc=kc, colh=colh, bB=bB: e.matmul(
                        self.pD.ap[0:65, colh * 512:(colh + 1) * 512],
                        self.vT.ap[:, kv, kc * 65:kc * 65 + 65],
                        bB.ap[:, colh * 512:(colh + 1) * 512],
                        start=(kc == 0), stop=(kc == 15)),
                      r=[self.vT, (id(bB), colh)], w=[("pD", colh)], tag="pvB")
                for hh, pv in ((0, self.pC), (1, self.pD)):
                    pvk = [("pC", 0), ("pC", 1)] if hh == 0 else [("pD", 0), ("pD", 1)]
                    v(lambda e, hh=hh, pv=pv: e.tensor_copy(
                        self.q2s.ap[0:65, hh * 1024:(hh + 1) * 1024],
                        pv.ap[0:65, 0:1024]),
                      r=pvk + [self.q2s], w=[self.q2s], tag="pvev")
                for hh in range(2):
                    dent = (self.h2t, self.sqd)[hh]
                    v(lambda e, hh=hh, dent=dent: e.reciprocal(
                        dent.ap[64:65, :],
                        self.q2s.ap[64:65, hh * 1024:(hh + 1) * 1024]),
                      r=[self.q2s, dent], w=[dent], tag="denrec")
                    self.dget(lambda e, hh=hh, dent=dent: e.dma_start(
                        out=self.bounce[hh:hh + 1, :],
                        in_=dent.ap[64:65, :]),
                         r=[dent], w=[("bnc", hh)], tag="den_out")
                    self.dget(lambda e, hh=hh: e.dma_start(
                        out=self.recb.ap[:, hh, :],
                        in_=self.bounce[hh:hh + 1, :].broadcast_to([64, 1024])),
                         r=[("bnc", hh), self.recb], w=[self.recb], tag="den_in")
                    if hh == 0:
                        v(lambda e, hp=hp, qg=qg: e.tensor_tensor(
                            self.attnT.ap[0:64, hp, qg * 1024:(qg + 1) * 1024],
                            self.q2s.ap[0:64, 0:1024], self.recb.ap[:, 0, :], AO.mult),
                          r=[self.q2s, self.recb], w=[self.attnT], tag="nrmA")
                    else:
                        v(lambda e, qg=qg: e.tensor_tensor(
                            self.ntmp.ap[:, qg, :], self.q2s.ap[0:64, 1024:2048],
                            self.recb.ap[:, 1, :], AO.mult),
                          r=[self.q2s, self.recb, self.ntmp], w=[self.ntmp], tag="nrmB")
                        self.dget(lambda e, hp=hp, qg=qg: e.dma_start(
                            out=self.attnT.ap[64:128, hp, qg * 1024:(qg + 1) * 1024],
                            in_=self.ntmp.ap[:, qg, :]),
                             r=[self.ntmp], w=[self.attnT], tag="nrmBmv")

    def phaseD(self, s):
        v, a, t = self.v, self.a, self.t
        for g_ in range(4):
            if s == 0:
                self.pump_wq(8)
            for dc in range(8):
                p = (self.pA, self.pB)[dc % 2]
                for hp in range(8):
                    t(lambda e, g_=g_, dc=dc, hp=hp, p=p: e.matmul(
                        p.ap[:, 0:512], self.owT.ap[:, hp, dc * 128:(dc + 1) * 128],
                        self.attnT.ap[:, hp, g_ * 512:(g_ + 1) * 512],
                        start=(hp == 0), stop=(hp == 7)),
                      r=[self.owT, self.attnT], w=[p], tag="mmo")
                cga = (2 * 8 + dc) * 4 + s
                a(lambda e, dc=dc, p=p, cga=cga, s=s: e.activation(
                    self.gd2.ap[:, 0, dc, :], p.ap[:, 0:512], AF.Identity,
                    bias=self.gobT.ap[:, dc * 4 + s:dc * 4 + s + 1],
                    scale=self.modT.ap[:, cga:cga + 1]),
                  r=[p, self.gobT, self.modT], w=[self.gd2], tag="gdev")
            for t4 in range(4):
                tc = g_ * 4 + t4
                buf = t4 % 2
                xk = ("xio", buf)
                self.dget(lambda e, s=s, tc=tc, buf=buf: e.dma_start(
                    out=self.xio.ap[:, buf, :],
                    in_=self.x_in[s, tc * 128:(tc + 1) * 128, :]),
                     r=[xk], w=[xk], tag="ldx2")
                p2 = (self.pC, self.pD)[t4 % 2]
                for dc in range(8):
                    t(lambda e, dc=dc, t4=t4, p2=p2: e.transpose(
                        p2.ap.bitcast(BF16)[:, dc * 128:(dc + 1) * 128],
                        self.gd2.ap[:, 0, dc, t4 * 128:(t4 + 1) * 128], self.identb.ap),
                      r=[self.gd2, self.identb], w=[p2], tag="trD")
                v(lambda e, buf=buf, p2=p2: e.tensor_tensor(
                    self.xio.ap[:, buf, :], self.xio.ap[:, buf, :],
                    p2.ap.bitcast(BF16)[:, 0:1024], AO.add),
                  r=[xk, p2], w=[xk], tag="resadd")
                self.dput(lambda e, tc=tc, buf=buf: e.dma_start(
                    out=self.xnew_d[tc * 128:(tc + 1) * 128, :],
                    in_=self.xio.ap[:, buf, :]),
                     r=[xk], w=[("xn_d", tc)], tag="stxn")
                if "xnew" in self.dbg and s == 0:
                    self.dput(lambda e, tc=tc, buf=buf: e.dma_start(
                        out=self.xnew_dbg[tc * 128:(tc + 1) * 128, :],
                        in_=self.xio.ap[:, buf, :]),
                         r=[xk], w=[("xnd", tc)], tag="dbgxn")

    def phaseE(self, s):
        v, a, t = self.v, self.a, self.t
        self.dget(lambda e, s=s: e.dma_start(
            out=self.cb1r.ap, in_=self.mod_d[s:s + 1, 4 * D:5 * D].broadcast_to([128, D])),
             r=[("mod_d", 8), ("mod_d", 9), self.cb1r], w=[self.cb1r], tag="cb1r")
        self.dget(lambda e, s=s: e.dma_start(
            out=self.sbr.ap, in_=self.mod_d[s:s + 1, 3 * D:4 * D].broadcast_to([128, D])),
             r=[("mod_d", 6), ("mod_d", 7), self.sbr], w=[self.sbr], tag="sbr")
        self._e4u = []
        for half in range(2):
            for tcl in range(8):
                tc = half * 8 + tcl
                buf = tcl % 2
                xk = ("xio", buf)
                self.dget(lambda e, tc=tc, buf=buf: e.dma_start(
                    out=self.xio.ap[:, buf, :],
                    in_=self.xnew_d[tc * 128:(tc + 1) * 128, :]),
                     r=[xk, ("xn_d", tc)], w=[xk], tag="ldxn")
                self.quant_flush()
                sc_, bi_ = self.ln_stats(self.xio.ap[:, buf, :], xk, tag="ln2",
                                         dump=self.h2e)
                a(lambda e, buf=buf, sc_=sc_, bi_=bi_: e.activation(
                    self.h2e.ap, self.xio.ap[:, buf, :], AF.Identity,
                    bias=bi_, scale=sc_),
                  r=[xk, self.st2, self.h2e], w=[self.h2e], tag="xn2")
                v(lambda e: e.tensor_mul(self.h2e.ap, self.h2e.ap, self.cb1r.ap),
                  r=[self.h2e, self.cb1r], w=[self.h2e])
                v(lambda e: e.tensor_add(self.h2e.ap, self.h2e.ap, self.sbr.ap),
                  r=[self.h2e, self.sbr], w=[self.h2e])
                for so in (0, 512):
                    aft = []
                    for c in range(so // 128, so // 128 + 4):
                        def mk(c=c, tcl=tcl):
                            pbk = ("pB", c % 2)
                            t(lambda e: e.transpose(
                                self.pB.ap.bitcast(BF16)[:, c % 2 * 1024:c % 2 * 1024 + 128],
                                self.hq2.ap[:, c * 128:(c + 1) * 128],
                                self.identb.ap), r=[self.hq2, self.identb], w=[pbk],
                              tag="trh")
                            a(lambda e: e.activation(
                                self.hqT.ap[:, c, tcl * 128:(tcl + 1) * 128],
                                self.pB.ap.bitcast(BF16)[:, c % 2 * 1024:c % 2 * 1024 + 128],
                                AF.Copy),
                              r=[pbk], w=[self.hqT], tag="hqcp")
                        aft.append(mk)
                    self.emit_quant(
                        Tile(self.h2e.ap[:, so:so + 512], list(self.h2e.keys)),
                        Tile(self.hq2.ap[:, so:so + 512], [self.hq2.keys[0]]),
                        512, after=aft)
                for _ in range(2):
                    if self._e4u:
                        self._e4u.pop(0)()
            self.quant_flush()
            chunks = [(0, 512), (512, 512), (1024, 512), (1536, 512), (2048, 512),
                      (2560, 256)]
            for ici, (woff, w_) in enumerate(chunks):
                wgt = self.wgs[ici % 2]
                wut = self.wus[ici % 2]
                self.dget(lambda e, woff=woff, w_=w_, wgt=wgt: e.dma_start(
                    out=wgt.ap[:, :, 0:w_],
                    in_=self.wgqT_d[:, woff:woff + w_].rearrange("(c p) n -> p c n", p=128)),
                     r=[wgt, ("wgq",)], w=[wgt], tag="ldwg")
                self.dget(lambda e, woff=woff, w_=w_, wut=wut: e.dma_start(
                    out=wut.ap[:, :, 0:w_],
                    in_=self.wuqT_d[:, woff:woff + w_].rearrange("(c p) n -> p c n", p=128)),
                     r=[wut, ("wuq",)], w=[wut], tag="ldwu")
                for tcl in range(8):
                    p = (self.pC, self.pD, self.pA)[(ici * 8 + tcl) % 3]
                    pk = [(p.keys[0][0], 0), (p.keys[0][0], 1)]
                    for c in range(8):
                        t(lambda e, c=c, tcl=tcl, wgt=wgt, w_=w_, p=p: e.matmul(
                            p.ap[:, 0:w_], self.hqT.ap[:, c, tcl * 128:(tcl + 1) * 128],
                            wgt.ap[:, c, 0:w_], start=(c == 0), stop=(c == 7)),
                          r=[self.hqT, wgt], w=[pk[0]], tag="mmg")
                    for c in range(8):
                        t(lambda e, c=c, tcl=tcl, wut=wut, w_=w_, p=p: e.matmul(
                            p.ap[:, 512:512 + w_],
                            self.hqT.ap[:, c, tcl * 128:(tcl + 1) * 128],
                            wut.ap[:, c, 0:w_], start=(c == 0), stop=(c == 7)),
                          r=[self.hqT, wut], w=[pk[1]], tag="mmu")
                    a(lambda e, w_=w_, p=p: e.activation(self.sg.ap[:, 0:w_],
                                                         p.ap[:, 0:w_], AF.Silu),
                      r=[pk[0], self.sg], w=[self.sg], tag="silu")
                    pre = [lambda w_=w_, p=p, pk=pk: v(
                        lambda e: e.tensor_tensor(
                            self.acts.ap[:, 0:w_], self.sg.ap[:, 0:w_],
                            p.ap[:, 512:512 + w_], AO.mult),
                        r=[self.sg, pk[1], self.acts], w=[self.acts], tag="actm")]
                    aft = []
                    for u in range(w_ // 128):
                        c2 = (woff + u * 128) // 128
                        def mk(u=u, c2=c2, tcl=tcl):
                            pbk = ("pB", u % 2)
                            t(lambda e: e.transpose(
                                self.pB.ap.bitcast(BF16)[:, u % 2 * 1024:u % 2 * 1024 + 128],
                                self.qT3.ap[:, u * 128:(u + 1) * 128],
                                self.identb.ap), r=[self.qT3, self.identb], w=[pbk],
                              tag="tra")
                            a(lambda e: e.activation(
                                self.actqT.ap[:, c2, tcl * 128:(tcl + 1) * 128],
                                self.pB.ap.bitcast(BF16)[:, u % 2 * 1024:u % 2 * 1024 + 128],
                                AF.Copy),
                              r=[pbk], w=[self.actqT], tag="aqcp")
                        aft.append(mk)
                    self.emit_quant(Tile(self.acts.ap[:, 0:w_], self.acts.keys),
                                    Tile(self.qT3.ap[:, 0:w_], self.qT3.keys), w_,
                                    qout=True, after=aft, pre=pre)
            self.quant_flush()
            def mk_e4_units(half=half, s=s):
                units = []
                for dc in range(8):
                    def u_dc(dc=dc, s=s):
                        wdt = self.wds[dc % 2]
                        self.dget(lambda e, dc=dc, wdt=wdt: e.dma_start(
                            out=wdt.ap,
                            in_=self.wdqT_d[:, dc * 128:(dc + 1) * 128]
                                .rearrange("(c p) n -> p c n", p=128)),
                             r=[wdt, ("wdq",)], w=[wdt], tag="ldwd")
                        for g2 in range(2):
                            p = (self.pC, self.pD)[g2]
                            for c2 in range(22):
                                t(lambda e, c2=c2, g2=g2, wdt=wdt, p=p: e.matmul(
                                    p.ap[:, 0:512], wdt.ap[:, c2, :],
                                    self.actqT.ap[:, c2, g2 * 512:(g2 + 1) * 512],
                                    start=(c2 == 0), stop=(c2 == 21)),
                                  r=[wdt, self.actqT], w=[p], tag="mmd")
                            cgb = (5 * 8 + dc) * 4 + s
                            a(lambda e, dc=dc, g2=g2, p=p, cgb=cgb: e.activation(
                                self.gd2.ap[:, g2, dc, :], p.ap[:, 0:512], AF.Identity,
                                scale=self.modT.ap[:, cgb:cgb + 1]),
                              r=[p, self.modT], w=[self.gd2], tag="fdev")
                    units.append(u_dc)
                for g2 in range(2):
                    for t4 in range(4):
                        def u_res(g2=g2, t4=t4, half=half, s=s):
                            tc = half * 8 + g2 * 4 + t4
                            rb = self.q2s.ap.bitcast(F32)[:, 0:1024]
                            self.dget(lambda e, tc=tc, rb=rb: e.dma_start(
                                out=rb,
                                in_=self.xnew_d[tc * 128:(tc + 1) * 128, :]),
                                 r=[self.q2s, ("xn_d", tc)], w=[self.q2s], tag="ldxn2")
                            p2 = (self.pC, self.pD)[t4 % 2]
                            for dc in range(8):
                                t(lambda e, dc=dc, g2=g2, t4=t4, p2=p2: e.transpose(
                                    p2.ap.bitcast(BF16)[:, dc * 128:(dc + 1) * 128],
                                    self.gd2.ap[:, g2, dc, t4 * 128:(t4 + 1) * 128],
                                    self.identb.ap),
                                  r=[self.gd2, self.identb], w=[p2], tag="trE")
                            v(lambda e, rb=rb, p2=p2: e.tensor_tensor(
                                rb, rb, p2.ap.bitcast(BF16)[:, 0:1024],
                                AO.add), r=[self.q2s, p2], w=[self.q2s], tag="fres")
                            self.dput(lambda e, s=s, tc=tc, rb=rb: e.dma_start(
                                out=self.out[s, tc * 128:(tc + 1) * 128, :],
                                in_=rb),
                                 r=[self.q2s], w=[("out", s, tc)], tag="stout")
                        units.append(u_res)
                return units

            self._e4u.extend(mk_e4_units())
        while self._e4u:
            self._e4u.pop(0)()

    def finish(self):
        sc = self.sc

        def final_wait(g):
            last = None
            for slot, n in sc.slot_n.items():
                last = g.wait_ge(sc.qsem[slot], 16 * n)
            return last
        sc.op("gpsimd", final_wait, tag="final_wait")
        sc.finalize(self.es)
        blocks = self.es.enter_context(self.nc.Block())
        for e in ENGS:
            getattr(blocks, e)(lambda eng, e=e: sc.emit(e, eng))
        self.es.close()



def kernel(L0_attn_hidden_states, L0_attn_scaling, L0_attn_qkv_w, L0_attn_qkv_b,
           L0_attn_o_w, L0_attn_o_b, L0_ffn_w_gate, L0_ffn_w_up, L0_ffn_w_down,
           L0_c, L0_wmod):
    from concourse.bass_utils import run_bass_kernel_spmd
    x = np.ascontiguousarray(np.asarray(L0_attn_hidden_states, dtype=np.float32))
    c = np.ascontiguousarray(np.asarray(L0_c, dtype=np.float32))
    common = {
        "scaling": np.ascontiguousarray(np.asarray(L0_attn_scaling, np.float32).reshape(HD, 1)),
        "qkv_w": np.ascontiguousarray(np.asarray(L0_attn_qkv_w, np.float32)),
        "qkv_b": np.ascontiguousarray(np.asarray(L0_attn_qkv_b, np.float32)),
        "o_w": np.ascontiguousarray(np.asarray(L0_attn_o_w, np.float32)),
        "o_b": np.ascontiguousarray(np.asarray(L0_attn_o_b, np.float32)),
        "w_gate": np.ascontiguousarray(np.asarray(L0_ffn_w_gate, np.float32)),
        "w_up": np.ascontiguousarray(np.asarray(L0_ffn_w_up, np.float32)),
        "w_down": np.ascontiguousarray(np.asarray(L0_ffn_w_down, np.float32)),
        "wmod": np.ascontiguousarray(np.asarray(L0_wmod, np.float32)),
    }
    b = BuildK().build()
    in_maps = [dict(common, x=np.ascontiguousarray(x[4 * i:4 * i + 4]),
                    c=np.ascontiguousarray(c[4 * i:4 * i + 4])) for i in range(8)]
    res = run_bass_kernel_spmd(b.nc, in_maps, core_ids=list(range(8)))
    return np.concatenate([np.asarray(r["out"]) for r in res.results], axis=0)
